# version 02
# baseline (speedup 1.0000x reference)
import numpy as np
from contextlib import ExitStack

import concourse.bass as bass
import concourse.mybir as mybir
from concourse.bass_utils import run_bass_kernel_spmd

F32 = mybir.dt.float32
BF16 = mybir.dt.bfloat16
AF = mybir.ActivationFunctionType
ALU = mybir.AluOpType
AX = mybir.AxisListType

S_CORE = 4096
D = 1024
NTB = S_CORE // 128
N_CORES = 8


class Prog:
    def __init__(self):
        self.nc = bass.Bass("TRN2", target_bir_lowering=False)
        self.es = ExitStack()
        self.inputs = {}
        self.outputs = {}
        self.scratches = {}
        self.sems = {}
        self.semc = {}
        self.block = None
        self.last = {}
        self.pd_total = 0
        self.waits = []
        self.semname = {}

    def inp(self, name, shape, dtype=F32):
        if name not in self.inputs:
            self.inputs[name] = self.nc.dram_tensor(name, list(shape), dtype, kind="ExternalInput").ap()
        return self.inputs[name]

    def outp(self, name, shape, dtype=F32):
        if name not in self.outputs:
            self.outputs[name] = self.nc.dram_tensor(name, list(shape), dtype, kind="ExternalOutput").ap()
        return self.outputs[name]

    def scratch(self, name, shape, dtype=F32):
        if name not in self.scratches:
            self.scratches[name] = self.nc.dram_tensor(name, list(shape), dtype, kind="Internal").ap()
        return self.scratches[name]

    def _uniq(self, name):
        if not hasattr(self, "_names"):
            self._names = {}
        n = self._names.get(name, 0)
        self._names[name] = n + 1
        return name if n == 0 else f"{name}__{n}"

    def sb(self, name, shape, dtype, stack=None):
        return (stack or self.es).enter_context(self.nc.sbuf_tensor(self._uniq(name), list(shape), dtype))

    def ps(self, name, shape, dtype=F32, stack=None):
        return (stack or self.es).enter_context(self.nc.psum_tensor(self._uniq(name), list(shape), dtype))

    PERSISTENT = {"pd", "ST", "WPREP", "LDG", "FIN"}

    def _physname(self, name):
        if not hasattr(self, "log2phys"):
            self.log2phys = {}
            self.pool_free = []
            self.pool_next = 0
        if name in self.log2phys:
            return self.log2phys[name]
        if name in self.PERSISTENT:
            p = "P_" + name
        elif self.pool_free:
            p = self.pool_free.pop()
        else:
            p = f"pool{self.pool_next}"
            self.pool_next += 1
        if p not in self.sems:
            self.sems[p] = self.es.enter_context(self.nc.semaphore(p))
            self.semc[p] = 0
            self.semname[id(self.sems[p])] = p
        self.log2phys[name] = p
        return p

    def sem(self, name):
        return self.sems[self._physname(name)]

    def inc(self, eng, instr, name, k=1):
        p = self._physname(name)
        instr.then_inc(self.sems[p], k)
        self.semc[p] += k
        assert self.semc[p] < 2 ** 30
        self.last[eng] = (p, self.semc[p])
        return self.semc[p]

    def val(self, name):
        return self.semc[self._physname(name)]

    def end_phase(self):
        pd = self.sem("pd")
        for e in sorted(self.last):
            sname, sval = self.last[e]
            sec = getattr(self.block, e)

            def body(h, sn=sname, sv=sval):
                h.wait_ge(self.sems[sn], sv)
                h.nop().then_inc(pd, 1)

            sec(body)
            self.pd_total += 1
            self.semc[self._physname("pd")] = self.pd_total
        self.last = {}
        for name, p in list(getattr(self, "log2phys", {}).items()):
            if name not in self.PERSISTENT:
                del self.log2phys[name]
                self.pool_free.append(p)

    def pw(self, eng):
        if self.pd_total > 0:
            eng.wait_ge(self.sem("pd"), self.pd_total)

    def check_waits(self):
        bad = []
        for sname, val, loc in self.waits:
            if val > self.semc.get(sname, 0):
                bad.append((sname, val, self.semc.get(sname, 0), loc))
        if bad:
            for b in bad[:20]:
                print("IMPOSSIBLE WAIT:", b)
            raise AssertionError(f"{len(bad)} impossible semaphore waits")


def build_program(phase_fns):
    P = Prog()
    G = {}

    P.inp("x_in", [S_CORE, D])
    P.outp("y", [S_CORE, D])
    G["x_state"] = [P.scratch("x_state0", [S_CORE, D]), P.scratch("x_state1", [S_CORE, D])]
    G["par"] = 0

    G["XHAT"] = P.sb("XHAT", [128, 8 * S_CORE], BF16)
    G["CTX"] = P.sb("CTX", [128, 8 * S_CORE], BF16)
    G["ident_bf"] = P.sb("ident_bf", [128, 128], BF16)
    G["ident_f32"] = P.sb("ident_f32", [128, 128], F32)
    G["ones_bf"] = P.sb("ones_bf", [1, 512], BF16)
    G["ones_f32"] = P.sb("ones_f32", [1, 512], F32)
    P.inp("ident", [128, 128])

    raw_block = P.es.enter_context(P.nc.Block())

    import traceback

    class EngProxy:
        def __init__(self, h):
            self._h = h

        def wait_ge(self, sem, val, **kw):
            nm = P.semname.get(id(sem))
            if nm is not None:
                tb = traceback.extract_stack(limit=3)[0]
                P.waits.append((nm, val, f"{tb.name}:{tb.lineno}"))
            return self._h.wait_ge(sem, val, **kw)

        def __getattr__(self, k):
            return getattr(self._h, k)

    class BlockProxy:
        def __getattr__(self, eng):
            sec = getattr(raw_block, eng)

            def reg(f):
                return sec(lambda h: f(EngProxy(h)))
            return reg

    G["block"] = BlockProxy()
    P.block = G["block"]

    for f in phase_fns:
        f(P, G)

    P.check_waits()
    P.es.close()
    return P



def ph_init(P, G):
    block = G["block"]

    @block.sync
    def _(s):
        P.pw(s)
        i1 = s.dma_start(out=G["x_state"][0][:, :], in_=P.inputs["x_in"][:, :])
        P.inc("sync", i1, "ST", 16)

    @block.gpsimd
    def _(g):
        P.pw(g)
        i1 = g.dma_start(out=G["ident_bf"][:], in_=P.inputs["ident"][:, :])
        P.inc("gpsimd", i1, "LDG", 16)
        i2 = g.dma_start(out=G["ident_f32"][:], in_=P.inputs["ident"][:, :])
        P.inc("gpsimd", i2, "LDG", 16)

    @block.vector
    def _(v):
        P.pw(v)
        v.memset(G["ones_bf"][:], 1.0)
        i = v.memset(G["ones_f32"][:], 1.0)
        P.inc("vector", i, "FIN", 1)

    P.end_phase()


def mk_prep(names):
    def ph(P, G):
        block = G["block"]

        @block.gpsimd
        def _(g):
            P.pw(g)
            for name, shape in names:
                src = P.inp(name, shape)
                flat = int(np.prod(shape))
                dst = P.scratch(name + "__bf", [flat], BF16)
                if len(shape) == 1:
                    sa = src
                else:
                    axes = " ".join(f"a{i}" for i in range(len(shape)))
                    sa = src.rearrange(f"{axes} -> ({axes})")
                ins = g.dma_start(out=dst[:], in_=sa)
                P.inc("gpsimd", ins, "WPREP", 16)

        P.end_phase()
    return ph


def bfview(P, name, ncols):
    return P.scratches[name + "__bf"].rearrange("(a b) -> a b", b=ncols)


def ph_ln(P, G):
    block = G["block"]
    xsrc = G["x_state"][G["par"]]
    XHAT = G["XHAT"]

    st = ExitStack()
    xt = [P.sb(f"ln_xt{i}", [128, D], F32, st) for i in range(2)]
    stage = P.sb("ln_stage", [128, D], F32, st)
    vtmp = P.sb("ln_vtmp", [128, D], F32, st)
    xn = [P.sb(f"ln_xn{i}", [128, D], BF16, st) for i in range(2)]
    cols = P.sb("ln_cols", [128, 8], F32, st)
    psT = [P.ps(f"ln_psT{i}", [128, 128], BF16, st) for i in range(2)]

    b_lda = P.val("LNLA"); b_ldb = P.val("LNLB")
    b_s1 = P.val("LNS"); b_v1 = P.val("LNV1"); b_v2 = P.val("LNV2")
    b_vg = P.val("LNVG"); b_r1 = P.val("LNR1"); b_r = P.val("LNR")
    b_s2 = P.val("LNN"); b_mm = P.val("LMM"); b_ev = P.val("LEV")

    @block.sync
    def _(s):
        P.pw(s)
        s.wait_ge(P.sem("ST"), P.val("ST"))
        s.wait_ge(P.sem("STG"), P.val("STG"))
        for i in range(NTB):
            if i >= 2:
                s.wait_ge(P.sem("LNN"), b_s2 + i - 1)
            ins = s.dma_start(out=xt[i % 2][:], in_=xsrc[i * 128:(i + 1) * 128, :])
            P.inc("sync", ins, "LNLA" if i % 2 == 0 else "LNLB", 16)

    @block.scalar
    def _(sc):
        P.pw(sc)
        for i in range(NTB):
            if i % 2 == 0:
                sc.wait_ge(P.sem("LNLA"), b_lda + (i // 2 + 1) * 16)
            else:
                sc.wait_ge(P.sem("LNLB"), b_ldb + (i // 2 + 1) * 16)
            x = xt[i % 2]
            sc.activation(out=stage[:], in_=x[:], func=AF.Copy, accum_out=cols[:, 0:1])
            ins = sc.activation(out=stage[:], in_=x[:], func=AF.Square, accum_out=cols[:, 1:2])
            P.inc("scalar", ins, "LNS", 1)
            sc.wait_ge(P.sem("LNVG"), b_vg + i + 1)
            if i >= 1:
                sc.wait_ge(P.sem("LNN"), b_s2 + i)
            ins = sc.activation(out=cols[:, 5:6], in_=cols[:, 3:4], func=AF.Ln)
            P.inc("scalar", ins, "LNR1", 1)
            sc.wait_ge(P.sem("LNR1"), b_r1 + i + 1)
            ins = sc.activation(out=cols[:, 4:5], in_=cols[:, 5:6], func=AF.Exp, scale=-0.5)
            P.inc("scalar", ins, "LNR", 1)

    @block.vector
    def _(v):
        P.pw(v)
        for i in range(NTB):
            v.wait_ge(P.sem("LNS"), b_s1 + i + 1)
            v.tensor_scalar(out=cols[:, 2:3], in0=cols[:, 0:1], scalar1=-1.0 / D,
                            scalar2=None, op0=ALU.mult)
            ins = v.tensor_scalar(out=cols[:, 7:8], in0=cols[:, 1:2], scalar1=1.0 / D,
                                  scalar2=1e-5, op0=ALU.mult, op1=ALU.add)
            P.inc("vector", ins, "LNV1", 1)
            v.wait_ge(P.sem("LNV1"), b_v1 + i + 1)
            ins = v.tensor_tensor(out=cols[:, 6:7], in0=cols[:, 2:3], in1=cols[:, 2:3], op=ALU.mult)
            P.inc("vector", ins, "LNV2", 1)
            v.wait_ge(P.sem("LNV2"), b_v2 + i + 1)
            ins = v.tensor_tensor(out=cols[:, 3:4], in0=cols[:, 7:8], in1=cols[:, 6:7], op=ALU.subtract)
            P.inc("vector", ins, "LNVG", 1)
            v.wait_ge(P.sem("LNR"), b_r + i + 1)
            v.tensor_tensor(out=vtmp[:], in0=xt[i % 2][:],
                            in1=cols[:, 2:3].broadcast_to([128, D]), op=ALU.add)
            ins = v.tensor_tensor(out=xn[i % 2][:], in0=vtmp[:],
                                  in1=cols[:, 4:5].broadcast_to([128, D]), op=ALU.mult)
            P.inc("vector", ins, "LNN", 1)
            for c in range(8):
                j = i * 8 + c
                v.wait_ge(P.sem("LMM"), b_mm + j + 1)
                ins = v.tensor_copy(
                    out=XHAT[:, c * S_CORE + i * 128: c * S_CORE + (i + 1) * 128],
                    in_=psT[j % 2][:])
                P.inc("vector", ins, "LEV", 1)

    @block.tensor
    def _(t):
        P.pw(t)
        t.wait_ge(P.sem("LDG"), P.val("LDG"))
        for i in range(NTB):
            t.wait_ge(P.sem("LNN"), b_s2 + i + 1)
            for c in range(8):
                j = i * 8 + c
                if j >= 2:
                    t.wait_ge(P.sem("LEV"), b_ev + j - 1)
                ins = t.transpose(psT[j % 2][:], xn[i % 2][:, c * 128:(c + 1) * 128], G["ident_bf"][:])
                P.inc("tensor", ins, "LMM", 1)

    P.end_phase()
    st.close()


def mk_attn(wq, wk, wv, bq, bk, bv, qscale, relbias=False):
    def ph(P, G):
        import os
        rbmode = int(os.environ.get("RBMODE", "2"))
        block = G["block"]
        XHAT = G["XHAT"]; CTX = G["CTX"]

        st = ExitStack()
        nwb = 1 if relbias else 2
        wqs = [P.sb(f"at_wq{i}", [128, 1024], BF16, st) for i in range(nwb)]
        wks = [P.sb(f"at_wk{i}", [128, 1024], BF16, st) for i in range(nwb)]
        wvs = [P.sb(f"at_wv{i}", [128, 1024], BF16, st) for i in range(nwb)]
        Q2 = P.sb("at_Q2", [128, S_CORE], BF16, st)
        K2 = P.sb("at_K2", [128, S_CORE], BF16, st)
        V2 = P.sb("at_V2", [128, NTB * 130], BF16, st)
        PT = [P.sb(f"at_PT{i}", [128, 512], BF16, st) for i in range(2)]
        OO = [P.sb(f"at_OO{i}", [65, 512], F32, st) for i in range(2)]
        lrec = P.sb("at_lrec", [128, 2], F32, st)
        bqr = P.sb("at_bqr", [1, D], BF16, st)
        bkr = P.sb("at_bkr", [1, D], BF16, st)
        bvr = P.sb("at_bvr", [1, D], BF16, st)
        psP = [P.ps(f"at_psP{i}", [128, 512], F32, st) for i in range(2)]
        psS = [P.ps(f"at_psS{i}", [128, 512], F32, st) for i in range(2)]
        psO = [P.ps(f"at_psO{i}", [65, 512], F32, st) for i in range(2)]
        psT = [P.ps(f"at_psT{i}", [128, 65], F32, st) for i in range(2)]
        if relbias:
            Bh = P.sb("at_Bh", [128, 8064], BF16, st)
            Gh = P.sb("at_Gh", [128, S_CORE], BF16, st)
            X2 = [P.sb(f"at_X2{i}", [128, 512], BF16, st) for i in range(2)]
            growrow = P.sb("at_growrow", [1, S_CORE], BF16, st)
            biasnorm_t = P.scratches["biasnorm"]
            grows_d = P.scratch("grows", [16, S_CORE])
            b_bl = P.val("BHL"); b_gh = P.val("GHM"); b_ghe = P.val("GHE"); b_vx = P.val("VX2")
            b_gr2 = P.val("GRL")

        wq_bf = bfview(P, wq, D)
        wk_bf = bfview(P, wk, D)
        wv_bf = bfview(P, wv, D)

        NHP, NQT, NKB = 8, 8, 32
        PROJ_G = 48

        b_wla = P.val("WLA"); b_wlb = P.val("WLB"); b_abr = P.val("ABR")
        b_pj = P.val("MMP"); b_eq = P.val("EVQ"); b_ev = P.val("EVV")
        b_smm = P.val("SMM"); b_exp = P.val("EXP"); b_av = P.val("AVD")
        b_oev = P.val("OEV"); b_tr = P.val("TRD"); b_cw = P.val("CTXW")
        b_bs = P.val("BSC"); b_rcp = P.val("RCP")

        def proj_war_wait(t, gg):
            g2 = gg - 2
            if g2 >= 0:
                lg = g2 % PROJ_G
                if lg < 16:
                    t.wait_ge(P.sem("EVQ"), b_eq + (g2 // PROJ_G) * 16 + lg + 1)
                else:
                    t.wait_ge(P.sem("EVV"), b_ev + (g2 // PROJ_G) * 32 + (lg - 16) + 1)

        @block.sync
        def _(s):
            P.pw(s)
            s.wait_ge(P.sem("WPREP"), P.val("WPREP"))
            for hp in range(NHP):
                if hp >= nwb:
                    s.wait_ge(P.sem("MMP"), b_pj + (hp - nwb + 1) * PROJ_G)
                wl = "WLA" if hp % 2 == 0 or nwb == 1 else "WLB"
                for dc in range(8):
                    i1 = s.dma_start_transpose(out=wqs[hp % nwb][:, dc * 128:(dc + 1) * 128],
                                               in_=wq_bf[hp * 128:(hp + 1) * 128, dc * 128:(dc + 1) * 128])
                    P.inc("sync", i1, wl, 16)
                    i2 = s.dma_start_transpose(out=wks[hp % nwb][:, dc * 128:(dc + 1) * 128],
                                               in_=wk_bf[hp * 128:(hp + 1) * 128, dc * 128:(dc + 1) * 128])
                    P.inc("sync", i2, wl, 16)
                    i3 = s.dma_start_transpose(out=wvs[hp % nwb][:, dc * 128:(dc + 1) * 128],
                                               in_=wv_bf[hp * 128:(hp + 1) * 128, dc * 128:(dc + 1) * 128])
                    P.inc("sync", i3, wl, 16)

        @block.gpsimd
        def _(g):
            P.pw(g)
            i1 = g.dma_start(out=bqr[0:1, :], in_=P.inp(bq, [D]).rearrange("(o a) -> o a", o=1))
            P.inc("gpsimd", i1, "ABR", 16)
            i2 = g.dma_start(out=bkr[0:1, :], in_=P.inp(bk, [D]).rearrange("(o a) -> o a", o=1))
            P.inc("gpsimd", i2, "ABR", 16)
            i3 = g.dma_start(out=bvr[0:1, :], in_=P.inp(bv, [D]).rearrange("(o a) -> o a", o=1))
            P.inc("gpsimd", i3, "ABR", 16)
            if relbias:
                for h in range(16):
                    if h >= 1:
                        g.wait_ge(P.sem("VX2"), b_vx + h * 8 * 32 * 2)
                        g.wait_ge(P.sem("GHM"), b_gh + h * 8)
                    ins = g.dma_start(out=growrow[0:1, :], in_=grows_d[h:h + 1, :])
                    P.inc("gpsimd", ins, "GRL", 16)
                    for part in range(2):
                        ap_in = bass.AP(biasnorm_t.tensor, h * 8192 + part * 4032,
                                        [[1, 128], [1, 4032]])
                        ins = g.dma_start(out=Bh[:, part * 4032:(part + 1) * 4032], in_=ap_in)
                        P.inc("gpsimd", ins, "BHL", 16)

        @block.vector
        def _(v):
            P.pw(v)
            v3 = V2.rearrange("p (b c) -> p b c", c=130)
            v.memset(v3[:, :, 64:65], 1.0)
            ins = v.memset(v3[:, :, 129:130], 1.0)
            P.inc("vector", ins, "BSC", 1)

        @block.tensor
        def _(t):
            P.pw(t)
            t.wait_ge(P.sem("ABR"), b_abr + 48)
            t.wait_ge(P.sem("BSC"), b_bs + 1)
            for hp in range(NHP):
                if nwb == 1:
                    t.wait_ge(P.sem("WLA"), b_wla + (hp + 1) * 24 * 16)
                else:
                    wl = "WLA" if hp % 2 == 0 else "WLB"
                    bwl = b_wla if hp % 2 == 0 else b_wlb
                    t.wait_ge(P.sem(wl), bwl + (hp // 2 + 1) * 24 * 16)
                for wi, (wsl, brow, dst) in enumerate(((wqs[hp % nwb], bqr, Q2), (wks[hp % nwb], bkr, K2))):
                    for tt in range(NQT):
                        gg = hp * PROJ_G + wi * 8 + tt
                        proj_war_wait(t, gg)
                        for dc in range(8):
                            t.matmul(psP[gg % 2][:], wsl[:, dc * 128:(dc + 1) * 128],
                                     XHAT[:, dc * S_CORE + tt * 512: dc * S_CORE + (tt + 1) * 512],
                                     start=(dc == 0), stop=False)
                        ins = t.matmul(psP[gg % 2][:], brow[0:1, hp * 128:(hp + 1) * 128],
                                       G["ones_bf"][0:1, 0:512], start=False, stop=True)
                        P.inc("tensor", ins, "MMP", 1)
                for tb in range(NTB):
                    gg = hp * PROJ_G + 16 + tb
                    proj_war_wait(t, gg)
                    for dc in range(8):
                        t.matmul(psP[gg % 2][:, 0:128],
                                 XHAT[:, dc * S_CORE + tb * 128: dc * S_CORE + (tb + 1) * 128],
                                 wvs[hp % nwb][:, dc * 128:(dc + 1) * 128],
                                 start=(dc == 0), stop=False)
                    ins = t.matmul(psP[gg % 2][:, 0:128], G["ones_bf"][0:1, 0:128],
                                   bvr[0:1, hp * 128:(hp + 1) * 128], start=False, stop=True)
                    P.inc("tensor", ins, "MMP", 1)
                t.wait_ge(P.sem("EVQ"), b_eq + (hp + 1) * 16)
                t.wait_ge(P.sem("EVV"), b_ev + (hp + 1) * 32)
                for hh in range(2):
                    h = 2 * hp + hh
                    r0 = 64 * hh
                    if relbias:
                        for gq in range(8):
                            if h * 8 + gq >= 2:
                                t.wait_ge(P.sem("GHE"), b_ghe + h * 8 + gq - 1)
                            if gq == 0:
                                t.wait_ge(P.sem("GRL"), b_gr2 + (h + 1) * 16)
                            ins = t.matmul(psS[gq % 2][:],
                                           G["ones_bf"][0:1, 0:128],
                                           growrow[0:1, gq * 512:(gq + 1) * 512],
                                           start=True, stop=True)
                            P.inc("tensor", ins, "GHM", 1)
                        t.wait_ge(P.sem("GHE"), b_ghe + (h + 1) * 8)
                        t.wait_ge(P.sem("BHL"), b_bl + (h + 1) * 32)
                    for qt in range(NQT):
                        gi = h * NQT + qt
                        if gi >= 2:
                            t.wait_ge(P.sem("OEV"), b_oev + gi - 1)
                        for kb in range(NKB):
                            ki = gi * NKB + kb
                            if not relbias:
                                if kb >= 2:
                                    t.wait_ge(P.sem("EXP"), b_exp + ki - 1)
                            else:
                                if kb >= 2:
                                    t.wait_ge(P.sem("VX2"), b_vx + 2 * ki - 2)
                            ins = t.matmul(psS[ki % 2][:],
                                           K2[r0:r0 + 64, kb * 128:(kb + 1) * 128],
                                           Q2[r0:r0 + 64, qt * 512:(qt + 1) * 512],
                                           start=True, stop=True)
                            P.inc("tensor", ins, "SMM", 1)
                            if kb >= 1:
                                t.wait_ge(P.sem("EXP"), b_exp + ki)
                                ins = t.matmul(psO[gi % 2][:],
                                               V2[:, (kb - 1) * 130 + 65 * hh: (kb - 1) * 130 + 65 * hh + 65],
                                               PT[(ki - 1) % 2][:],
                                               start=(kb - 1 == 0), stop=False)
                                P.inc("tensor", ins, "AVD", 1)
                        ki_last = gi * NKB + NKB - 1
                        t.wait_ge(P.sem("EXP"), b_exp + ki_last + 1)
                        ins = t.matmul(psO[gi % 2][:],
                                       V2[:, (NKB - 1) * 130 + 65 * hh: (NKB - 1) * 130 + 65 * hh + 65],
                                       PT[ki_last % 2][:],
                                       start=False, stop=True)
                        P.inc("tensor", ins, "AVD", 1)
                        t.wait_ge(P.sem("OEV"), b_oev + gi + 1)
                        for qb in range(4):
                            tj = gi * 4 + qb
                            if tj >= 2:
                                t.wait_ge(P.sem("CTXW"), b_cw + tj - 1)
                            ins = t.transpose(psT[tj % 2][:], OO[gi % 2][:, qb * 128:(qb + 1) * 128],
                                              G["ident_f32"][0:65, 0:65])
                            P.inc("tensor", ins, "TRD", 1)

        @block.scalar
        def _(sc):
            P.pw(sc)
            for hp in range(NHP):
                for wi, (dst, scale) in enumerate(((Q2, qscale), (K2, 1.0))):
                    for tt in range(NQT):
                        gg = hp * PROJ_G + wi * 8 + tt
                        sc.wait_ge(P.sem("MMP"), b_pj + gg + 1)
                        ins = sc.activation(out=dst[:, tt * 512:(tt + 1) * 512], in_=psP[gg % 2][:],
                                            func=AF.Copy, scale=float(scale))
                        P.inc("scalar", ins, "EVQ", 1)
                for hh in range(2):
                    h = 2 * hp + hh
                    for qt in range(NQT):
                        gi = h * NQT + qt
                        for kb in range(NKB):
                            ki = gi * NKB + kb
                            if kb >= 2:
                                sc.wait_ge(P.sem("AVD"), b_av + ki - 1)
                            if not relbias:
                                sc.wait_ge(P.sem("SMM"), b_smm + ki + 1)
                                ins = sc.activation(out=PT[ki % 2][:], in_=psS[ki % 2][:], func=AF.Exp)
                            else:
                                sc.wait_ge(P.sem("VX2"), b_vx + 2 * ki + 2)
                                ins = sc.activation(out=PT[ki % 2][:], in_=X2[ki % 2][:], func=AF.Exp)
                            P.inc("scalar", ins, "EXP", 1)
                        sc.wait_ge(P.sem("AVD"), b_av + (gi + 1) * NKB)
                        if gi >= 2:
                            sc.wait_ge(P.sem("TRD"), b_tr + (gi - 1) * 4)
                        ins = sc.copy(out=OO[gi % 2][:, :], in_=psO[gi % 2][:, :])
                        P.inc("scalar", ins, "OEV", 1)

        @block.vector
        def _(v):
            for hp in range(NHP):
                for tb in range(NTB):
                    gg = hp * PROJ_G + 16 + tb
                    v.wait_ge(P.sem("MMP"), b_pj + gg + 1)
                    ins = v.tensor_copy(
                        out=V2[:, tb * 130: tb * 130 + 130].rearrange("p (g c) -> p g c", c=65)[:, :, 0:64],
                        in_=psP[gg % 2].rearrange("p (g c) -> p g c", c=64)[:, 0:2, :])
                    P.inc("vector", ins, "EVV", 1)
                for hh in range(2):
                    h = 2 * hp + hh
                    if relbias:
                        for gq in range(8):
                            v.wait_ge(P.sem("GHM"), b_gh + h * 8 + gq + 1)
                            ins = v.tensor_copy(out=Gh[:, gq * 512:(gq + 1) * 512], in_=psS[gq % 2][:])
                            P.inc("vector", ins, "GHE", 1)
                    for qt in range(NQT):
                        gi = h * NQT + qt
                        if relbias:
                            for kb in range(NKB):
                                ki = gi * NKB + kb
                                v.wait_ge(P.sem("SMM"), b_smm + ki + 1)
                                if kb >= 2:
                                    v.wait_ge(P.sem("EXP"), b_exp + ki - 1)
                                off = 128 * (kb - 4 * qt + 31) + 127
                                if rbmode >= 2:
                                    bap = bass.AP(Bh.tensor if hasattr(Bh, "tensor") else Bh,
                                                  off, [[8064, 128], [-1, 512]])
                                else:
                                    bap = Bh[:, 0:512]
                                ins = v.tensor_tensor(out=X2[ki % 2][:],
                                                      in0=Gh[:, qt * 512:(qt + 1) * 512],
                                                      in1=bap, op=ALU.mult)
                                P.inc("vector", ins, "VX2", 1)
                                v.wait_ge(P.sem("VX2"), P.val("VX2"))
                                ins = v.tensor_tensor(out=X2[ki % 2][:], in0=psS[ki % 2][:],
                                                      in1=X2[ki % 2][:], op=ALU.add)
                                P.inc("vector", ins, "VX2", 1)
                        for qb in range(4):
                            tj = gi * 4 + qb
                            tb = qt * 4 + qb
                            v.wait_ge(P.sem("TRD"), b_tr + tj + 1)
                            ins = v.reciprocal(out=lrec[:, 0:1], in_=psT[tj % 2][:, 64:65])
                            P.inc("vector", ins, "RCP", 1)
                            v.wait_ge(P.sem("RCP"), b_rcp + tj + 1)
                            ins = v.tensor_tensor(
                                out=CTX[:, tb * 1024 + h * 64: tb * 1024 + h * 64 + 64],
                                in0=psT[tj % 2][:, 0:64],
                                in1=lrec[:, 0:1].broadcast_to([128, 64]), op=ALU.mult)
                            P.inc("vector", ins, "CTXW", 1)

        P.end_phase()
        st.close()
    return ph


def mk_proj_residual(wname, bname, gname=None):
    def ph(P, G):
        block = G["block"]
        CTX = G["CTX"]
        par = G["par"]
        xsrc = G["x_state"][par]
        xdst = G["x_state"][1 - par]
        G["par"] = 1 - par

        st = ExitStack()
        wos = P.sb("pr_wos", [128, 8 * 1024], BF16, st)
        bor = P.sb("pr_bor", [1, D], BF16, st)
        cfm = P.sb("pr_cfm", [128, 8 * 128], BF16, st)
        xold = [P.sb(f"pr_xold{i}", [128, D], F32, st) for i in range(2)]
        xnew = [P.sb(f"pr_xnew{i}", [128, D], F32, st) for i in range(2)]
        if gname is not None:
            gtile = [P.sb(f"pr_gt{i}", [128, D], F32, st) for i in range(2)]
            gstage = P.sb("pr_gstage", [128, 512], F32, st)
            gsrc = P.scratch(gname, [S_CORE, D])
        psT = [P.ps(f"pr_psT{i}", [128, 128], BF16, st) for i in range(2)]
        psU = [P.ps(f"pr_psU{i}", [128, 512], F32, st) for i in range(4)]

        w_bf = bfview(P, wname, D)

        b_wl = P.val("PWL"); b_la = P.val("PLA"); b_lb = P.val("PLB")
        b_tr = P.val("PTR"); b_te = P.val("PTE")
        b_mm = P.val("PMM"); b_ad = P.val("PAD"); b_sg = P.val("STG"); b_bo = P.val("PBOR")

        npt = 16 if gname is None else 32

        @block.sync
        def _(s):
            P.pw(s)
            s.wait_ge(P.sem("WPREP"), P.val("WPREP"))
            for dc in range(8):
                ins = s.dma_start_transpose(out=wos[:, dc * 1024:(dc + 1) * 1024],
                                            in_=w_bf[0:1024, dc * 128:(dc + 1) * 128])
                P.inc("sync", ins, "PWL", 16)
            for tb in range(NTB):
                if tb >= 2:
                    s.wait_ge(P.sem("PAD"), b_ad + (tb - 1) * 2)
                ins = s.dma_start(out=xold[tb % 2][:], in_=xsrc[tb * 128:(tb + 1) * 128, :])
                P.inc("sync", ins, "PLA" if tb % 2 == 0 else "PLB", 16)
                if gname is not None:
                    ins = s.dma_start(out=gtile[tb % 2][:], in_=gsrc[tb * 128:(tb + 1) * 128, :])
                    P.inc("sync", ins, "PLA" if tb % 2 == 0 else "PLB", 16)

        @block.gpsimd
        def _(g):
            P.pw(g)
            if bname is not None:
                ins = g.dma_start(out=bor[0:1, :], in_=P.inp(bname, [D]).rearrange("(o a) -> o a", o=1))
                P.inc("gpsimd", ins, "PBOR", 16)
            for tb in range(NTB):
                g.wait_ge(P.sem("PAD"), b_ad + tb * 2 + 2)
                ins = g.dma_start(out=xdst[tb * 128:(tb + 1) * 128, :], in_=xnew[tb % 2][:])
                P.inc("gpsimd", ins, "STG", 16)

        @block.tensor
        def _(t):
            P.pw(t)
            t.wait_ge(P.sem("PWL"), b_wl + 8 * 16)
            if bname is not None:
                t.wait_ge(P.sem("PBOR"), b_bo + 16)
            for tb in range(NTB):
                for dc in range(8):
                    j = tb * 8 + dc
                    if j >= 2:
                        j2 = j - 2
                        t.wait_ge(P.sem("PTE"), b_te + (j2 // 8) * 9 + (j2 % 8) + 1)
                    ins = t.transpose(psT[j % 2][:], CTX[:, tb * 1024 + dc * 128: tb * 1024 + (dc + 1) * 128],
                                      G["ident_bf"][:])
                    P.inc("tensor", ins, "PTR", 1)
                for half in range(2):
                    u = tb * 2 + half
                    if u >= 4:
                        t.wait_ge(P.sem("PAD"), b_ad + u - 3)
                    for dc in range(8):
                        t.wait_ge(P.sem("PTE"), b_te + tb * 9 + dc + 1)
                        ins = t.matmul(psU[u % 4][:], cfm[:, dc * 128:(dc + 1) * 128],
                                       wos[:, dc * 1024 + half * 512: dc * 1024 + (half + 1) * 512],
                                       start=(dc == 0), stop=(dc == 7 and bname is None))
                    if bname is not None:
                        ins = t.matmul(psU[u % 4][:], G["ones_bf"][0:1, 0:128],
                                       bor[0:1, half * 512:(half + 1) * 512], start=False, stop=True)
                    P.inc("tensor", ins, "PMM", 1)

        @block.vector
        def _(v):
            P.pw(v)
            for tb in range(NTB):
                for dc in range(8):
                    j = tb * 8 + dc
                    v.wait_ge(P.sem("PTR"), b_tr + j + 1)
                    if tb >= 1:
                        v.wait_ge(P.sem("PMM"), b_mm + tb * 2)
                    ins = v.tensor_copy(out=cfm[:, dc * 128:(dc + 1) * 128], in_=psT[j % 2][:])
                    P.inc("vector", ins, "PTE", 1)
                ins = v.tensor_copy(out=cfm[:, 0:1], in_=cfm[:, 0:1])
                P.inc("vector", ins, "PTE", 1)
                for half in range(2):
                    u = tb * 2 + half
                    v.wait_ge(P.sem("PMM"), b_mm + u + 1)
                    if tb % 2 == 0:
                        v.wait_ge(P.sem("PLA"), b_la + (tb // 2 + 1) * npt)
                    else:
                        v.wait_ge(P.sem("PLB"), b_lb + (tb // 2 + 1) * npt)
                    if tb >= 2:
                        v.wait_ge(P.sem("STG"), b_sg + (tb - 1) * 16)
                    if gname is not None:
                        v.tensor_tensor(out=gstage[:, :], in0=psU[u % 4][:],
                                        in1=gtile[tb % 2][:, half * 512:(half + 1) * 512],
                                        op=ALU.mult)
                        ins = v.tensor_tensor(
                            out=xnew[tb % 2][:, half * 512:(half + 1) * 512],
                            in0=gstage[:, :],
                            in1=xold[tb % 2][:, half * 512:(half + 1) * 512],
                            op=ALU.add)
                    else:
                        ins = v.scalar_tensor_tensor(
                            out=xnew[tb % 2][:, half * 512:(half + 1) * 512],
                            in0=psU[u % 4][:], scalar=1.0,
                            in1=xold[tb % 2][:, half * 512:(half + 1) * 512],
                            op0=ALU.mult, op1=ALU.add)
                    P.inc("vector", ins, "PAD", 1)

        P.end_phase()
        st.close()
    return ph


def mk_dump(which):
    def ph(P, G):
        block = G["block"]
        buf = G[which]
        out = P.outp("dump_" + which, [128, 8 * S_CORE], F32)
        st = ExitStack()
        tmp = [P.sb(f"dp_tmp{i}", [128, 4096], F32, st) for i in range(2)]
        b_dc = P.val("DPC"); b_da = P.val("DPSA"); b_db = P.val("DPSB")

        @block.vector
        def _(v):
            P.pw(v)
            for c in range(8):
                if c >= 2:
                    if c % 2 == 0:
                        v.wait_ge(P.sem("DPSA"), b_da + (c // 2) * 16)
                    else:
                        v.wait_ge(P.sem("DPSB"), b_db + (c // 2) * 16)
                ins = v.tensor_copy(out=tmp[c % 2][:], in_=buf[:, c * 4096:(c + 1) * 4096])
                P.inc("vector", ins, "DPC", 1)

        @block.sync
        def _(s):
            P.pw(s)
            for c in range(8):
                s.wait_ge(P.sem("DPC"), b_dc + c + 1)
                ins = s.dma_start(out=out[:, c * 4096:(c + 1) * 4096], in_=tmp[c % 2][:])
                P.inc("sync", ins, "DPSA" if c % 2 == 0 else "DPSB", 16)

        P.end_phase()
        st.close()
    return ph


def ph_copyout(P, G):
    block = G["block"]

    @block.sync
    def _(s):
        P.pw(s)
        ins = s.dma_start(out=P.outputs["y"][:, :], in_=G["x_state"][G["par"]][:, :])
        P.inc("sync", ins, "YOUT", 16)
        s.wait_ge(P.sem("YOUT"), P.val("YOUT"))

    P.end_phase()




def mk_moe_l0():
    def ph(P, G):
        block = G["block"]
        XHAT = G["XHAT"]; CTX = G["CTX"]
        NE, DFF = 8, 4096
        NQ = 4
        QF = DFF // NQ
        NTT = 8
        NQE = NE * NQ

        w1_in = P.inp("L0_ffn_w1", [NE, D, DFF])
        w2_in = P.inp("L0_ffn_w2", [NE, DFF, D])
        gw_in = P.inp("L0_ffn_gate_w", [D, NE])
        b1_in = P.inp("L0_ffn_b1", [NE, DFF])
        b2_in = P.inp("L0_ffn_b2", [NE, D])

        st = ExitStack()
        gw = P.sb("mo_gw", [128, 8 * 8], BF16, st)
        CB = P.sb("mo_CB", [128, NTB * 8], F32, st)
        rcols = P.sb("mo_rcols", [128, 8], F32, st)
        retile = P.sb("mo_retile", [128, 8], F32, st)
        rmask = P.sb("mo_rmask", [128, 8], F32, st)
        wb1 = P.sb("mo_wb1", [128, 8 * QF], BF16, st)
        wb2 = P.sb("mo_wb2", [128, 8 * D], BF16, st)
        b1c = P.sb("mo_b1c", [128, 8], F32, st)
        b2r = P.sb("mo_b2r", [1, D], BF16, st)
        hb = P.sb("mo_hb", [128, 8 * 512], BF16, st)
        psR = P.ps("mo_psR", [128, 8], F32, st)
        psH = [P.ps(f"mo_psH{i}", [128, 512], F32, st) for i in range(2)]
        psU = [P.ps(f"mo_psU{i}", [128, 512], F32, st) for i in range(4)]

        b_gl = P.val("MGL"); b_rm = P.val("MRM"); b_rv = P.val("MRV")
        b_re = P.val("MRE"); b_rr = P.val("MRR"); b_rc = P.val("MRC")
        b_w1 = P.val("MW1"); b_w2 = P.val("MW2")
        b_hm = P.val("MHM"); b_he = P.val("MHE"); b_ym = P.val("MYM"); b_ya = P.val("MYA")
        b_cz = P.val("MCZ")

        mw2_after = []
        tot = 0
        for qe in range(NQE):
            tot += 8 * 16 + (16 if qe % NQ == 0 else 0)
            mw2_after.append(tot)

        @block.gpsimd
        def _(g):
            P.pw(g)
            for dc in range(8):
                ins = g.dma_start(out=gw[:, dc * 8:(dc + 1) * 8], in_=gw_in[dc * 128:(dc + 1) * 128, :])
                P.inc("gpsimd", ins, "MGL", 16)
            for qe in range(NQE):
                e, q = qe // NQ, qe % NQ
                if qe >= 1:
                    g.wait_ge(P.sem("MHM"), b_hm + qe * NTT * 8)
                for dc in range(8):
                    ins = g.dma_start(out=wb1[:, dc * QF:(dc + 1) * QF],
                                      in_=w1_in[e, dc * 128:(dc + 1) * 128, q * QF:(q + 1) * QF])
                    P.inc("gpsimd", ins, "MW1", 16)
                ins = g.dma_start(out=b1c[:, 0:8],
                                  in_=b1_in[e, q * QF:(q + 1) * QF].rearrange("(c p) -> p c", p=128),
                                  allow_slow_non_contiguous=True)
                P.inc("gpsimd", ins, "MW1", 16)
                if qe >= 1:
                    g.wait_ge(P.sem("MYM"), b_ym + qe * NTT * 8)
                for fc in range(8):
                    ins = g.dma_start(out=wb2[:, fc * D:(fc + 1) * D],
                                      in_=w2_in[e, q * QF + fc * 128: q * QF + (fc + 1) * 128, :])
                    P.inc("gpsimd", ins, "MW2", 16)
                if q == 0:
                    ins = g.dma_start(out=b2r[0:1, :], in_=b2_in[e, :].rearrange("(o a) -> o a", o=1))
                    P.inc("gpsimd", ins, "MW2", 16)

        @block.vector
        def _(v):
            P.pw(v)
            ins = v.memset(CTX[:], 0.0)
            P.inc("vector", ins, "MCZ", 1)
            for tb in range(NTB):
                v.wait_ge(P.sem("MRM"), b_rm + tb + 1)
                ins = v.reduce_max(out=rcols[:, 0:1], in_=psR[:], axis=AX.X)
                P.inc("vector", ins, "MRV", 1)
                v.wait_ge(P.sem("MRV"), b_rv + tb * 2 + 1)
                ins = v.tensor_scalar(out=rcols[:, 1:2], in0=rcols[:, 0:1], scalar1=-1.0,
                                      scalar2=None, op0=ALU.mult)
                P.inc("vector", ins, "MRV", 1)
                v.wait_ge(P.sem("MRE"), b_re + tb + 1)
                ins = v.reciprocal(out=rcols[:, 3:4], in_=rcols[:, 2:3])
                P.inc("vector", ins, "MRR", 1)
                v.wait_ge(P.sem("MRR"), b_rr + tb * 3 + 1)
                ins = v.tensor_single_scalar(out=rmask[:, :], in_=retile[:, :], scalar=1.0, op=ALU.is_equal)
                P.inc("vector", ins, "MRR", 1)
                v.wait_ge(P.sem("MRR"), b_rr + tb * 3 + 2)
                ins = v.tensor_tensor(out=CB[:, tb * 8:(tb + 1) * 8], in0=rmask[:, :],
                                      in1=rcols[:, 3:4].broadcast_to([128, 8]), op=ALU.mult)
                P.inc("vector", ins, "MRR", 1)
                ins = v.tensor_copy(out=rcols[:, 4:5], in_=rcols[:, 3:4])
                P.inc("vector", ins, "MRC", 1)

        @block.scalar
        def _(sc):
            P.pw(sc)
            for tb in range(NTB):
                sc.wait_ge(P.sem("MRV"), b_rv + tb * 2 + 2)
                if tb >= 1:
                    sc.wait_ge(P.sem("MRC"), b_rc + tb)
                ins = sc.activation(out=retile[:, :], in_=psR[:], func=AF.Exp,
                                    bias=rcols[:, 1:2], accum_out=rcols[:, 2:3])
                P.inc("scalar", ins, "MRE", 1)
            for qe in range(NQE):
                for tt in range(NTT):
                    sc.wait_ge(P.sem("MYM"), b_ym + (qe * NTT + tt) * 8)
                    for fc in range(8):
                        gi = (qe * NTT + tt) * 8 + fc
                        sc.wait_ge(P.sem("MHM"), b_hm + gi + 1)
                        ins = sc.activation(out=hb[:, fc * 512:(fc + 1) * 512],
                                            in_=psH[gi % 2][:], func=AF.Gelu,
                                            bias=b1c[:, fc:fc + 1])
                        P.inc("scalar", ins, "MHE", 1)

        @block.tensor
        def _(t):
            P.pw(t)
            t.wait_ge(P.sem("MGL"), b_gl + 128)
            for tb in range(NTB):
                if tb >= 1:
                    t.wait_ge(P.sem("MRE"), b_re + tb)
                for dc in range(8):
                    ins = t.matmul(psR[:], XHAT[:, dc * S_CORE + tb * 128: dc * S_CORE + (tb + 1) * 128],
                                   gw[:, dc * 8:(dc + 1) * 8], start=(dc == 0), stop=(dc == 7))
                P.inc("tensor", ins, "MRM", 1)
            for qe in range(NQE):
                e, q = qe // NQ, qe % NQ
                t.wait_ge(P.sem("MW1"), b_w1 + (qe + 1) * 9 * 16)
                t.wait_ge(P.sem("MW2"), b_w2 + mw2_after[qe])
                for tt in range(NTT):
                    for fc in range(8):
                        gi = (qe * NTT + tt) * 8 + fc
                        if gi >= 2:
                            t.wait_ge(P.sem("MHE"), b_he + gi - 1)
                        for dc in range(8):
                            ins = t.matmul(psH[gi % 2][:],
                                           wb1[:, dc * QF + fc * 128: dc * QF + (fc + 1) * 128],
                                           XHAT[:, dc * S_CORE + tt * 512: dc * S_CORE + (tt + 1) * 512],
                                           start=(dc == 0), stop=(dc == 7))
                        P.inc("tensor", ins, "MHM", 1)
                    t.wait_ge(P.sem("MHE"), b_he + (qe * NTT + tt + 1) * 8)
                    for tk in range(4):
                        for half in range(2):
                            u = ((qe * NTT + tt) * 4 + tk) * 2 + half
                            if u >= 4:
                                t.wait_ge(P.sem("MYA"), b_ya + u - 3)
                            for fc in range(8):
                                ins = t.matmul(psU[u % 4][:],
                                               hb[:, fc * 512 + tk * 128: fc * 512 + (tk + 1) * 128],
                                               wb2[:, fc * D + half * 512: fc * D + (half + 1) * 512],
                                               start=(fc == 0), stop=(fc == 7 and q != 0))
                            if q == 0:
                                ins = t.matmul(psU[u % 4][:], G["ones_bf"][0:1, 0:128],
                                               b2r[0:1, half * 512:(half + 1) * 512],
                                               start=False, stop=True)
                            P.inc("tensor", ins, "MYM", 1)

        @block.vector
        def _(v):
            for qe in range(NQE):
                e = qe // NQ
                for tt in range(NTT):
                    for tk in range(4):
                        tb = tt * 4 + tk
                        for half in range(2):
                            u = ((qe * NTT + tt) * 4 + tk) * 2 + half
                            v.wait_ge(P.sem("MYM"), b_ym + u + 1)
                            ins = v.scalar_tensor_tensor(
                                out=CTX[:, tb * 1024 + half * 512: tb * 1024 + (half + 1) * 512],
                                in0=psU[u % 4][:], scalar=CB[:, tb * 8 + e: tb * 8 + e + 1],
                                in1=CTX[:, tb * 1024 + half * 512: tb * 1024 + (half + 1) * 512],
                                op0=ALU.mult, op1=ALU.add)
                            P.inc("vector", ins, "MYA", 1)

        P.end_phase()
        st.close()
    return ph


def mk_residual_add():
    def ph(P, G):
        block = G["block"]
        CTX = G["CTX"]
        par = G["par"]
        xsrc = G["x_state"][par]
        xdst = G["x_state"][1 - par]
        G["par"] = 1 - par

        st = ExitStack()
        xold = [P.sb(f"ra_xold{i}", [128, D], F32, st) for i in range(2)]
        xnew = [P.sb(f"ra_xnew{i}", [128, D], F32, st) for i in range(2)]
        b_la = P.val("RLA"); b_lb = P.val("RLB"); b_ad = P.val("RAD"); b_sg = P.val("STG")

        @block.sync
        def _(s):
            P.pw(s)
            for tb in range(NTB):
                if tb >= 2:
                    s.wait_ge(P.sem("RAD"), b_ad + tb - 1)
                ins = s.dma_start(out=xold[tb % 2][:], in_=xsrc[tb * 128:(tb + 1) * 128, :])
                P.inc("sync", ins, "RLA" if tb % 2 == 0 else "RLB", 16)

        @block.vector
        def _(v):
            P.pw(v)
            for tb in range(NTB):
                if tb % 2 == 0:
                    v.wait_ge(P.sem("RLA"), b_la + (tb // 2 + 1) * 16)
                else:
                    v.wait_ge(P.sem("RLB"), b_lb + (tb // 2 + 1) * 16)
                if tb >= 2:
                    v.wait_ge(P.sem("STG"), b_sg + (tb - 1) * 16)
                ins = v.tensor_tensor(out=xnew[tb % 2][:], in0=xold[tb % 2][:],
                                      in1=CTX[:, tb * 1024:(tb + 1) * 1024], op=ALU.add)
                P.inc("vector", ins, "RAD", 1)

        @block.gpsimd
        def _(g):
            P.pw(g)
            for tb in range(NTB):
                g.wait_ge(P.sem("RAD"), b_ad + tb + 1)
                ins = g.dma_start(out=xdst[tb * 128:(tb + 1) * 128, :], in_=xnew[tb % 2][:])
                P.inc("gpsimd", ins, "STG", 16)

        P.end_phase()
        st.close()
    return ph


def l0_ffn_phases():
    return [ph_ln, mk_moe_l0(), mk_residual_add()]


def mk_moe_l1():
    def ph(P, G):
        import os
        ROUTER_ONLY = bool(os.environ.get("L1F_ROUTER_ONLY"))
        block = G["block"]
        XHAT = G["XHAT"]; CTX = G["CTX"]
        NE, FI = 16, 512
        NTT = 8

        rw_bf = bfview(P, "L1_ffn_router_weight", D)
        wg_bf = bfview(P, "L1_ffn_w_gate", D)
        wu_bf = bfview(P, "L1_ffn_w_up", D)
        wd_bf = bfview(P, "L1_ffn_w_down", FI)
        rb_in = P.inp("L1_ffn_router_bias", [NE])

        st = ExitStack()
        rw = P.sb("m1_rw", [128, 8 * NE], BF16, st)
        rbr = P.sb("m1_rbr", [1, NE], F32, st)
        rbb = P.sb("m1_rbb", [128, NE], F32, st)
        CB = P.sb("m1_CB", [128, NTB * NE], F32, st)
        sco = P.sb("m1_sco", [128, NE], F32, st)
        sch = P.sb("m1_sch", [128, NE], F32, st)
        cur = P.sb("m1_cur", [128, NE], F32, st)
        curB = P.sb("m1_curB", [128, NE], F32, st)
        msk = P.sb("m1_msk", [128, NE], F32, st)
        wv = P.sb("m1_wv", [128, NE], F32, st)
        neg2 = P.sb("m1_neg2", [128, NE], F32, st)
        msk8 = P.sb("m1_msk8", [128, NE], mybir.dt.uint8, st)
        gm8 = P.sb("m1_gm8", [128, 4], mybir.dt.uint8, st)
        gsc = P.sb("m1_gsc", [128, 40], F32, st)
        rc = P.sb("m1_rc", [128, 8], F32, st)
        wgT = P.sb("m1_wgT", [128, 8 * FI], BF16, st)
        wuT = P.sb("m1_wuT", [128, 8 * FI], BF16, st)
        wdT = P.sb("m1_wdT", [128, 4 * D], BF16, st)
        gb = P.sb("m1_gb", [128, 4 * 512], BF16, st)
        gu = P.sb("m1_gu", [128, 4 * 512], BF16, st)
        psR = P.ps("m1_psR", [128, 16], F32, st)
        psB = P.ps("m1_psB", [128, 16], F32, st)
        psG = [P.ps(f"m1_psG{i}", [128, 512], F32, st) for i in range(2)]
        psU = [P.ps(f"m1_psU{i}", [128, 512], F32, st) for i in range(4)]

        b_rl = P.val("NRL"); b_rm = P.val("NRM"); b_rsg = P.val("NSG")
        b_rv = P.val("NRV"); b_rc = P.val("NRC")
        b_wl = P.val("NWL"); b_gm = P.val("NGM"); b_ge = P.val("NGE")
        b_um = P.val("NUM"); b_gu = P.val("NGU"); b_ym = P.val("NYM"); b_ya = P.val("NYA")
        b_cz = P.val("NCZ"); b_bb = P.val("NBB")

        @block.sync
        def _(s):
            P.pw(s)
            s.wait_ge(P.sem("WPREP"), P.val("WPREP"))
            for dc in range(8):
                ins = s.dma_start_transpose(out=rw[:, dc * NE:(dc + 1) * NE],
                                            in_=rw_bf[0:16, dc * 128:(dc + 1) * 128])
                P.inc("sync", ins, "NRL", 16)
            for e in range(NE if not ROUTER_ONLY else 0):
                if e >= 1:
                    s.wait_ge(P.sem("NYM"), b_ym + e * NTT * 8)
                for dc in range(8):
                    i1 = s.dma_start_transpose(out=wgT[:, dc * FI:(dc + 1) * FI],
                                               in_=wg_bf[e * FI:(e + 1) * FI, dc * 128:(dc + 1) * 128])
                    P.inc("sync", i1, "NWL", 16)
                    i2 = s.dma_start_transpose(out=wuT[:, dc * FI:(dc + 1) * FI],
                                               in_=wu_bf[e * FI:(e + 1) * FI, dc * 128:(dc + 1) * 128])
                    P.inc("sync", i2, "NWL", 16)
                for ic in range(4):
                    i3 = s.dma_start_transpose(out=wdT[:, ic * D:(ic + 1) * D],
                                               in_=wd_bf[e * D:(e + 1) * D, ic * 128:(ic + 1) * 128])
                    P.inc("sync", i3, "NWL", 16)

        @block.gpsimd
        def _(g):
            P.pw(g)
            ins = g.dma_start(out=rbr[0:1, :], in_=rb_in.rearrange("(o a) -> o a", o=1))
            P.inc("gpsimd", ins, "NBB", 16)

        @block.tensor
        def _(t):
            P.pw(t)
            t.wait_ge(P.sem("NRL"), b_rl + 128)
            t.wait_ge(P.sem("NBB"), b_bb + 16)
            ins = t.matmul(psB[:], G["ones_f32"][0:1, 0:128], rbr[0:1, :], start=True, stop=True)
            P.inc("tensor", ins, "NRM", 1)
            for tb in range(NTB):
                if tb >= 1:
                    t.wait_ge(P.sem("NSG"), b_rsg + tb)
                for dc in range(8):
                    ins = t.matmul(psR[:], XHAT[:, dc * S_CORE + tb * 128: dc * S_CORE + (tb + 1) * 128],
                                   rw[:, dc * NE:(dc + 1) * NE], start=(dc == 0), stop=(dc == 7))
                P.inc("tensor", ins, "NRM", 1)
            for e in range(NE if not ROUTER_ONLY else 0):
                t.wait_ge(P.sem("NWL"), b_wl + (e + 1) * 20 * 16)
                for tt in range(NTT):
                    t.wait_ge(P.sem("NYA"), b_ya + (e * NTT + tt) * 8)
                    for fc in range(4):
                        gi = (e * NTT + tt) * 4 + fc
                        if gi >= 2:
                            t.wait_ge(P.sem("NGE"), b_ge + gi - 1)
                        for dc in range(8):
                            ins = t.matmul(psG[gi % 2][:],
                                           wgT[:, dc * FI + fc * 128: dc * FI + (fc + 1) * 128],
                                           XHAT[:, dc * S_CORE + tt * 512: dc * S_CORE + (tt + 1) * 512],
                                           start=(dc == 0), stop=(dc == 7))
                        P.inc("tensor", ins, "NGM", 1)
                        uu = gi % 4
                        t.wait_ge(P.sem("NGU"), b_gu + max(0, gi - 3))
                        for dc in range(8):
                            ins = t.matmul(psU[uu][:],
                                           wuT[:, dc * FI + fc * 128: dc * FI + (fc + 1) * 128],
                                           XHAT[:, dc * S_CORE + tt * 512: dc * S_CORE + (tt + 1) * 512],
                                           start=(dc == 0), stop=(dc == 7))
                        P.inc("tensor", ins, "NUM", 1)
                    t.wait_ge(P.sem("NGU"), b_gu + (e * NTT + tt + 1) * 4)
                    for tk in range(4):
                        for half in range(2):
                            u = ((e * NTT + tt) * 4 + tk) * 2 + half
                            uu = u % 4
                            t.wait_ge(P.sem("NYA"), b_ya + max(0, u - 3))
                            for ic in range(4):
                                ins = t.matmul(psU[uu][:],
                                               gu[:, ic * 512 + tk * 128: ic * 512 + (tk + 1) * 128],
                                               wdT[:, ic * D + half * 512: ic * D + (half + 1) * 512],
                                               start=(ic == 0), stop=(ic == 3))
                            P.inc("tensor", ins, "NYM", 1)

        @block.scalar
        def _(sc):
            P.pw(sc)
            for tb in range(NTB):
                sc.wait_ge(P.sem("NRM"), b_rm + tb + 2)
                if tb >= 1:
                    sc.wait_ge(P.sem("NRC"), b_rc + tb)
                ins = sc.activation(out=sco[:, :], in_=psR[:], func=AF.Sigmoid)
                P.inc("scalar", ins, "NSG", 1)
            for e in range(NE if not ROUTER_ONLY else 0):
                for tt in range(NTT):
                    sc.wait_ge(P.sem("NGU"), b_gu + (e * NTT + tt) * 4)
                    for fc in range(4):
                        gi = (e * NTT + tt) * 4 + fc
                        sc.wait_ge(P.sem("NGM"), b_gm + gi + 1)
                        ins = sc.activation(out=gb[:, fc * 512:(fc + 1) * 512],
                                            in_=psG[gi % 2][:], func=AF.Silu)
                        P.inc("scalar", ins, "NGE", 1)

        @block.vector
        def _(v):
            P.pw(v)
            ins = v.memset(CTX[:], 0.0)
            P.inc("vector", ins, "NCZ", 1)
            ins = v.memset(neg2[:, :], -2.0e9)
            P.inc("vector", ins, "NCZ", 1)
            v.wait_ge(P.sem("NRM"), b_rm + 1)
            ins = v.tensor_copy(out=rbb[:, :], in_=psB[:])
            P.inc("vector", ins, "NCZ", 1)
            RT = "NRT"
            rt = P.val(RT)
            def step(instr):
                nonlocal rt
                rt = P.inc("vector", instr, RT, 1)
                v.wait_ge(P.sem(RT), rt)
            sgT = P.sb("m1_sgT", [128, NE], F32, st)
            m16 = P.sb("m1_m16", [128, NE], mybir.dt.uint8, st)
            zt = P.sb("m1_zt", [128, NE], F32, st)
            step0 = v.memset(zt[:, :], 0.0)
            P.inc("vector", step0, "NRT", 1)
            pairs = [(0, 1), (0, 2), (0, 3), (1, 2), (1, 3), (2, 3)]
            for tb in range(NTB):
                v.wait_ge(P.sem("NSG"), b_rsg + tb + 1)
                step(v.tensor_tensor(out=sch[:, :], in0=sco[:, :], in1=rbb[:, :], op=ALU.add))
                step(v.tensor_copy(out=sgT[:, :].rearrange("p (k g) -> p k g", g=4),
                                  in_=sch[:, :].rearrange("p (g k) -> p k g", k=4)))
                for pi, (a, b) in enumerate(pairs):
                    step(v.tensor_tensor(out=gsc[:, pi * 4:(pi + 1) * 4],
                                         in0=sgT[:, a * 4:(a + 1) * 4], in1=sgT[:, b * 4:(b + 1) * 4],
                                         op=ALU.add))
                step(v.tensor_tensor(out=gsc[:, 24:28], in0=gsc[:, 0:4], in1=gsc[:, 4:8], op=ALU.max))
                step(v.tensor_tensor(out=gsc[:, 24:28], in0=gsc[:, 24:28], in1=gsc[:, 8:12], op=ALU.max))
                step(v.tensor_tensor(out=gsc[:, 24:28], in0=gsc[:, 24:28], in1=gsc[:, 12:16], op=ALU.max))
                step(v.tensor_tensor(out=gsc[:, 24:28], in0=gsc[:, 24:28], in1=gsc[:, 16:20], op=ALU.max))
                step(v.tensor_tensor(out=gsc[:, 24:28], in0=gsc[:, 24:28], in1=gsc[:, 20:24], op=ALU.max))
                gg = gsc[:, 24:28]
                for pi, (a, b) in enumerate(pairs):
                    step(v.tensor_tensor(out=gsc[:, 28 + pi:29 + pi],
                                         in0=gg[:, a:a + 1], in1=gg[:, b:b + 1], op=ALU.min))
                step(v.tensor_tensor(out=rc[:, 0:1], in0=gsc[:, 28:29], in1=gsc[:, 29:30], op=ALU.max))
                step(v.tensor_tensor(out=rc[:, 0:1], in0=rc[:, 0:1], in1=gsc[:, 30:31], op=ALU.max))
                step(v.tensor_tensor(out=rc[:, 0:1], in0=rc[:, 0:1], in1=gsc[:, 31:32], op=ALU.max))
                step(v.tensor_tensor(out=rc[:, 0:1], in0=rc[:, 0:1], in1=gsc[:, 32:33], op=ALU.max))
                step(v.tensor_tensor(out=rc[:, 0:1], in0=rc[:, 0:1], in1=gsc[:, 33:34], op=ALU.max))
                step(v.tensor_tensor(out=gm8[:, :], in0=gsc[:, 24:28],
                                     in1=rc[:, 0:1].broadcast_to([128, 4]), op=ALU.is_ge))
                for g8 in range(4):
                    step(v.tensor_copy(out=m16[:, g8 * 4:(g8 + 1) * 4],
                                       in_=gm8[:, g8:g8 + 1].broadcast_to([128, 4])))
                step(v.select(out=cur[:, :], mask=m16[:, :], on_true=sch[:, :], on_false=neg2[:, :]))
                bufs = [cur, curB]
                for r in range(4):
                    src_t = bufs[r % 2]
                    step(v.reduce_max(out=rc[:, 1:2], in_=src_t[:, :], axis=AX.X))
                    if r < 3:
                        step(v.tensor_tensor(out=msk8[:, :], in0=src_t[:, :],
                                             in1=rc[:, 1:2].broadcast_to([128, NE]), op=ALU.is_ge))
                        step(v.select(out=bufs[(r + 1) % 2][:, :], mask=msk8[:, :],
                                      on_true=neg2[:, :], on_false=src_t[:, :]))
                step(v.select(out=msk[:, :], mask=m16[:, :], on_true=sch[:, :], on_false=neg2[:, :]))
                step(v.tensor_tensor(out=msk8[:, :], in0=msk[:, :],
                                     in1=rc[:, 1:2].broadcast_to([128, NE]), op=ALU.is_ge))
                step(v.select(out=wv[:, :], mask=msk8[:, :], on_true=sco[:, :], on_false=zt[:, :]))
                step(v.reduce_sum(out=rc[:, 2:3], in_=wv[:, :], axis=AX.X))
                step(v.tensor_scalar(out=rc[:, 3:4], in0=rc[:, 2:3], scalar1=1e-20,
                                     scalar2=None, op0=ALU.add))
                step(v.reciprocal(out=rc[:, 4:5], in_=rc[:, 3:4]))
                step(v.tensor_scalar(out=rc[:, 5:6], in0=rc[:, 4:5], scalar1=2.5,
                                     scalar2=None, op0=ALU.mult))
                ins = v.tensor_tensor(out=CB[:, tb * NE:(tb + 1) * NE], in0=wv[:, :],
                                      in1=rc[:, 5:6].broadcast_to([128, NE]), op=ALU.mult)
                P.inc("vector", ins, "NRC", 1)
            if ROUTER_ONLY:
                v.wait_ge(P.sem("NRC"), b_rc + NTB)
                ins = v.tensor_copy(out=CTX[:, 0:NTB * NE], in_=CB[:, :])
                P.inc("vector", ins, "NYA", 1)
                P.end_phase()
                st.close()
                return
            for e in range(NE):
                for tt in range(NTT):
                    v.wait_ge(P.sem("NYM"), b_ym + (e * NTT + tt) * 8)
                    for fc in range(4):
                        gi = (e * NTT + tt) * 4 + fc
                        uu = gi % 4
                        v.wait_ge(P.sem("NGE"), b_ge + gi + 1)
                        v.wait_ge(P.sem("NUM"), b_um + gi + 1)
                        ins = v.tensor_tensor(out=gu[:, fc * 512:(fc + 1) * 512],
                                              in0=gb[:, fc * 512:(fc + 1) * 512],
                                              in1=psU[uu][:], op=ALU.mult)
                        P.inc("vector", ins, "NGU", 1)
                    for tk in range(4):
                        tb = tt * 4 + tk
                        for half in range(2):
                            u = ((e * NTT + tt) * 4 + tk) * 2 + half
                            v.wait_ge(P.sem("NYM"), b_ym + u + 1)
                            if e == 0 and tt == 0:
                                v.wait_ge(P.sem("NCZ"), b_cz + 3)
                                v.wait_ge(P.sem("NRC"), b_rc + NTB)
                            ins = v.scalar_tensor_tensor(
                                out=CTX[:, tb * 1024 + half * 512: tb * 1024 + (half + 1) * 512],
                                in0=psU[u % 4][:], scalar=CB[:, tb * NE + e: tb * NE + e + 1],
                                in1=CTX[:, tb * 1024 + half * 512: tb * 1024 + (half + 1) * 512],
                                op0=ALU.mult, op1=ALU.add)
                            P.inc("vector", ins, "NYA", 1)

        P.end_phase()
        st.close()
    return ph


def l1_ffn_phases():
    return [
        mk_prep([("L1_ffn_router_weight", [16, D]), ("L1_ffn_w_gate", [16, 512, D]),
                 ("L1_ffn_w_up", [16, 512, D]), ("L1_ffn_w_down", [16, D, 512])]),
        ph_ln, mk_moe_l1(), mk_residual_add(),
    ]


def mk_relbias_prep():
    def ph(P, G):
        block = G["block"]
        XHAT = G["XHAT"]
        oh = P.inp("onehot_norm", [320, 8192])
        re_in = P.inp("L1_attn_b_rel_embed", [320, 16])
        gw_in = P.inp("L1_attn_b_gru_W", [8, 64])
        gb_in = P.inp("L1_attn_b_gru_b", [8])
        gc_in = P.inp("L1_attn_b_gru_const", [1, 16, 1, 1])
        biasnorm = P.scratch("biasnorm", [16, 8192])

        st = ExitStack()
        ret = P.sb("rb_ret", [128, 3 * 16], F32, st)
        oht = [P.sb(f"rb_oht{i}", [128, 3 * 512], F32, st) for i in range(2)]
        bout = [P.sb(f"rb_bout{i}", [16, 512], F32, st) for i in range(2)]
        gwL = P.sb("rb_gwL", [8, 64], F32, st)
        gwTT = P.sb("rb_gwTT", [128, 8], BF16, st)
        gbr = P.sb("rb_gbr", [1, 8], BF16, st)
        gcr = P.sb("rb_gcr", [1, 16], F32, st)
        gcc = P.sb("rb_gcc", [128, 16], F32, st)
        gcols = P.sb("rb_gcols", [128, 8], F32, st)
        gsig = P.sb("rb_gsig", [128, 2], F32, st)
        gout = P.sb("rb_gout", [128, 1], F32, st)
        growrow = P.sb("rb_growrow", [1, S_CORE], F32, st)
        grows_d = P.scratch("grows", [16, S_CORE])
        psB = [P.ps(f"rb_psB{i}", [16, 512], F32, st) for i in range(2)]
        psG = P.ps("rb_psG", [128, 8], F32, st)
        psC = P.ps("rb_psC", [128, 16], F32, st)
        psW = P.ps("rb_psW", [128, 8], F32, st)
        psR = P.ps("rb_psR", [1, 128], F32, st)

        b_ol = P.val("RBOA"); b_ob = P.val("RBOB"); b_rl = P.val("RBRL")
        b_mm = P.val("RBMM"); b_bv = P.val("RBEV"); b_bs = P.val("RBST")
        b_gl = P.val("RBGL"); b_gm = P.val("RBGM"); b_gs = P.val("RBGS")
        b_sg = P.val("RBSG"); b_gv = P.val("RBGV"); b_gt = P.val("RBGT"); b_gr = P.val("RBGR")
        b_ge = P.val("RBGE")
        b_gw2 = P.val("RBGW2")
        b_gwst = P.val("RBGW")

        NSL = 16
        NG = 16 * NTB

        @block.sync
        def _(s):
            P.pw(s)
            i = s.dma_start(out=ret[:, 0:16], in_=re_in[0:128, :])
            P.inc("sync", i, "RBRL", 16)
            i = s.dma_start(out=ret[:, 16:32], in_=re_in[128:256, :])
            P.inc("sync", i, "RBRL", 16)
            i = s.dma_start(out=ret[0:64, 32:48], in_=re_in[256:320, :])
            P.inc("sync", i, "RBRL", 16)
            i = s.dma_start(out=gwL[:, :], in_=gw_in[:, :])
            P.inc("sync", i, "RBRL", 16)
            i = s.dma_start(out=gcr[0:1, :], in_=gc_in.rearrange("o h a b -> (o a b) h"))
            P.inc("sync", i, "RBRL", 16)
            for sl in range(NSL):
                par = "RBOA" if sl % 2 == 0 else "RBOB"
                if sl >= 2:
                    s.wait_ge(P.sem("RBMM"), b_mm + sl - 1)
                i1 = s.dma_start(out=oht[sl % 2][:, 0:512], in_=oh[0:128, sl * 512:(sl + 1) * 512])
                P.inc("sync", i1, par, 16)
                i2 = s.dma_start(out=oht[sl % 2][:, 512:1024], in_=oh[128:256, sl * 512:(sl + 1) * 512])
                P.inc("sync", i2, par, 16)
                i3 = s.dma_start(out=oht[sl % 2][0:64, 1024:1536], in_=oh[256:320, sl * 512:(sl + 1) * 512])
                P.inc("sync", i3, par, 16)

        @block.gpsimd
        def _(g):
            P.pw(g)
            i = g.dma_start(out=gbr[0:1, :], in_=gb_in.rearrange("(o a) -> o a", o=1))
            P.inc("gpsimd", i, "RBGL", 16)

            for sl in range(NSL):
                g.wait_ge(P.sem("RBEV"), b_bv + sl + 1)
                ins = g.dma_start(out=biasnorm[:, sl * 512:(sl + 1) * 512], in_=bout[sl % 2][:, :])
                P.inc("gpsimd", ins, "RBST", 16)
            g.wait_ge(P.sem("RBGE"), b_ge + 1)
            ins = g.dma_start(out=gwTT[64:128, :], in_=gwTT[0:64, :])
            P.inc("gpsimd", ins, "RBGW2", 16)
            for h in range(16):
                g.wait_ge(P.sem("RBGR"), b_gr + (h + 1) * NTB)
                ins = g.dma_start(out=grows_d[h:h + 1, :], in_=growrow[0:1, :])
                P.inc("gpsimd", ins, "RBGW", 16)

        @block.tensor
        def _(t):
            P.pw(t)
            t.wait_ge(P.sem("RBRL"), b_rl + 80)
            ins = t.transpose(psW[0:64, :], gwL[0:8, 0:64], G["ident_f32"][0:8, 0:8])
            P.inc("tensor", ins, "RBGM", 1)
            ins = t.matmul(psC[:], G["ones_f32"][0:1, 0:128], gcr[0:1, :], start=True, stop=True)
            P.inc("tensor", ins, "RBGM", 1)
            for sl in range(NSL):
                if sl % 2 == 0:
                    t.wait_ge(P.sem("RBOA"), b_ol + (sl // 2 + 1) * 48)
                else:
                    t.wait_ge(P.sem("RBOB"), b_ob + (sl // 2 + 1) * 48)
                if sl >= 2:
                    t.wait_ge(P.sem("RBEV"), b_bv + sl - 1)
                t.matmul(psB[sl % 2][:], ret[:, 0:16], oht[sl % 2][:, 0:512], start=True, stop=False)
                t.matmul(psB[sl % 2][:], ret[:, 16:32], oht[sl % 2][:, 512:1024], start=False, stop=False)
                ins = t.matmul(psB[sl % 2][:], ret[0:64, 32:48], oht[sl % 2][0:64, 1024:1536],
                               start=False, stop=True)
                P.inc("tensor", ins, "RBMM", 1)
            t.wait_ge(P.sem("RBGE"), b_ge + 2)
            t.wait_ge(P.sem("RBGW2"), b_gw2 + 16)
            t.wait_ge(P.sem("RBGL"), b_gl + 16)
            for h in range(16):
                for tb in range(NTB):
                    gi = h * NTB + tb
                    if gi >= 1:
                        t.wait_ge(P.sem("RBGS"), b_gs + gi)
                    r0g = 64 * (h % 2)
                    t.matmul(psG[:], XHAT[r0g:r0g + 64,
                                          (h // 2) * S_CORE + tb * 128:(h // 2) * S_CORE + (tb + 1) * 128],
                             gwTT[r0g:r0g + 64, :], start=True, stop=False)
                    ins = t.matmul(psG[:], G["ones_bf"][0:1, 0:128], gbr[0:1, :], start=False, stop=True)
                    P.inc("tensor", ins, "RBGM", 1)
                    t.wait_ge(P.sem("RBGV"), b_gv + gi + 1)
                    if gi >= 1:
                        t.wait_ge(P.sem("RBGR"), b_gr + gi)
                    ins = t.transpose(psR[:, :], gout[:, 0:1], G["ident_f32"][:, :])
                    P.inc("tensor", ins, "RBGT", 1)

        @block.vector
        def _(v):
            P.pw(v)
            for sl in range(NSL):
                v.wait_ge(P.sem("RBMM"), b_mm + sl + 1)
                if sl >= 2:
                    v.wait_ge(P.sem("RBST"), b_bs + (sl - 1) * 16)
                ins = v.tensor_copy(out=bout[sl % 2][:, :], in_=psB[sl % 2][:, :])
                P.inc("vector", ins, "RBEV", 1)
            v.wait_ge(P.sem("RBGM"), b_gm + 2)
            ins = v.tensor_copy(out=gwTT[0:64, :], in_=psW[0:64, :])
            P.inc("vector", ins, "RBGE", 1)
            ins = v.tensor_copy(out=gcc[:, :], in_=psC[:, :])
            P.inc("vector", ins, "RBGE", 1)
            RT = "RBRT"
            rt = P.val(RT)
            def step(instr):
                nonlocal rt
                rt = P.inc("vector", instr, RT, 1)
                v.wait_ge(P.sem(RT), rt)
            for h in range(16):
                for tb in range(NTB):
                    gi = h * NTB + tb
                    v.wait_ge(P.sem("RBGM"), b_gm + 2 + gi + 1)
                    if gi >= 1:
                        v.wait_ge(P.sem("RBSG"), b_sg + gi)
                    step(v.reduce_sum(out=gcols[:, 0:1], in_=psG[:, 0:4], axis=AX.X))
                    ins = v.reduce_sum(out=gcols[:, 1:2], in_=psG[:, 4:8], axis=AX.X)
                    P.inc("vector", ins, "RBGS", 1)
                    v.wait_ge(P.sem("RBSG"), b_sg + gi + 1)
                    step(v.tensor_tensor(out=gcols[:, 3:4], in0=gsig[:, 1:2], in1=gcc[:, h:h + 1], op=ALU.mult))
                    step(v.tensor_scalar(out=gcols[:, 4:5], in0=gcols[:, 3:4], scalar1=-1.0,
                                         scalar2=None, op0=ALU.add))
                    step(v.tensor_tensor(out=gcols[:, 5:6], in0=gsig[:, 0:1], in1=gcols[:, 4:5], op=ALU.mult))
                    ins = v.tensor_scalar(out=gout[:, 0:1], in0=gcols[:, 5:6], scalar1=2.0,
                                          scalar2=None, op0=ALU.add)
                    P.inc("vector", ins, "RBGV", 1)
                    v.wait_ge(P.sem("RBGT"), b_gt + gi + 1)
                    if tb == 0 and h >= 1:
                        v.wait_ge(P.sem("RBGW"), b_gwst + h * 16)
                    ins = v.tensor_copy(out=growrow[0:1, tb * 128:(tb + 1) * 128],
                                        in_=psR[0:1, :])
                    P.inc("vector", ins, "RBGR", 1)

        @block.scalar
        def _(sc):
            P.pw(sc)
            for h in range(16):
                for tb in range(NTB):
                    gi = h * NTB + tb
                    sc.wait_ge(P.sem("RBGS"), b_gs + gi + 1)
                    if gi >= 1:
                        sc.wait_ge(P.sem("RBGV"), b_gv + gi)
                    ins = sc.activation(out=gsig[:, 0:2], in_=gcols[:, 0:2], func=AF.Sigmoid)
                    P.inc("scalar", ins, "RBSG", 1)

        P.end_phase()
        st.close()
    return ph


class Chain:

    def __init__(self, P, name):
        self.P = P
        self.name = name
        self.n = P.val(name)

    def do(self, engname, eng, instr):
        self.n = self.P.inc(engname, instr, self.name, 1)
        return instr

    def wait(self, eng):
        eng.wait_ge(self.P.sem(self.name), self.n)


def mk_mamba():
    def ph(P, G):
        block = G["block"]
        XHAT = G["XHAT"]; CTX = G["CTX"]
        H, NS, CH, PD, PROJ = 16, 16, 64, 64, 2576
        NCH = S_CORE // CH
        yscr = P.scratch("mamba_y", [S_CORE, D])

        win_bf = bfview(P, "L0_attn_b_W_in", D)
        dtw_in = P.inp("L0_attn_b_dt_W", [16, 16])
        dtb_in = P.inp("L0_attn_b_dt_b", [16])
        al_in = P.inp("L0_attn_b_A_log", [16])
        dp_in = P.inp("L0_attn_b_D_param", [16])
        nw_in = P.inp("L0_attn_b_norm_w", [D])
        rep16 = P.inp("rep16", [16, 1024])
        maskrep = P.inp("maskrep", [64, 1024])

        st = ExitStack()
        wTf = P.sb("mb_wTf", [128, 8 * 448], BF16, st)
        repf = P.sb("mb_repf", [16, 1024], F32, st)
        mrep = P.sb("mb_mrep", [64, 1024], F32, st)
        dtwT = P.sb("mb_dtwT", [16, 16], BF16, st)
        prow = P.sb("mb_prow", [1, 64], F32, st)
        arow = P.sb("mb_arow", [1, 16], F32, st)
        acol = P.sb("mb_acol", [16, 1], F32, st)
        dcol = P.sb("mb_dcol", [16, 1], F32, st)
        dtbrow = P.sb("mb_dtbrow", [1, 16], BF16, st)
        nwrow = P.sb("mb_nwrow", [1, D], F32, st)
        nwb = P.sb("mb_nwb", [64, D], F32, st)
        dbc = P.sb("mb_dbc", [64, 1024], F32, st)
        drow = P.sb("mb_drow", [1, 1024], F32, st)
        pjz = P.sb("mb_pjz", [64, 1040], F32, st)
        pjb = P.sb("mb_pjb", [64, PROJ], BF16, st)
        dtlT = P.sb("mb_dtlT", [16, 64], BF16, st)
        dtt = P.sb("mb_dtt", [64, 16], F32, st)
        sp1 = P.sb("mb_sp1", [64, 16], F32, st)
        sp2 = P.sb("mb_sp2", [64, 16], F32, st)
        dtf = P.sb("mb_dtf", [16, 64], F32, st)
        dafm = P.sb("mb_dafm", [16, 64], F32, st)
        lcs = P.sb("mb_lcs", [16, 64], F32, st)
        zz16 = P.sb("mb_zz16", [16, 64], F32, st)
        t2m = P.sb("mb_t2m", [64, 1024], F32, st)
        tdt = P.sb("mb_tdt", [64, 1024], F32, st)
        e1 = P.sb("mb_e1", [64, 1024], F32, st)
        sS = P.sb("mb_sS", [64, 1024], BF16, st)
        bcfm = P.sb("mb_bcfm", [16, 2048], BF16, st)
        explf = P.sb("mb_explf", [16, 64], F32, st)
        expltm = P.sb("mb_expltm", [64, 16], F32, st)
        d2e = P.sb("mb_d2e", [16, 64], F32, st)
        ecol = P.sb("mb_ecol", [16, 1], F32, st)
        wx = P.sb("mb_wx", [64, 1024], BF16, st)
        stt_ = P.sb("mb_stt", [16, 1024], F32, st)
        stb = P.sb("mb_stb", [16, 1024], BF16, st)
        ytm = P.sb("mb_ytm", [64, 1024], F32, st)
        rcols = P.sb("mb_rcols", [64, 8], F32, st)
        psP = P.ps("mb_psP", [64, 512], F32, st)
        psA = P.ps("mb_psA", [64, 1024], F32, st)
        psB2 = P.ps("mb_psB2", [64, 1024], F32, st)
        psD2 = P.ps("mb_psD2", [16, 1024], F32, st)
        psBC = P.ps("mb_psBC", [16, 1024], BF16, st)

        b_wl = P.val("ZWL"); b_cl = P.val("ZCL")
        C = Chain(P, "ZCH")
        dmn = [P.val("ZDM")]
        dtn = [P.val("ZDT2")]


        @block.gpsimd
        def _(g):
            P.pw(g)
            for dst, srcap in [
                (repf[:, :], rep16[:, :]),
                (mrep[:, :], maskrep[:, :]),
                (prow[0:1, 0:16], al_in.rearrange("(o a) -> o a", o=1)),
                (prow[0:1, 32:48], dp_in.rearrange("(o a) -> o a", o=1)),
                (nwrow[0:1, :], nw_in.rearrange("(o a) -> o a", o=1)),
                (dtbrow[0:1, :], dtb_in.rearrange("(o a) -> o a", o=1)),
            ]:
                ins = g.dma_start(out=dst, in_=srcap)
                P.inc("gpsimd", ins, "ZCL", 16)
            ins = g.dma_start(out=dtwT[:, :], in_=dtw_in.rearrange("a b -> b a"),
                              allow_slow_non_contiguous=True)
            P.inc("gpsimd", ins, "ZCL", 16)

        script = []

        def op(engname, fn):
            script.append((engname, fn))

        def chain_op(engname, emit):
            def fn(h, emit=emit, engname=engname):
                C.wait(h)
                h.wait_ge(P.sem("ZDM"), dmn[0])
                h.wait_ge(P.sem("ZDT2"), dtn[0])
                ins = emit(h)
                C.do(engname, h, ins)
            op(engname, fn)

        def chain_dma_sync(emit):
            def fn(h, emit=emit):
                C.wait(h)
                h.wait_ge(P.sem("ZDM"), dmn[0])
                h.wait_ge(P.sem("ZDT2"), dtn[0])
                ins = emit(h)
                if ins is not None:
                    dtn[0] = P.inc("sync", ins, "ZDT2", 16)
            op("sync", fn)

        def chain_dma(emit):
            def fn(h, emit=emit):
                C.wait(h)
                h.wait_ge(P.sem("ZDM"), dmn[0])
                ins = emit(h)
                dmn[0] = P.inc("gpsimd", ins, "ZDM", 16)
            op("gpsimd", fn)

        chain_op("scalar", lambda sc: sc.activation(out=arow[0:1, :], in_=prow[0:1, 0:16], func=AF.Exp))
        chain_op("vector", lambda v: v.tensor_scalar(out=arow[0:1, :], in0=arow[0:1, :], scalar1=-1.0, scalar2=None, op0=ALU.mult))
        chain_op("tensor", lambda t: t.transpose(psA[0:16, 0:1], arow[0:1, 0:16], G["ident_f32"][0:1, 0:1]))
        chain_op("vector", lambda v: v.tensor_copy(out=acol[:, :], in_=psA[0:16, 0:1]))
        chain_op("tensor", lambda t: t.transpose(psA[0:16, 0:1], prow[0:1, 32:48], G["ident_f32"][0:1, 0:1]))
        chain_op("vector", lambda v: v.tensor_copy(out=dcol[:, :], in_=psA[0:16, 0:1]))
        def mm_drow(t):
            t.matmul(psA[0:1, 0:512], dcol[:, 0:1], repf[:, 0:512], start=True, stop=True)
            return t.matmul(psA[0:1, 512:1024], dcol[:, 0:1], repf[:, 512:1024], start=True, stop=True)
        chain_op("tensor", mm_drow)
        chain_op("vector", lambda v: v.tensor_copy(out=drow[0:1, :], in_=psA[0:1, 0:1024]))
        def mm_dbc(t):
            t.matmul(psA[:, 0:512], G["ones_f32"][0:1, 0:64], drow[0:1, 0:512], start=True, stop=True)
            return t.matmul(psA[:, 512:1024], G["ones_f32"][0:1, 0:64], drow[0:1, 512:1024], start=True, stop=True)
        chain_op("tensor", mm_dbc)
        chain_op("vector", lambda v: v.tensor_copy(out=dbc[:, :], in_=psA[0:64, :]))
        chain_op("tensor", lambda t: t.matmul(psP[:], G["ones_f32"][0:1, 0:64], nwrow[0:1, 0:512], start=True, stop=True))
        chain_op("vector", lambda v: v.tensor_copy(out=nwb[:, 0:512], in_=psP[:]))
        chain_op("tensor", lambda t: t.matmul(psP[:], G["ones_f32"][0:1, 0:64], nwrow[0:1, 512:1024], start=True, stop=True))
        chain_op("vector", lambda v: v.tensor_copy(out=nwb[:, 512:1024], in_=psP[:]))
        chain_op("vector", lambda v: v.memset(zz16[:, :], 0.0))
        chain_op("vector", lambda v: v.memset(stt_[:, :], 0.0))
        chain_op("vector", lambda v: v.memset(stb[:, :], 0.0))

        for c in range(NCH):
            tok0 = c * CH
            for sl in range(6):
                w0 = sl * 448
                wid = 448 if sl < 5 else PROJ - 5 * 448
                def ld_w(s, w0=w0, wid=wid):
                    for dc in range(8):
                        ins = s.dma_start_transpose(
                            out=wTf[:, dc * 448: dc * 448 + wid],
                            in_=win_bf[w0:w0 + wid, dc * 128:(dc + 1) * 128])
                        dtn[0] = P.inc("sync", ins, "ZDT2", 16)
                    return None
                chain_dma_sync(ld_w)
                def mm_sl(t, w0=w0, wid=wid, sl=sl, tok0=tok0):
                    last = None
                    for dc in range(8):
                        last = t.matmul(psP[:, 0:wid],
                                        XHAT[:, dc * S_CORE + tok0: dc * S_CORE + tok0 + CH],
                                        wTf[:, dc * 448: dc * 448 + wid],
                                        start=(dc == 0), stop=(dc == 7))
                    return last
                chain_op("tensor", mm_sl)
                def ev_sl(v, w0=w0, wid=wid, sl=sl):
                    last = v.tensor_copy(out=pjb[:, w0:w0 + wid], in_=psP[:, 0:wid])
                    lo = max(w0, 1024); hi = min(w0 + wid, 2048)
                    if lo < hi:
                        last = v.tensor_copy(out=pjz[:, lo - 1024:hi - 1024],
                                             in_=psP[:, lo - w0:hi - w0])
                    lo = max(w0, 2560); hi = min(w0 + wid, 2576)
                    if lo < hi:
                        last = v.tensor_copy(out=pjz[:, 1024 + lo - 2560:1024 + hi - 2560],
                                             in_=psP[:, lo - w0:hi - w0])
                    return last
                chain_op("vector", ev_sl)
            chain_op("tensor", lambda t: t.transpose(psA[0:16, 0:64], pjz[:, 1024:1040], G["ident_f32"][0:64, 0:64]))
            chain_op("vector", lambda v: v.tensor_copy(out=dtlT[:, :], in_=psA[0:16, 0:64]))
            def mm_dt(t):
                t.matmul(psP[:, 0:16], dtlT[:, :], dtwT[:, :], start=True, stop=False)
                return t.matmul(psP[:, 0:16], G["ones_bf"][0:1, 0:64], dtbrow[0:1, :], start=False, stop=True)
            chain_op("tensor", mm_dt)
            chain_op("vector", lambda v: v.tensor_copy(out=sp2[:, :], in_=psP[:, 0:16]))
            chain_op("vector", lambda v: v.tensor_scalar(out=sp1[:, :], in0=sp2[:, :], scalar1=-1.0, scalar2=None, op0=ALU.mult))
            chain_op("vector", lambda v: v.tensor_tensor(out=sp1[:, :], in0=sp1[:, :], in1=sp2[:, :], op=ALU.max))
            chain_op("scalar", lambda sc: sc.activation(out=dtt[:, :], in_=sp1[:, :], func=AF.Exp, scale=-1.0))
            chain_op("scalar", lambda sc: sc.activation(out=sp1[:, :], in_=dtt[:, :], func=AF.Ln, bias=1.0))
            chain_op("vector", lambda v: v.tensor_relu(out=sp2[:, :], in_=sp2[:, :]))
            chain_op("vector", lambda v: v.tensor_tensor(out=dtt[:, :], in0=sp2[:, :], in1=sp1[:, :], op=ALU.add))
            chain_op("tensor", lambda t: t.transpose(psA[0:16, 0:64], dtt[:, :], G["ident_f32"][0:64, 0:64]))
            chain_op("vector", lambda v: v.tensor_copy(out=dtf[:, :], in_=psA[0:16, 0:64]))
            chain_op("vector", lambda v: v.tensor_tensor(out=dafm[:, :], in0=dtf[:, :], in1=acol[:, 0:1].broadcast_to([16, 64]), op=ALU.mult))
            chain_op("vector", lambda v: v.tensor_tensor_scan(out=lcs[:, :], data0=dafm[:, :], data1=zz16[:, :], initial=0.0, op0=ALU.add, op1=ALU.add))
            def mm_t2(t):
                t.matmul(psA[:, 0:512], lcs[:, :], repf[:, 0:512], start=True, stop=True)
                return t.matmul(psA[:, 512:1024], lcs[:, :], repf[:, 512:1024], start=True, stop=True)
            chain_op("tensor", mm_t2)
            chain_op("vector", lambda v: v.tensor_copy(out=t2m[:, :], in_=psA[0:64, :]))
            def mm_tdt(t):
                t.matmul(psA[:, 0:512], dtf[:, :], repf[:, 0:512], start=True, stop=True)
                return t.matmul(psA[:, 512:1024], dtf[:, :], repf[:, 512:1024], start=True, stop=True)
            chain_op("tensor", mm_tdt)
            chain_op("vector", lambda v: v.tensor_copy(out=tdt[:, :], in_=psA[0:64, :]))
            def mm_t1(t):
                last = None
                for h in range(16):
                    last = t.transpose(psB2[:, h * 64:(h + 1) * 64], t2m[:, h * 64:(h + 1) * 64],
                                       G["ident_f32"][0:64, 0:64])
                return last
            chain_op("tensor", mm_t1)
            chain_op("vector", lambda v: v.tensor_tensor(out=e1[:, :], in0=psB2[0:64, :], in1=t2m[:, :], op=ALU.subtract))
            chain_op("vector", lambda v: v.tensor_tensor(out=e1[:, :], in0=e1[:, :], in1=mrep[:, :], op=ALU.add))
            chain_op("scalar", lambda sc: sc.activation(out=e1[:, :], in_=e1[:, :], func=AF.Exp))
            def mm_bc(t):
                last = None
                for h in range(16):
                    last = t.transpose(psBC[:, h * 64:(h + 1) * 64], pjb[:, 2048 + h * 16:2048 + (h + 1) * 16],
                                       G["ident_bf"][0:64, 0:64])
                return last
            chain_op("tensor", mm_bc)
            chain_op("vector", lambda v: v.tensor_copy(out=bcfm[:, 0:1024], in_=psBC[0:16, :]))
            def mm_cc(t):
                last = None
                for h in range(16):
                    last = t.transpose(psBC[:, h * 64:(h + 1) * 64], pjb[:, 2304 + h * 16:2304 + (h + 1) * 16],
                                       G["ident_bf"][0:64, 0:64])
                return last
            chain_op("tensor", mm_cc)
            chain_op("vector", lambda v: v.tensor_copy(out=bcfm[:, 1024:2048], in_=psBC[0:16, :]))
            def mm_sc(t):
                last = None
                for h in range(16):
                    last = t.matmul(psA[:, h * 64:(h + 1) * 64], bcfm[0:16, h * 64:(h + 1) * 64],
                                    bcfm[0:16, 1024 + h * 64:1024 + (h + 1) * 64], start=True, stop=True)
                return last
            chain_op("tensor", mm_sc)
            chain_op("vector", lambda v: v.tensor_tensor(out=sS[:, :], in0=psA[0:64, :], in1=e1[:, :], op=ALU.mult))
            chain_op("vector", lambda v: v.tensor_tensor(out=sS[:, :], in0=sS[:, :], in1=tdt[:, :], op=ALU.mult))
            def mm_yi(t):
                last = None
                for h in range(16):
                    last = t.matmul(psB2[:, h * 64:(h + 1) * 64], sS[:, h * 64:(h + 1) * 64],
                                    pjb[:, h * 64:(h + 1) * 64], start=True, stop=True)
                return last
            chain_op("tensor", mm_yi)
            def mm_yc(t):
                last = None
                for h in range(16):
                    last = t.matmul(psA[:, h * 64:(h + 1) * 64], bcfm[0:16, 1024 + h * 64:1024 + (h + 1) * 64],
                                    stb[:, h * 64:(h + 1) * 64], start=True, stop=True)
                return last
            chain_op("tensor", mm_yc)
            chain_op("scalar", lambda sc: sc.activation(out=explf[:, :], in_=lcs[:, :], func=AF.Exp))
            chain_op("tensor", lambda t: t.transpose(psP[:, 0:16], explf[:, :], G["ident_f32"][0:16, 0:16]))
            chain_op("vector", lambda v: v.tensor_copy(out=expltm[:, :], in_=psP[0:64, 0:16]))
            chain_op("vector", lambda v: v.tensor_copy(out=ytm[:, :], in_=psB2[0:64, :]))
            def sc_yc(v):
                last = None
                for h in range(16):
                    last = v.scalar_tensor_tensor(out=ytm[:, h * 64:(h + 1) * 64], in0=psA[0:64, h * 64:(h + 1) * 64],
                                                  scalar=expltm[:, h:h + 1], in1=ytm[:, h * 64:(h + 1) * 64],
                                                  op0=ALU.mult, op1=ALU.add)
                return last
            chain_op("vector", sc_yc)
            def d_term(v):
                v.tensor_tensor(out=e1[:, :], in0=pjb[:, 0:1024], in1=dbc[:, :], op=ALU.mult)
                return v.tensor_tensor(out=ytm[:, :], in0=ytm[:, :], in1=e1[:, :], op=ALU.add)
            chain_op("vector", d_term)
            chain_op("vector", lambda v: v.tensor_tensor(out=d2e[:, :], in0=lcs[:, 63:64].broadcast_to([16, 64]), in1=lcs[:, :], op=ALU.subtract))
            chain_op("scalar", lambda sc: sc.activation(out=d2e[:, :], in_=d2e[:, :], func=AF.Exp))
            def mm_d2e(t):
                t.matmul(psA[:, 0:512], d2e[:, :], repf[:, 0:512], start=True, stop=True)
                return t.matmul(psA[:, 512:1024], d2e[:, :], repf[:, 512:1024], start=True, stop=True)
            chain_op("tensor", mm_d2e)
            chain_op("vector", lambda v: v.tensor_tensor(out=e1[:, :], in0=psA[0:64, :], in1=tdt[:, :], op=ALU.mult))
            chain_op("vector", lambda v: v.tensor_tensor(out=wx[:, :], in0=e1[:, :], in1=pjb[:, 0:1024], op=ALU.mult))
            def mm_cs(t):
                last = None
                for h in range(16):
                    last = t.matmul(psD2[:, h * 64:(h + 1) * 64], pjb[:, 2048 + h * 16:2048 + (h + 1) * 16],
                                    wx[:, h * 64:(h + 1) * 64], start=True, stop=True)
                return last
            chain_op("tensor", mm_cs)
            chain_op("vector", lambda v: v.tensor_copy(out=ecol[:, :], in_=explf[:, 63:64]))
            def mm_cd(t):
                t.matmul(psA[0:1, 0:512], ecol[:, 0:1], repf[:, 0:512], start=True, stop=True)
                return t.matmul(psA[0:1, 512:1024], ecol[:, 0:1], repf[:, 512:1024], start=True, stop=True)
            chain_op("tensor", mm_cd)
            chain_op("vector", lambda v: v.tensor_copy(out=drow[0:1, :], in_=psA[0:1, 0:1024]))
            def mm_cdb2(t):
                t.matmul(psB2[0:16, 0:512], G["ones_f32"][0:1, 0:16], drow[0:1, 0:512], start=True, stop=True)
                return t.matmul(psB2[0:16, 512:1024], G["ones_f32"][0:1, 0:16], drow[0:1, 512:1024], start=True, stop=True)
            chain_op("tensor", mm_cdb2)
            chain_op("vector", lambda v: v.tensor_tensor(out=stt_[:, :], in0=stt_[:, :], in1=psB2[0:16, :], op=ALU.mult))
            chain_op("vector", lambda v: v.tensor_tensor(out=stt_[:, :], in0=stt_[:, :], in1=psD2[0:16, :], op=ALU.add))
            chain_op("vector", lambda v: v.tensor_copy(out=stb[:, :], in_=stt_[:, :]))
            chain_op("scalar", lambda sc: sc.activation(out=t2m[:, :], in_=pjz[:, 0:1024], func=AF.Silu))
            chain_op("scalar", lambda sc: sc.activation(out=e1[:, :], in_=ytm[:, :], func=AF.Square, accum_out=rcols[:, 0:1]))
            chain_op("vector", lambda v: v.tensor_scalar(out=rcols[:, 1:2], in0=rcols[:, 0:1], scalar1=1.0 / D, scalar2=1e-6, op0=ALU.mult, op1=ALU.add))
            chain_op("scalar", lambda sc: sc.activation(out=rcols[:, 2:3], in_=rcols[:, 1:2], func=AF.Ln))
            chain_op("scalar", lambda sc: sc.activation(out=rcols[:, 3:4], in_=rcols[:, 2:3], func=AF.Exp, scale=-0.5))
            chain_op("vector", lambda v: v.tensor_tensor(out=ytm[:, :], in0=ytm[:, :], in1=rcols[:, 3:4].broadcast_to([64, D]), op=ALU.mult))
            chain_op("vector", lambda v: v.tensor_tensor(out=ytm[:, :], in0=ytm[:, :], in1=nwb[:, :], op=ALU.mult))
            chain_op("vector", lambda v: v.tensor_tensor(out=ytm[:, :], in0=ytm[:, :], in1=t2m[:, :], op=ALU.mult))
            def st_y(g, tok0=tok0):
                return g.dma_start(out=yscr[tok0:tok0 + CH, :], in_=ytm[:, :])
            chain_dma(st_y)

        for tb in range(NTB):
            def ld2(g, tb=tb):
                return g.dma_start(out=CTX[:, tb * 1024:(tb + 1) * 1024],
                                   in_=yscr[tb * 128:(tb + 1) * 128, :])
            chain_dma(ld2)

        def emit():
            i = 0
            n = len(script)
            first = {e: True for e in ("tensor", "vector", "scalar", "gpsimd", "sync")}
            while i < n:
                eng = script[i][0]
                j = i
                fns = []
                while j < n and script[j][0] == eng:
                    fns.append(script[j][1])
                    j += 1
                sec = getattr(block, eng)

                def body(h, fns=fns, eng=eng):
                    if first[eng]:
                        P.pw(h)
                        h.wait_ge(P.sem("ZCL"), P.val("ZCL"))
                        if eng == "sync":
                            h.wait_ge(P.sem("WPREP"), P.val("WPREP"))
                        first[eng] = False
                    for f in fns:
                        f(h)
                sec(body)
                i = j

        emit()
        P.end_phase()
        st.close()
    return ph


def l0_attn_b_phases():
    return [
        mk_prep([("L0_attn_b_W_in", [2576, D]), ("L0_attn_b_W_out", [D, D])]),
        ph_ln,
        mk_mamba(),
        mk_proj_residual("L0_attn_b_W_out", None),
    ]


def _serial_chain(P, G, prefix):
    C = Chain(P, prefix + "CH")
    dmn = [P.val(prefix + "DM")]
    dtn = [P.val(prefix + "DT")]
    script = []

    def chain_op(engname, emit_fn):
        def fn(h, emit_fn=emit_fn, engname=engname):
            C.wait(h)
            h.wait_ge(P.sem(prefix + "DM"), dmn[0])
            h.wait_ge(P.sem(prefix + "DT"), dtn[0])
            ins = emit_fn(h)
            C.do(engname, h, ins)
        script.append((engname, fn))

    def chain_dma_syncN(emit_fn):
        def fn(h, emit_fn=emit_fn):
            C.wait(h)
            h.wait_ge(P.sem(prefix + "DM"), dmn[0])
            h.wait_ge(P.sem(prefix + "DT"), dtn[0])

            def inc16(ins):
                dtn[0] = P.inc("sync", ins, prefix + "DT", 16)
            emit_fn(h, inc16)
        script.append(("sync", fn))

    def chain_dma(emit_fn):
        def fn(h, emit_fn=emit_fn):
            C.wait(h)
            h.wait_ge(P.sem(prefix + "DM"), dmn[0])
            ins = emit_fn(h)
            dmn[0] = P.inc("gpsimd", ins, prefix + "DM", 16)
        script.append(("gpsimd", fn))

    def emit():
        block = G["block"]
        i = 0
        n = len(script)
        first = {e: True for e in ("tensor", "vector", "scalar", "gpsimd", "sync")}
        while i < n:
            eng = script[i][0]
            j = i
            fns = []
            while j < n and script[j][0] == eng:
                fns.append(script[j][1])
                j += 1
            sec = getattr(block, eng)

            def body(h, fns=fns, eng=eng):
                if first[eng]:
                    P.pw(h)
                    if eng == "sync":
                        h.wait_ge(P.sem("WPREP"), P.val("WPREP"))
                    first[eng] = False
                for f in fns:
                    f(h)
            sec(body)
            i = j
    return chain_op, chain_dma_syncN, chain_dma, emit


def mk_delta_pre():
    def ph(P, G):
        XHAT = G["XHAT"]
        qkv_bf = bfview(P, "L1_attn_a_qkv_w", D)
        ctrl_bf = bfview(P, "L1_attn_a_ctrl_w", D)
        ctrl_w_in = P.inp("L1_attn_a_ctrl_w", [1056, D])
        cb_in = P.inp("L1_attn_a_ctrl_b", [1056])
        convs = [P.inp("L1_attn_a_convq_w", [D, 4]), P.inp("L1_attn_a_convk_w", [D, 4]),
                 P.inp("L1_attn_a_convv_w", [D, 4])]
        tscr = [P.scratch("dn_q", [S_CORE, D]), P.scratch("dn_k", [S_CORE, D]),
                P.scratch("dn_v", [S_CORE, D])]
        labscr = P.scratch("dn_lab", [S_CORE, 32])
        gscr = P.scratch("dn_gate", [S_CORE, D])

        st = ExitStack()
        wT = P.sb("dp_wT", [128, 8 * 128], BF16, st)
        wTab = P.sb("dp_wTab", [128, 8 * 32], BF16, st)
        wTg = P.sb("dp_wTg", [128, 8 * 1024], BF16, st)
        cw = P.sb("dp_cw", [128, 96], F32, st)
        cbr = P.sb("dp_cbr", [1, 1056], BF16, st)
        fmb = P.sb("dp_fmb", [128, 4100], F32, st)
        cst = P.sb("dp_cst", [128, 4096], F32, st)
        tmS = P.sb("dp_tmS", [128, 128], F32, st)
        sgt = P.sb("dp_sgt", [128, 1024], F32, st)
        abt = P.sb("dp_abt", [128, 32], F32, st)
        lbt = P.sb("dp_lbt", [128, 32], F32, st)
        lb2 = P.sb("dp_lb2", [128, 32], F32, st)
        psP = P.ps("dp_psP", [128, 512], F32, st)
        psT = P.ps("dp_psT", [128, 512], F32, st)

        chain_op, chain_dma_syncN, chain_dma, emit = _serial_chain(P, G, "DP")

        for wi in range(3):
            for cc in range(8):
                chain_dma(lambda g, wi=wi, cc=cc: g.dma_start(
                    out=cw[:, wi * 32 + cc * 4: wi * 32 + (cc + 1) * 4],
                    in_=convs[wi][cc * 128:(cc + 1) * 128, :]))
        chain_dma(lambda g: g.dma_start(out=cbr[0:1, :], in_=cb_in.rearrange("(o a) -> o a", o=1)))
        for dc in range(8):
            chain_dma(lambda g, dc=dc: g.dma_start(
                out=wTab[:, dc * 32:(dc + 1) * 32],
                in_=ctrl_w_in.rearrange("a b -> b a")[dc * 128:(dc + 1) * 128, 0:32],
                allow_slow_non_contiguous=True))
        chain_op("vector", lambda v: v.memset(fmb[:, 0:4], 0.0))

        for wi in range(3):
            for oc in range(8):
                def ld_w(s, inc16, wi=wi, oc=oc):
                    for dc in range(8):
                        inc16(s.dma_start_transpose(
                            out=wT[:, dc * 128:(dc + 1) * 128],
                            in_=qkv_bf[wi * 1024 + oc * 128: wi * 1024 + (oc + 1) * 128,
                                       dc * 128:(dc + 1) * 128]))
                chain_dma_syncN(ld_w)
                for t8 in range(8):
                    def mm8(t, t8=t8):
                        last = None
                        for dc in range(8):
                            last = t.matmul(psP[:, 0:512], wT[:, dc * 128:(dc + 1) * 128],
                                            XHAT[:, dc * S_CORE + t8 * 512: dc * S_CORE + (t8 + 1) * 512],
                                            start=(dc == 0), stop=(dc == 7))
                        return last
                    chain_op("tensor", mm8)
                    chain_op("vector", lambda v, t8=t8: v.tensor_copy(
                        out=fmb[:, 4 + t8 * 512: 4 + (t8 + 1) * 512], in_=psP[0:128, 0:512]))
                cwo = wi * 32 + oc * 4
                chain_op("vector", lambda v, cwo=cwo: v.tensor_tensor(
                    out=cst[:, :], in0=fmb[:, 1:4097],
                    in1=cw[:, cwo:cwo + 1].broadcast_to([128, 4096]), op=ALU.mult))
                for j in range(1, 4):
                    chain_op("vector", lambda v, cwo=cwo, j=j: v.scalar_tensor_tensor(
                        out=cst[:, :], in0=fmb[:, 1 + j:1 + j + 4096],
                        scalar=cw[:, cwo + j:cwo + j + 1],
                        in1=cst[:, :], op0=ALU.mult, op1=ALU.add))
                chain_op("scalar", lambda sc: sc.activation(out=fmb[:, 4:4100], in_=cst[:, :], func=AF.Silu))
                for tb in range(NTB):
                    chain_op("tensor", lambda t, tb=tb: t.transpose(
                        psT[:, 0:128], fmb[:, 4 + tb * 128: 4 + (tb + 1) * 128], G["ident_f32"][:, :]))
                    chain_op("vector", lambda v: v.tensor_copy(out=tmS[:, :], in_=psT[0:128, 0:128]))
                    chain_dma(lambda g, wi=wi, oc=oc, tb=tb: g.dma_start(
                        out=tscr[wi][tb * 128:(tb + 1) * 128, oc * 128:(oc + 1) * 128], in_=tmS[:, :]))

        def ld_wg(s, inc16):
            for dc in range(8):
                for oc in range(8):
                    inc16(s.dma_start_transpose(
                        out=wTg[:, dc * 1024 + oc * 128: dc * 1024 + (oc + 1) * 128],
                        in_=ctrl_bf[32 + oc * 128: 32 + (oc + 1) * 128, dc * 128:(dc + 1) * 128]))
        chain_dma_syncN(ld_wg)
        for t3 in range(NTB):
            def mmab(t, t3=t3):
                for dc in range(8):
                    t.matmul(psP[:, 0:32], XHAT[:, dc * S_CORE + t3 * 128: dc * S_CORE + (t3 + 1) * 128],
                             wTab[:, dc * 32:(dc + 1) * 32], start=(dc == 0), stop=False)
                return t.matmul(psP[:, 0:32], G["ones_bf"][0:1, 0:128], cbr[0:1, 0:32],
                                start=False, stop=True)
            chain_op("tensor", mmab)
            chain_op("scalar", lambda sc: sc.activation(out=abt[:, :], in_=psP[0:128, 0:32], func=AF.Sigmoid))
            chain_op("vector", lambda v: v.tensor_scalar(out=lbt[:, 0:16], in0=abt[:, 0:16],
                                                         scalar1=0.98, scalar2=0.01, op0=ALU.mult, op1=ALU.add))
            chain_op("scalar", lambda sc: sc.activation(out=lb2[:, 0:16], in_=lbt[:, 0:16], func=AF.Ln))
            chain_op("vector", lambda v: v.tensor_copy(out=lb2[:, 16:32], in_=abt[:, 16:32]))
            chain_dma(lambda g, t3=t3: g.dma_start(out=labscr[t3 * 128:(t3 + 1) * 128, :], in_=lb2[:, :]))
            for half in range(2):
                def mmg(t, t3=t3, half=half):
                    for dc in range(8):
                        t.matmul(psP[:, 0:512], XHAT[:, dc * S_CORE + t3 * 128: dc * S_CORE + (t3 + 1) * 128],
                                 wTg[:, dc * 1024 + half * 512: dc * 1024 + (half + 1) * 512],
                                 start=(dc == 0), stop=False)
                    return t.matmul(psP[:, 0:512], G["ones_bf"][0:1, 0:128],
                                    cbr[0:1, 32 + half * 512: 32 + (half + 1) * 512], start=False, stop=True)
                chain_op("tensor", mmg)
                chain_op("scalar", lambda sc, half=half: sc.activation(
                    out=sgt[:, half * 512:(half + 1) * 512], in_=psP[0:128, 0:512], func=AF.Sigmoid))
            chain_dma(lambda g, t3=t3: g.dma_start(out=gscr[t3 * 128:(t3 + 1) * 128, :], in_=sgt[:, :]))

        emit()
        P.end_phase()
        st.close()
    return ph


def mk_delta_main():
    def ph(P, G):
        CTX = G["CTX"]
        dq_scr = P.scratch("dn_q", [S_CORE, D])
        dk_scr = P.scratch("dn_k", [S_CORE, D])
        dv_scr = P.scratch("dn_v", [S_CORE, D])
        lab_scr = P.scratch("dn_lab", [S_CORE, 32])
        yscr = P.scratch("mamba_y", [S_CORE, D])
        msl_in = P.inp("masksl", [128, 128])
        mge_in = P.inp("maskge", [128, 128])
        rpf_in = P.inp("repf128", [16, 2048])
        nw_in = P.inp("L1_attn_a_norm_w", [64])

        st = ExitStack()
        msl = P.sb("dm_msl", [128, 128], F32, st)
        mge = P.sb("dm_mge", [128, 128], F32, st)
        rpf = P.sb("dm_rpf", [16, 2048], F32, st)
        nwrow = P.sb("dm_nwrow", [1, 64], F32, st)
        nwb = P.sb("dm_nwb", [128, 64], F32, st)
        qtm = P.sb("dm_qtm", [128, 1024], F32, st)
        ktm = P.sb("dm_ktm", [128, 1024], F32, st)
        vtm = P.sb("dm_vtm", [128, 1024], F32, st)
        labtm = P.sb("dm_labtm", [128, 32], F32, st)
        laf = P.sb("dm_laf", [16, 128], F32, st)
        lcs = P.sb("dm_lcs", [16, 128], F32, st)
        zz = P.sb("dm_zz", [16, 128], F32, st)
        explf = P.sb("dm_explf", [16, 128], F32, st)
        dtmp16 = P.sb("dm_dtmp16", [16, 128], F32, st)
        gtailf = P.sb("dm_gtailf", [16, 128], F32, st)
        gcr = P.sb("dm_gcr", [1, 16], F32, st)
        cg = P.sb("dm_cg", [128, 96], F32, st)
        sscols = P.sb("dm_ss", [128, 32], F32, st)
        rcA = P.sb("dm_rcA", [128, 32], F32, st)
        rcB = P.sb("dm_rcB", [128, 32], F32, st)
        kqfm = P.sb("dm_kqfm", [64, 4096], F32, st)
        t2m = P.sb("dm_t2m", [128, 2048], F32, st)
        e1sh = P.sb("dm_e1sh", [128, 128], F32, st)
        e1sx = P.sb("dm_e1sx", [128, 128], F32, st)
        e2tx = P.sb("dm_e2tx", [128, 128], F32, st)
        nA = P.sb("dm_nA", [128, 128], F32, st)
        B0 = P.sb("dm_B0", [128, 128], F32, st)
        MbT = P.sb("dm_MbT", [128, 128], F32, st)
        BD = [P.sb(f"dm_BD{i}", [128, 256], F32, st) for i in range(2)]
        Wc = [P.sb(f"dm_W{i}", [128, 64], F32, st) for i in range(2)]
        tmpW = P.sb("dm_tmpW", [128, 64], F32, st)
        q0s = P.sb("dm_q0s", [128, 64], F32, st)
        Wg = P.sb("dm_Wg", [128, 64], F32, st)
        Sst = [P.sb(f"dm_S{i}", [64, 1024], F32, st) for i in range(2)]
        ytm = P.sb("dm_ytm", [128, 1024], F32, st)
        rms = P.sb("dm_rms", [128, 64], F32, st)
        psBig = P.ps("dm_psBig", [128, 2048], F32, st)
        psKKQ = P.ps("dm_psKKQ", [128, 512], F32, st)
        psM = P.ps("dm_psM", [128, 512], F32, st)
        psBD = P.ps("dm_psBD", [128, 512], F32, st)
        psW = P.ps("dm_psW", [128, 512], F32, st)

        chain_op, chain_dma_syncN, chain_dma, emit = _serial_chain(P, G, "DQ")

        chain_dma(lambda g: g.dma_start(out=msl[:, :], in_=msl_in[:, :]))
        chain_dma(lambda g: g.dma_start(out=mge[:, :], in_=mge_in[:, :]))
        chain_dma(lambda g: g.dma_start(out=rpf[:, :], in_=rpf_in[:, :]))
        chain_dma(lambda g: g.dma_start(out=nwrow[0:1, :], in_=nw_in.rearrange("(o a) -> o a", o=1)))
        chain_op("tensor", lambda t: t.matmul(psM[:, 0:64], G["ones_f32"][0:1, 0:128],
                                              nwrow[0:1, 0:64], start=True, stop=True))
        chain_op("vector", lambda v: v.tensor_copy(out=nwb[:, :], in_=psM[0:128, 0:64]))
        chain_op("vector", lambda v: v.memset(zz[:, :], 0.0))
        chain_op("vector", lambda v: v.memset(Sst[0][:, :], 0.0))

        for c in range(32):
            tok0 = c * 128
            sp, sq = c % 2, 1 - c % 2
            chain_dma(lambda g, tok0=tok0: g.dma_start(out=qtm[:, :], in_=dq_scr[tok0:tok0 + 128, :]))
            chain_dma(lambda g, tok0=tok0: g.dma_start(out=ktm[:, :], in_=dk_scr[tok0:tok0 + 128, :]))
            chain_dma(lambda g, tok0=tok0: g.dma_start(out=vtm[:, :], in_=dv_scr[tok0:tok0 + 128, :]))
            chain_dma(lambda g, tok0=tok0: g.dma_start(out=labtm[:, :], in_=lab_scr[tok0:tok0 + 128, :]))
            chain_op("tensor", lambda t: t.transpose(psM[0:16, 0:128], labtm[:, 0:16], G["ident_f32"][:, :]))
            chain_op("vector", lambda v: v.tensor_copy(out=laf[:, :], in_=psM[0:16, 0:128]))
            chain_op("vector", lambda v: v.tensor_tensor_scan(out=lcs[:, :], data0=laf[:, :], data1=zz[:, :],
                                                              initial=0.0, op0=ALU.add, op1=ALU.add))
            chain_op("scalar", lambda sc: sc.activation(out=explf[:, :], in_=lcs[:, :], func=AF.Exp))
            chain_op("vector", lambda v: v.tensor_tensor(out=dtmp16[:, :],
                                                         in0=lcs[:, 127:128].broadcast_to([16, 128]),
                                                         in1=lcs[:, :], op=ALU.subtract))
            chain_op("scalar", lambda sc: sc.activation(out=gtailf[:, :], in_=dtmp16[:, :], func=AF.Exp))
            chain_op("tensor", lambda t: t.transpose(psM[0:1, 128:144], explf[:, 127:128],
                                                     G["ident_f32"][0:16, 0:16]))
            chain_op("vector", lambda v: v.tensor_copy(out=gcr[0:1, :], in_=psM[0:1, 128:144]))
            chain_op("tensor", lambda t: t.transpose(psM[0:128, 128:144], explf[:, :],
                                                     G["ident_f32"][0:16, 0:16]))
            chain_op("vector", lambda v: v.tensor_copy(out=cg[:, 0:16], in_=psM[0:128, 128:144]))
            chain_op("tensor", lambda t: t.transpose(psM[0:128, 144:160], gtailf[:, :],
                                                     G["ident_f32"][0:16, 0:16]))
            chain_op("vector", lambda v: v.tensor_copy(out=cg[:, 16:32], in_=psM[0:128, 144:160]))
            chain_op("vector", lambda v: v.tensor_scalar(out=cg[:, 32:48], in0=labtm[:, 16:32],
                                                         scalar1=-1.0, scalar2=None, op0=ALU.mult))
            chain_op("vector", lambda v: v.tensor_scalar(out=cg[:, 48:64], in0=cg[:, 0:16],
                                                         scalar1=-1.0, scalar2=None, op0=ALU.mult))
            chain_op("tensor", lambda t: t.matmul(psM[:, 160:176], G["ones_f32"][0:1, 0:128],
                                                  gcr[0:1, 0:16], start=True, stop=True))
            chain_op("vector", lambda v: v.tensor_copy(out=cg[:, 64:80], in_=psM[0:128, 160:176]))
            def mmt2(t):
                last = None
                for j in range(4):
                    last = t.matmul(psBig[:, j * 512:(j + 1) * 512], lcs[:, :],
                                    rpf[:, j * 512:(j + 1) * 512], start=True, stop=True)
                return last
            chain_op("tensor", mmt2)
            chain_op("vector", lambda v: v.tensor_copy(out=t2m[:, :], in_=psBig[0:128, :]))
            def mmt1(t):
                last = None
                for h in range(16):
                    last = t.transpose(psBig[:, h * 128:(h + 1) * 128], t2m[:, h * 128:(h + 1) * 128],
                                       G["ident_f32"][:, :])
                return last
            chain_op("tensor", mmt1)
            chain_op("vector", lambda v: v.tensor_tensor(out=t2m[:, :], in0=t2m[:, :],
                                                         in1=psBig[0:128, :], op=ALU.subtract))
            def sqqk(sc):
                last = None
                for h in range(16):
                    sc.activation(out=rms[:, 0:64], in_=qtm[:, h * 64:(h + 1) * 64],
                                  func=AF.Square, accum_out=sscols[:, h:h + 1])
                    last = sc.activation(out=rms[:, 0:64], in_=ktm[:, h * 64:(h + 1) * 64],
                                         func=AF.Square, accum_out=sscols[:, 16 + h:17 + h])
                return last
            chain_op("scalar", sqqk)
            chain_op("scalar", lambda sc: sc.activation(out=rcA[:, :], in_=sscols[:, :], func=AF.Ln))
            chain_op("scalar", lambda sc: sc.activation(out=rcB[:, :], in_=rcA[:, :], func=AF.Exp, scale=0.5))
            chain_op("vector", lambda v: v.tensor_scalar(out=rcA[:, :], in0=rcB[:, :],
                                                         scalar1=1e-6, scalar2=None, op0=ALU.max))
            chain_op("vector", lambda v: v.reciprocal(out=rcB[:, :], in_=rcA[:, :]))
            def nrm(v):
                last = None
                for h in range(16):
                    v.tensor_tensor(out=qtm[:, h * 64:(h + 1) * 64], in0=qtm[:, h * 64:(h + 1) * 64],
                                    in1=rcB[:, h:h + 1].broadcast_to([128, 64]), op=ALU.mult)
                    last = v.tensor_tensor(out=ktm[:, h * 64:(h + 1) * 64], in0=ktm[:, h * 64:(h + 1) * 64],
                                           in1=rcB[:, 16 + h:17 + h].broadcast_to([128, 64]), op=ALU.mult)
                return last
            chain_op("vector", nrm)
            for h in range(16):
                chain_op("tensor", lambda t, h=h: (
                    t.transpose(psW[0:64, 128:256], ktm[:, h * 64:(h + 1) * 64], G["ident_f32"][:, :]),
                    t.transpose(psW[0:64, 256:384], qtm[:, h * 64:(h + 1) * 64], G["ident_f32"][:, :]))[-1])
                chain_op("vector", lambda v, h=h: v.tensor_copy(
                    out=kqfm[:, h * 256:(h + 1) * 256], in_=psW[0:64, 128:384]))
            for h in range(16):
                chain_op("vector", lambda v, h=h: v.tensor_tensor(
                    out=e1sh[:, :], in0=t2m[:, h * 128:(h + 1) * 128], in1=msl[:, :], op=ALU.add))
                chain_op("scalar", lambda sc: sc.activation(out=e1sx[:, :], in_=e1sh[:, :], func=AF.Exp))
                chain_op("vector", lambda v, h=h: v.tensor_tensor(
                    out=e1sh[:, :], in0=mge[:, :], in1=t2m[:, h * 128:(h + 1) * 128], op=ALU.subtract))
                chain_op("scalar", lambda sc: sc.activation(out=e2tx[:, :], in_=e1sh[:, :], func=AF.Exp))
                chain_op("tensor", lambda t, h=h: t.matmul(
                    psKKQ[:, 0:256], kqfm[:, h * 256: h * 256 + 128],
                    kqfm[:, h * 256: h * 256 + 256], start=True, stop=True))
                chain_op("vector", lambda v, h=h: v.scalar_tensor_tensor(
                    out=nA[:, :], in0=psKKQ[0:128, 0:128], scalar=cg[:, 32 + h:33 + h],
                    in1=e1sx[:, :], op0=ALU.mult, op1=ALU.mult))
                chain_op("vector", lambda v: v.tensor_tensor(
                    out=MbT[:, :], in0=psKKQ[0:128, 128:256], in1=e2tx[:, :], op=ALU.mult))
                chain_op("tensor", lambda t: t.transpose(psW[:, 0:128], nA[:, :], G["ident_f32"][:, :]))
                chain_op("vector", lambda v: v.tensor_copy(out=B0[:, :], in_=psW[0:128, 0:128]))
                def mmk0(t, h=h, sp=sp):
                    t.matmul(psM[:, 0:64], kqfm[:, h * 256: h * 256 + 128],
                             Sst[sp][:, h * 64:(h + 1) * 64], start=True, stop=True)
                    return t.matmul(psM[:, 64:128], kqfm[:, h * 256 + 128: h * 256 + 256],
                                    Sst[sp][:, h * 64:(h + 1) * 64], start=True, stop=True)
                chain_op("tensor", mmk0)
                chain_op("vector", lambda v: v.tensor_copy(out=q0s[:, :], in_=psM[0:128, 64:128]))
                chain_op("vector", lambda v, h=h: v.scalar_tensor_tensor(
                    out=tmpW[:, :], in0=psM[0:128, 0:64], scalar=cg[:, 48 + h:49 + h],
                    in1=vtm[:, h * 64:(h + 1) * 64], op0=ALU.mult, op1=ALU.add))
                chain_op("vector", lambda v, h=h: v.tensor_tensor(
                    out=Wc[0][:, :], in0=tmpW[:, :],
                    in1=labtm[:, 16 + h:17 + h].broadcast_to([128, 64]), op=ALU.mult))
                for kk in range(7):
                    if kk == 0:
                        Bap, Dap = B0[:, 0:128], nA[:, 0:128]
                    else:
                        Bap, Dap = BD[(kk - 1) % 2][:, 0:128], BD[(kk - 1) % 2][:, 128:256]
                    chain_op("tensor", lambda t, Bap=Bap, kk=kk: t.matmul(
                        psW[:, 0:64], Bap, Wc[kk % 2][:, :], start=True, stop=True))
                    chain_op("vector", lambda v, kk=kk: v.tensor_tensor(
                        out=Wc[(kk + 1) % 2][:, :], in0=Wc[kk % 2][:, :],
                        in1=psW[0:128, 0:64], op=ALU.add))
                    if kk < 6:
                        def sqr(t, Bap=Bap, Dap=Dap):
                            t.matmul(psBD[:, 0:128], Dap, Bap, start=True, stop=True)
                            return t.matmul(psBD[:, 128:256], Bap, Dap, start=True, stop=True)
                        chain_op("tensor", sqr)
                        chain_op("vector", lambda v, kk=kk: v.tensor_copy(
                            out=BD[kk % 2][:, :], in_=psBD[0:128, 0:256]))
                chain_op("vector", lambda v, h=h: v.tensor_tensor(
                    out=Wg[:, :], in0=Wc[1][:, :],
                    in1=cg[:, 16 + h:17 + h].broadcast_to([128, 64]), op=ALU.mult))
                chain_op("tensor", lambda t, h=h: t.matmul(
                    psM[0:64, 192:256], ktm[:, h * 64:(h + 1) * 64], Wg[:, :], start=True, stop=True))
                chain_op("vector", lambda v, h=h, sp=sp, sq=sq: v.scalar_tensor_tensor(
                    out=Sst[sq][:, h * 64:(h + 1) * 64], in0=Sst[sp][:, h * 64:(h + 1) * 64],
                    scalar=cg[0:64, 64 + h:65 + h], in1=psM[0:64, 192:256],
                    op0=ALU.mult, op1=ALU.add))
                chain_op("tensor", lambda t: t.matmul(
                    psM[:, 128:192], MbT[:, :], Wc[1][:, :], start=True, stop=True))
                chain_op("vector", lambda v, h=h: v.scalar_tensor_tensor(
                    out=ytm[:, h * 64:(h + 1) * 64], in0=q0s[:, :],
                    scalar=cg[:, h:h + 1], in1=psM[0:128, 128:192], op0=ALU.mult, op1=ALU.add))
                chain_op("scalar", lambda sc, h=h: sc.activation(
                    out=rms[:, 0:64], in_=ytm[:, h * 64:(h + 1) * 64],
                    func=AF.Square, accum_out=sscols[:, h:h + 1]))
            chain_op("vector", lambda v: v.tensor_scalar(out=rcA[:, 0:16], in0=sscols[:, 0:16],
                                                         scalar1=1.0 / 64, scalar2=1e-6,
                                                         op0=ALU.mult, op1=ALU.add))
            chain_op("scalar", lambda sc: sc.activation(out=rcB[:, 0:16], in_=rcA[:, 0:16], func=AF.Ln))
            chain_op("scalar", lambda sc: sc.activation(out=rcA[:, 16:32], in_=rcB[:, 0:16],
                                                        func=AF.Exp, scale=-0.5))
            def yfin(v):
                last = None
                for h in range(16):
                    last = v.scalar_tensor_tensor(out=ytm[:, h * 64:(h + 1) * 64],
                                                  in0=ytm[:, h * 64:(h + 1) * 64],
                                                  scalar=rcA[:, 16 + h:17 + h], in1=nwb[:, :],
                                                  op0=ALU.mult, op1=ALU.mult)
                return last
            chain_op("vector", yfin)
            chain_dma(lambda g, tok0=tok0: g.dma_start(out=yscr[tok0:tok0 + 128, :], in_=ytm[:, :]))

        for tb in range(NTB):
            chain_dma(lambda g, tb=tb: g.dma_start(out=CTX[:, tb * 1024:(tb + 1) * 1024],
                                                   in_=yscr[tb * 128:(tb + 1) * 128, :]))
        emit()
        P.end_phase()
        st.close()
    return ph


def l1_attn_a_phases():
    return [
        mk_prep([("L1_attn_a_qkv_w", [3 * D, D]), ("L1_attn_a_ctrl_w", [1056, D]),
                 ("L1_attn_a_out_w", [D, D])]),
        ph_ln,
        mk_delta_pre(),
        mk_delta_main(),
        mk_proj_residual("L1_attn_a_out_w", "L1_attn_a_out_b", gname="dn_gate"),
    ]



def l0_attn_a_phases():
    return [
        mk_prep([("L0_attn_a_Wq", [D, D]), ("L0_attn_a_Wk", [D, D]),
                 ("L0_attn_a_Wv", [D, D]), ("L0_attn_a_Wo", [D, D])]),
        ph_ln,
        mk_attn("L0_attn_a_Wq", "L0_attn_a_Wk", "L0_attn_a_Wv",
                "L0_attn_a_bq", "L0_attn_a_bk", "L0_attn_a_bv", qscale=0.125),
        mk_proj_residual("L0_attn_a_Wo", "L0_attn_a_bo"),
    ]


def l1_attn_b_phases():
    return [
        mk_prep([("L1_attn_b_Wq", [D, D]), ("L1_attn_b_Wk", [D, D]),
                 ("L1_attn_b_Wv", [D, D]), ("L1_attn_b_Wo", [D, D])]),
        ph_ln,
        mk_relbias_prep(),
        mk_attn("L1_attn_b_Wq", "L1_attn_b_Wk", "L1_attn_b_Wv",
                "L1_attn_b_bq", "L1_attn_b_bk", "L1_attn_b_bv", qscale=0.125, relbias=True),
        mk_proj_residual("L1_attn_b_Wo", "L1_attn_b_bo"),
    ]


def _make_onehot_rev():
    import math
    dd = np.arange(8192)
    rel = dd - 4095
    nb = 160
    buckets = (rel > 0).astype(np.int64) * nb
    rel_abs = np.abs(rel)
    max_exact = 80
    is_small = rel_abs < max_exact
    large = np.log(np.maximum(rel_abs, 1) / max_exact) / math.log(800 / 80) * (nb - max_exact)
    large = (max_exact + large).astype(np.int64)
    large = np.minimum(large, nb - 1)
    b = buckets + np.where(is_small, rel_abs, large)
    oh = np.zeros((320, 8192), dtype=np.float32)
    oh[b, np.arange(8192)] = 1.0
    return oh


def _run(phase_lists, inputs, n_cores=N_CORES, outs=None):
    phases = [ph_init]
    for pl in phase_lists:
        phases.extend(pl)
    phases.append(ph_copyout)
    P = build_program(phases)
    x = np.ascontiguousarray(inputs["L0_attn_a_inputs"], dtype=np.float32)
    ident = np.eye(128, dtype=np.float32)
    consts = {"ident": ident}
    if "onehot_norm" in P.inputs:
        consts["onehot_norm"] = _make_onehot_rev()
    if "rep16" in P.inputs:
        r = np.zeros((16, 1024), dtype=np.float32)
        for h in range(16):
            r[h, h * 64:(h + 1) * 64] = 1.0
        consts["rep16"] = r
        m = np.zeros((64, 1024), dtype=np.float32)
        jj = np.arange(64)[:, None]
        ii = np.arange(64)[None, :]
        blkmask = np.where(ii >= jj, 0.0, -1e30).astype(np.float32)
        for h in range(16):
            m[:, h * 64:(h + 1) * 64] = blkmask
        consts["maskrep"] = m
    if "repf128" in P.inputs:
        r = np.zeros((16, 2048), dtype=np.float32)
        for h in range(16):
            r[h, h * 128:(h + 1) * 128] = 1.0
        consts["repf128"] = r
        jj = np.arange(128)[:, None]
        ii = np.arange(128)[None, :]
        consts["masksl"] = np.where(ii < jj, 0.0, -1e30).astype(np.float32)
        consts["maskge"] = np.where(ii >= jj, 0.0, -1e30).astype(np.float32)
    in_maps = []
    for b in range(n_cores):
        m = {"x_in": x[b]}
        m.update(consts)
        for name in P.inputs:
            if name in m or name == "x_in":
                continue
            m[name] = np.ascontiguousarray(inputs[name], dtype=np.float32)
        in_maps.append(m)
    res = run_bass_kernel_spmd(P.nc, in_maps, core_ids=list(range(n_cores)))
    keys = outs or ["y"]
    return np.stack([r[keys[0]] for r in res.results], axis=0)


def kernel(**inputs):
    return _run([
        l0_attn_a_phases(),
        l0_ffn_phases(),
        l0_attn_b_phases(),
        l0_ffn_phases(),
        l1_attn_a_phases(),
        l1_ffn_phases(),
        l1_attn_b_phases(),
        l1_ffn_phases(),
    ], inputs)


# version 03 (final)
# speedup vs baseline: 1.0494x; 1.0494x over previous
import numpy as np
from contextlib import ExitStack

import concourse.bass as bass
import concourse.mybir as mybir
from concourse.bass_utils import run_bass_kernel_spmd

F32 = mybir.dt.float32
BF16 = mybir.dt.bfloat16
AF = mybir.ActivationFunctionType
ALU = mybir.AluOpType
AX = mybir.AxisListType

S_CORE = 4096
D = 1024
NTB = S_CORE // 128
N_CORES = 8


class Prog:
    def __init__(self):
        self.nc = bass.Bass("TRN2", target_bir_lowering=False)
        self.es = ExitStack()
        self.inputs = {}
        self.outputs = {}
        self.scratches = {}
        self.sems = {}
        self.semc = {}
        self.block = None
        self.last = {}
        self.pd_total = 0
        self.waits = []
        self.semname = {}

    def inp(self, name, shape, dtype=F32):
        if name not in self.inputs:
            self.inputs[name] = self.nc.dram_tensor(name, list(shape), dtype, kind="ExternalInput").ap()
        return self.inputs[name]

    def outp(self, name, shape, dtype=F32):
        if name not in self.outputs:
            self.outputs[name] = self.nc.dram_tensor(name, list(shape), dtype, kind="ExternalOutput").ap()
        return self.outputs[name]

    def scratch(self, name, shape, dtype=F32):
        if name not in self.scratches:
            self.scratches[name] = self.nc.dram_tensor(name, list(shape), dtype, kind="Internal").ap()
        return self.scratches[name]

    def _uniq(self, name):
        if not hasattr(self, "_names"):
            self._names = {}
        n = self._names.get(name, 0)
        self._names[name] = n + 1
        return name if n == 0 else f"{name}__{n}"

    def sb(self, name, shape, dtype, stack=None):
        return (stack or self.es).enter_context(self.nc.sbuf_tensor(self._uniq(name), list(shape), dtype))

    def ps(self, name, shape, dtype=F32, stack=None):
        return (stack or self.es).enter_context(self.nc.psum_tensor(self._uniq(name), list(shape), dtype))

    PERSISTENT = {"pd", "ST", "WPREP", "LDG", "FIN"}

    def _physname(self, name):
        if not hasattr(self, "log2phys"):
            self.log2phys = {}
            self.pool_free = []
            self.pool_next = 0
        if name in self.log2phys:
            return self.log2phys[name]
        if name in self.PERSISTENT:
            p = "P_" + name
        elif self.pool_free:
            p = self.pool_free.pop()
        else:
            p = f"pool{self.pool_next}"
            self.pool_next += 1
        if p not in self.sems:
            self.sems[p] = self.es.enter_context(self.nc.semaphore(p))
            self.semc[p] = 0
            self.semname[id(self.sems[p])] = p
        self.log2phys[name] = p
        return p

    def sem(self, name):
        return self.sems[self._physname(name)]

    def inc(self, eng, instr, name, k=1):
        p = self._physname(name)
        instr.then_inc(self.sems[p], k)
        self.semc[p] += k
        assert self.semc[p] < 2 ** 30
        self.last[eng] = (p, self.semc[p])
        return self.semc[p]

    def val(self, name):
        return self.semc[self._physname(name)]

    def end_phase(self):
        pd = self.sem("pd")
        for e in sorted(self.last):
            sname, sval = self.last[e]
            sec = getattr(self.block, e)

            def body(h, sn=sname, sv=sval):
                h.wait_ge(self.sems[sn], sv)
                h.nop().then_inc(pd, 1)

            sec(body)
            self.pd_total += 1
            self.semc[self._physname("pd")] = self.pd_total
        self.last = {}
        for name, p in list(getattr(self, "log2phys", {}).items()):
            if name not in self.PERSISTENT:
                del self.log2phys[name]
                self.pool_free.append(p)

    def pw(self, eng):
        if self.pd_total > 0:
            eng.wait_ge(self.sem("pd"), self.pd_total)

    def check_waits(self):
        bad = []
        for sname, val, loc in self.waits:
            if val > self.semc.get(sname, 0):
                bad.append((sname, val, self.semc.get(sname, 0), loc))
        if bad:
            for b in bad[:20]:
                print("IMPOSSIBLE WAIT:", b)
            raise AssertionError(f"{len(bad)} impossible semaphore waits")


def build_program(phase_fns):
    P = Prog()
    G = {}

    P.inp("x_in", [S_CORE, D])
    P.outp("y", [S_CORE, D])
    G["x_state"] = [P.scratch("x_state0", [S_CORE, D]), P.scratch("x_state1", [S_CORE, D])]
    G["par"] = 0

    G["XHAT"] = P.sb("XHAT", [128, 8 * S_CORE], BF16)
    G["CTX"] = P.sb("CTX", [128, 8 * S_CORE], BF16)
    G["ident_bf"] = P.sb("ident_bf", [128, 128], BF16)
    G["ident_f32"] = P.sb("ident_f32", [128, 128], F32)
    G["ones_bf"] = P.sb("ones_bf", [1, 512], BF16)
    G["ones_f32"] = P.sb("ones_f32", [1, 512], F32)
    P.inp("ident", [128, 128])

    raw_block = P.es.enter_context(P.nc.Block())

    import traceback

    class EngProxy:
        def __init__(self, h):
            self._h = h

        def wait_ge(self, sem, val, **kw):
            nm = P.semname.get(id(sem))
            if nm is not None:
                tb = traceback.extract_stack(limit=3)[0]
                P.waits.append((nm, val, f"{tb.name}:{tb.lineno}"))
            return self._h.wait_ge(sem, val, **kw)

        def __getattr__(self, k):
            return getattr(self._h, k)

    class BlockProxy:
        def __getattr__(self, eng):
            sec = getattr(raw_block, eng)

            def reg(f):
                return sec(lambda h: f(EngProxy(h)))
            return reg

    G["block"] = BlockProxy()
    P.block = G["block"]

    for f in phase_fns:
        f(P, G)

    P.check_waits()
    P.es.close()
    return P



def ph_init(P, G):
    block = G["block"]

    @block.sync
    def _(s):
        P.pw(s)
        i1 = s.dma_start(out=G["x_state"][0][:, :], in_=P.inputs["x_in"][:, :])
        P.inc("sync", i1, "ST", 16)

    @block.gpsimd
    def _(g):
        P.pw(g)
        i1 = g.dma_start(out=G["ident_bf"][:], in_=P.inputs["ident"][:, :])
        P.inc("gpsimd", i1, "LDG", 16)
        i2 = g.dma_start(out=G["ident_f32"][:], in_=P.inputs["ident"][:, :])
        P.inc("gpsimd", i2, "LDG", 16)

    @block.vector
    def _(v):
        P.pw(v)
        v.memset(G["ones_bf"][:], 1.0)
        i = v.memset(G["ones_f32"][:], 1.0)
        P.inc("vector", i, "FIN", 1)

    P.end_phase()


def mk_prep(names):
    def ph(P, G):
        block = G["block"]

        @block.gpsimd
        def _(g):
            P.pw(g)
            for name, shape in names:
                src = P.inp(name, shape)
                flat = int(np.prod(shape))
                dst = P.scratch(name + "__bf", [flat], BF16)
                if len(shape) == 1:
                    sa = src
                else:
                    axes = " ".join(f"a{i}" for i in range(len(shape)))
                    sa = src.rearrange(f"{axes} -> ({axes})")
                ins = g.dma_start(out=dst[:], in_=sa)
                P.inc("gpsimd", ins, "WPREP", 16)

        P.end_phase()
    return ph


def bfview(P, name, ncols):
    return P.scratches[name + "__bf"].rearrange("(a b) -> a b", b=ncols)


def ph_ln(P, G):
    block = G["block"]
    xsrc = G["x_state"][G["par"]]
    XHAT = G["XHAT"]

    st = ExitStack()
    xt = [P.sb(f"ln_xt{i}", [128, D], F32, st) for i in range(2)]
    stage = P.sb("ln_stage", [128, D], F32, st)
    vtmp = P.sb("ln_vtmp", [128, D], F32, st)
    xn = [P.sb(f"ln_xn{i}", [128, D], BF16, st) for i in range(2)]
    cols = P.sb("ln_cols", [128, 8], F32, st)
    psT = [P.ps(f"ln_psT{i}", [128, 128], BF16, st) for i in range(2)]

    b_lda = P.val("LNLA"); b_ldb = P.val("LNLB")
    b_s1 = P.val("LNS"); b_v1 = P.val("LNV1"); b_v2 = P.val("LNV2")
    b_vg = P.val("LNVG"); b_r1 = P.val("LNR1"); b_r = P.val("LNR")
    b_s2 = P.val("LNN"); b_mm = P.val("LMM"); b_ev = P.val("LEV")

    @block.sync
    def _(s):
        P.pw(s)
        s.wait_ge(P.sem("ST"), P.val("ST"))
        s.wait_ge(P.sem("STG"), P.val("STG"))
        for i in range(NTB):
            if i >= 2:
                s.wait_ge(P.sem("LNN"), b_s2 + i - 1)
            ins = s.dma_start(out=xt[i % 2][:], in_=xsrc[i * 128:(i + 1) * 128, :])
            P.inc("sync", ins, "LNLA" if i % 2 == 0 else "LNLB", 16)

    @block.scalar
    def _(sc):
        P.pw(sc)
        for i in range(NTB):
            if i % 2 == 0:
                sc.wait_ge(P.sem("LNLA"), b_lda + (i // 2 + 1) * 16)
            else:
                sc.wait_ge(P.sem("LNLB"), b_ldb + (i // 2 + 1) * 16)
            x = xt[i % 2]
            sc.activation(out=stage[:], in_=x[:], func=AF.Copy, accum_out=cols[:, 0:1])
            ins = sc.activation(out=stage[:], in_=x[:], func=AF.Square, accum_out=cols[:, 1:2])
            P.inc("scalar", ins, "LNS", 1)
            sc.wait_ge(P.sem("LNVG"), b_vg + i + 1)
            if i >= 1:
                sc.wait_ge(P.sem("LNN"), b_s2 + i)
            ins = sc.activation(out=cols[:, 5:6], in_=cols[:, 3:4], func=AF.Ln)
            P.inc("scalar", ins, "LNR1", 1)
            sc.wait_ge(P.sem("LNR1"), b_r1 + i + 1)
            ins = sc.activation(out=cols[:, 4:5], in_=cols[:, 5:6], func=AF.Exp, scale=-0.5)
            P.inc("scalar", ins, "LNR", 1)

    @block.vector
    def _(v):
        P.pw(v)
        for i in range(NTB):
            v.wait_ge(P.sem("LNS"), b_s1 + i + 1)
            v.tensor_scalar(out=cols[:, 2:3], in0=cols[:, 0:1], scalar1=-1.0 / D,
                            scalar2=None, op0=ALU.mult)
            ins = v.tensor_scalar(out=cols[:, 7:8], in0=cols[:, 1:2], scalar1=1.0 / D,
                                  scalar2=1e-5, op0=ALU.mult, op1=ALU.add)
            P.inc("vector", ins, "LNV1", 1)
            v.wait_ge(P.sem("LNV1"), b_v1 + i + 1)
            ins = v.tensor_tensor(out=cols[:, 6:7], in0=cols[:, 2:3], in1=cols[:, 2:3], op=ALU.mult)
            P.inc("vector", ins, "LNV2", 1)
            v.wait_ge(P.sem("LNV2"), b_v2 + i + 1)
            ins = v.tensor_tensor(out=cols[:, 3:4], in0=cols[:, 7:8], in1=cols[:, 6:7], op=ALU.subtract)
            P.inc("vector", ins, "LNVG", 1)
            v.wait_ge(P.sem("LNR"), b_r + i + 1)
            v.tensor_tensor(out=vtmp[:], in0=xt[i % 2][:],
                            in1=cols[:, 2:3].broadcast_to([128, D]), op=ALU.add)
            ins = v.tensor_tensor(out=xn[i % 2][:], in0=vtmp[:],
                                  in1=cols[:, 4:5].broadcast_to([128, D]), op=ALU.mult)
            P.inc("vector", ins, "LNN", 1)
            for c in range(8):
                j = i * 8 + c
                v.wait_ge(P.sem("LMM"), b_mm + j + 1)
                ins = v.tensor_copy(
                    out=XHAT[:, c * S_CORE + i * 128: c * S_CORE + (i + 1) * 128],
                    in_=psT[j % 2][:])
                P.inc("vector", ins, "LEV", 1)

    @block.tensor
    def _(t):
        P.pw(t)
        t.wait_ge(P.sem("LDG"), P.val("LDG"))
        for i in range(NTB):
            t.wait_ge(P.sem("LNN"), b_s2 + i + 1)
            for c in range(8):
                j = i * 8 + c
                if j >= 2:
                    t.wait_ge(P.sem("LEV"), b_ev + j - 1)
                ins = t.transpose(psT[j % 2][:], xn[i % 2][:, c * 128:(c + 1) * 128], G["ident_bf"][:])
                P.inc("tensor", ins, "LMM", 1)

    P.end_phase()
    st.close()


def mk_attn(wq, wk, wv, bq, bk, bv, qscale, relbias=False):
    def ph(P, G):
        import os
        rbmode = int(os.environ.get("RBMODE", "2"))
        block = G["block"]
        XHAT = G["XHAT"]; CTX = G["CTX"]

        st = ExitStack()
        nwb = 1 if relbias else 2
        wqs = [P.sb(f"at_wq{i}", [128, 1024], BF16, st) for i in range(nwb)]
        wks = [P.sb(f"at_wk{i}", [128, 1024], BF16, st) for i in range(nwb)]
        wvs = [P.sb(f"at_wv{i}", [128, 1024], BF16, st) for i in range(nwb)]
        Q2 = P.sb("at_Q2", [128, S_CORE], BF16, st)
        K2 = P.sb("at_K2", [128, S_CORE], BF16, st)
        V2 = P.sb("at_V2", [128, NTB * 130], BF16, st)
        PT = [P.sb(f"at_PT{i}", [128, 512], BF16, st) for i in range(2)]
        OO = [P.sb(f"at_OO{i}", [65, 512], F32, st) for i in range(2)]
        lrec = P.sb("at_lrec", [128, 2], F32, st)
        bqr = P.sb("at_bqr", [1, D], BF16, st)
        bkr = P.sb("at_bkr", [1, D], BF16, st)
        bvr = P.sb("at_bvr", [1, D], BF16, st)
        psP = [P.ps(f"at_psP{i}", [128, 512], F32, st) for i in range(2)]
        psS = [P.ps(f"at_psS{i}", [128, 512], F32, st) for i in range(2)]
        psO = [P.ps(f"at_psO{i}", [65, 512], F32, st) for i in range(2)]
        psT = [P.ps(f"at_psT{i}", [128, 65], F32, st) for i in range(2)]
        if relbias:
            Bh = P.sb("at_Bh", [128, 8064], BF16, st)
            Gh = P.sb("at_Gh", [128, S_CORE], BF16, st)
            X2 = [P.sb(f"at_X2{i}", [128, 512], BF16, st) for i in range(2)]
            growrow = P.sb("at_growrow", [1, S_CORE], BF16, st)
            biasnorm_t = P.scratches["biasnorm"]
            grows_d = P.scratch("grows", [16, S_CORE])
            b_bl = P.val("BHL"); b_gh = P.val("GHM"); b_ghe = P.val("GHE"); b_vx = P.val("VX2")
            b_gr2 = P.val("GRL")

        wq_bf = bfview(P, wq, D)
        wk_bf = bfview(P, wk, D)
        wv_bf = bfview(P, wv, D)

        NHP, NQT, NKB = 8, 8, 32
        PROJ_G = 48

        b_wla = P.val("WLA"); b_wlb = P.val("WLB"); b_abr = P.val("ABR")
        b_pj = P.val("MMP"); b_eq = P.val("EVQ"); b_ev = P.val("EVV")
        b_smm = P.val("SMM"); b_exp = P.val("EXP"); b_av = P.val("AVD")
        b_oev = P.val("OEV"); b_tr = P.val("TRD"); b_cw = P.val("CTXW")
        b_bs = P.val("BSC"); b_rcp = P.val("RCP")

        def proj_war_wait(t, gg):
            g2 = gg - 2
            if g2 >= 0:
                lg = g2 % PROJ_G
                if lg < 16:
                    t.wait_ge(P.sem("EVQ"), b_eq + (g2 // PROJ_G) * 16 + lg + 1)
                else:
                    t.wait_ge(P.sem("EVV"), b_ev + (g2 // PROJ_G) * 32 + (lg - 16) + 1)

        @block.sync
        def _(s):
            P.pw(s)
            s.wait_ge(P.sem("WPREP"), P.val("WPREP"))
            for hp in range(NHP):
                if hp >= nwb:
                    s.wait_ge(P.sem("MMP"), b_pj + (hp - nwb + 1) * PROJ_G)
                wl = "WLA" if hp % 2 == 0 or nwb == 1 else "WLB"
                for dc in range(8):
                    i1 = s.dma_start_transpose(out=wqs[hp % nwb][:, dc * 128:(dc + 1) * 128],
                                               in_=wq_bf[hp * 128:(hp + 1) * 128, dc * 128:(dc + 1) * 128])
                    P.inc("sync", i1, wl, 16)
                    i2 = s.dma_start_transpose(out=wks[hp % nwb][:, dc * 128:(dc + 1) * 128],
                                               in_=wk_bf[hp * 128:(hp + 1) * 128, dc * 128:(dc + 1) * 128])
                    P.inc("sync", i2, wl, 16)
                    i3 = s.dma_start_transpose(out=wvs[hp % nwb][:, dc * 128:(dc + 1) * 128],
                                               in_=wv_bf[hp * 128:(hp + 1) * 128, dc * 128:(dc + 1) * 128])
                    P.inc("sync", i3, wl, 16)

        @block.gpsimd
        def _(g):
            P.pw(g)
            i1 = g.dma_start(out=bqr[0:1, :], in_=P.inp(bq, [D]).rearrange("(o a) -> o a", o=1))
            P.inc("gpsimd", i1, "ABR", 16)
            i2 = g.dma_start(out=bkr[0:1, :], in_=P.inp(bk, [D]).rearrange("(o a) -> o a", o=1))
            P.inc("gpsimd", i2, "ABR", 16)
            i3 = g.dma_start(out=bvr[0:1, :], in_=P.inp(bv, [D]).rearrange("(o a) -> o a", o=1))
            P.inc("gpsimd", i3, "ABR", 16)
            if relbias:
                for h in range(16):
                    if h >= 1:
                        g.wait_ge(P.sem("VX2"), b_vx + h * 8 * 32 * 2)
                        g.wait_ge(P.sem("GHM"), b_gh + h * 8)
                    ins = g.dma_start(out=growrow[0:1, :], in_=grows_d[h:h + 1, :])
                    P.inc("gpsimd", ins, "GRL", 16)
                    for part in range(2):
                        ap_in = bass.AP(biasnorm_t.tensor, h * 8192 + part * 4032,
                                        [[1, 128], [1, 4032]])
                        ins = g.dma_start(out=Bh[:, part * 4032:(part + 1) * 4032], in_=ap_in)
                        P.inc("gpsimd", ins, "BHL", 16)

        @block.vector
        def _(v):
            P.pw(v)
            v3 = V2.rearrange("p (b c) -> p b c", c=130)
            v.memset(v3[:, :, 64:65], 1.0)
            ins = v.memset(v3[:, :, 129:130], 1.0)
            P.inc("vector", ins, "BSC", 1)

        @block.tensor
        def _(t):
            P.pw(t)
            t.wait_ge(P.sem("ABR"), b_abr + 48)
            t.wait_ge(P.sem("BSC"), b_bs + 1)
            for hp in range(NHP):
                if nwb == 1:
                    t.wait_ge(P.sem("WLA"), b_wla + (hp + 1) * 24 * 16)
                else:
                    wl = "WLA" if hp % 2 == 0 else "WLB"
                    bwl = b_wla if hp % 2 == 0 else b_wlb
                    t.wait_ge(P.sem(wl), bwl + (hp // 2 + 1) * 24 * 16)
                for wi, (wsl, brow, dst) in enumerate(((wqs[hp % nwb], bqr, Q2), (wks[hp % nwb], bkr, K2))):
                    for tt in range(NQT):
                        gg = hp * PROJ_G + wi * 8 + tt
                        proj_war_wait(t, gg)
                        for dc in range(8):
                            t.matmul(psP[gg % 2][:], wsl[:, dc * 128:(dc + 1) * 128],
                                     XHAT[:, dc * S_CORE + tt * 512: dc * S_CORE + (tt + 1) * 512],
                                     start=(dc == 0), stop=False)
                        ins = t.matmul(psP[gg % 2][:], brow[0:1, hp * 128:(hp + 1) * 128],
                                       G["ones_bf"][0:1, 0:512], start=False, stop=True)
                        P.inc("tensor", ins, "MMP", 1)
                for tb in range(NTB):
                    gg = hp * PROJ_G + 16 + tb
                    proj_war_wait(t, gg)
                    for dc in range(8):
                        t.matmul(psP[gg % 2][:, 0:128],
                                 XHAT[:, dc * S_CORE + tb * 128: dc * S_CORE + (tb + 1) * 128],
                                 wvs[hp % nwb][:, dc * 128:(dc + 1) * 128],
                                 start=(dc == 0), stop=False)
                    ins = t.matmul(psP[gg % 2][:, 0:128], G["ones_bf"][0:1, 0:128],
                                   bvr[0:1, hp * 128:(hp + 1) * 128], start=False, stop=True)
                    P.inc("tensor", ins, "MMP", 1)
                t.wait_ge(P.sem("EVQ"), b_eq + (hp + 1) * 16)
                t.wait_ge(P.sem("EVV"), b_ev + (hp + 1) * 32)
                for hh in range(2):
                    h = 2 * hp + hh
                    r0 = 64 * hh
                    if relbias:
                        for gq in range(8):
                            if h * 8 + gq >= 2:
                                t.wait_ge(P.sem("GHE"), b_ghe + h * 8 + gq - 1)
                            if gq == 0:
                                t.wait_ge(P.sem("GRL"), b_gr2 + (h + 1) * 16)
                            ins = t.matmul(psS[gq % 2][:],
                                           G["ones_bf"][0:1, 0:128],
                                           growrow[0:1, gq * 512:(gq + 1) * 512],
                                           start=True, stop=True)
                            P.inc("tensor", ins, "GHM", 1)
                        t.wait_ge(P.sem("GHE"), b_ghe + (h + 1) * 8)
                        t.wait_ge(P.sem("BHL"), b_bl + (h + 1) * 32)
                    for qt in range(NQT):
                        gi = h * NQT + qt
                        if gi >= 2:
                            t.wait_ge(P.sem("OEV"), b_oev + gi - 1)
                        for kb in range(NKB):
                            ki = gi * NKB + kb
                            if not relbias:
                                if kb >= 2:
                                    t.wait_ge(P.sem("EXP"), b_exp + ki - 1)
                            else:
                                if kb >= 2:
                                    t.wait_ge(P.sem("VX2"), b_vx + 2 * ki - 2)
                            ins = t.matmul(psS[ki % 2][:],
                                           K2[r0:r0 + 64, kb * 128:(kb + 1) * 128],
                                           Q2[r0:r0 + 64, qt * 512:(qt + 1) * 512],
                                           start=True, stop=True)
                            P.inc("tensor", ins, "SMM", 1)
                            if kb >= 1:
                                t.wait_ge(P.sem("EXP"), b_exp + ki)
                                ins = t.matmul(psO[gi % 2][:],
                                               V2[:, (kb - 1) * 130 + 65 * hh: (kb - 1) * 130 + 65 * hh + 65],
                                               PT[(ki - 1) % 2][:],
                                               start=(kb - 1 == 0), stop=False)
                                P.inc("tensor", ins, "AVD", 1)
                        ki_last = gi * NKB + NKB - 1
                        t.wait_ge(P.sem("EXP"), b_exp + ki_last + 1)
                        ins = t.matmul(psO[gi % 2][:],
                                       V2[:, (NKB - 1) * 130 + 65 * hh: (NKB - 1) * 130 + 65 * hh + 65],
                                       PT[ki_last % 2][:],
                                       start=False, stop=True)
                        P.inc("tensor", ins, "AVD", 1)
                        t.wait_ge(P.sem("OEV"), b_oev + gi + 1)
                        for qb in range(4):
                            tj = gi * 4 + qb
                            if tj >= 2:
                                t.wait_ge(P.sem("CTXW"), b_cw + tj - 1)
                            ins = t.transpose(psT[tj % 2][:], OO[gi % 2][:, qb * 128:(qb + 1) * 128],
                                              G["ident_f32"][0:65, 0:65])
                            P.inc("tensor", ins, "TRD", 1)

        @block.scalar
        def _(sc):
            P.pw(sc)
            for hp in range(NHP):
                for wi, (dst, scale) in enumerate(((Q2, qscale), (K2, 1.0))):
                    for tt in range(NQT):
                        gg = hp * PROJ_G + wi * 8 + tt
                        sc.wait_ge(P.sem("MMP"), b_pj + gg + 1)
                        ins = sc.activation(out=dst[:, tt * 512:(tt + 1) * 512], in_=psP[gg % 2][:],
                                            func=AF.Copy, scale=float(scale))
                        P.inc("scalar", ins, "EVQ", 1)
                for hh in range(2):
                    h = 2 * hp + hh
                    for qt in range(NQT):
                        gi = h * NQT + qt
                        for kb in range(NKB):
                            ki = gi * NKB + kb
                            if kb >= 2:
                                sc.wait_ge(P.sem("AVD"), b_av + ki - 1)
                            if not relbias:
                                sc.wait_ge(P.sem("SMM"), b_smm + ki + 1)
                                ins = sc.activation(out=PT[ki % 2][:], in_=psS[ki % 2][:], func=AF.Exp)
                            else:
                                sc.wait_ge(P.sem("VX2"), b_vx + 2 * ki + 2)
                                ins = sc.activation(out=PT[ki % 2][:], in_=X2[ki % 2][:], func=AF.Exp)
                            P.inc("scalar", ins, "EXP", 1)
                        sc.wait_ge(P.sem("AVD"), b_av + (gi + 1) * NKB)
                        if gi >= 2:
                            sc.wait_ge(P.sem("TRD"), b_tr + (gi - 1) * 4)
                        ins = sc.copy(out=OO[gi % 2][:, :], in_=psO[gi % 2][:, :])
                        P.inc("scalar", ins, "OEV", 1)

        @block.vector
        def _(v):
            for hp in range(NHP):
                for tb in range(NTB):
                    gg = hp * PROJ_G + 16 + tb
                    v.wait_ge(P.sem("MMP"), b_pj + gg + 1)
                    ins = v.tensor_copy(
                        out=V2[:, tb * 130: tb * 130 + 130].rearrange("p (g c) -> p g c", c=65)[:, :, 0:64],
                        in_=psP[gg % 2].rearrange("p (g c) -> p g c", c=64)[:, 0:2, :])
                    P.inc("vector", ins, "EVV", 1)
                for hh in range(2):
                    h = 2 * hp + hh
                    if relbias:
                        for gq in range(8):
                            v.wait_ge(P.sem("GHM"), b_gh + h * 8 + gq + 1)
                            ins = v.tensor_copy(out=Gh[:, gq * 512:(gq + 1) * 512], in_=psS[gq % 2][:])
                            P.inc("vector", ins, "GHE", 1)
                    for qt in range(NQT):
                        gi = h * NQT + qt
                        if relbias:
                            for kb in range(NKB):
                                ki = gi * NKB + kb
                                v.wait_ge(P.sem("SMM"), b_smm + ki + 1)
                                if kb >= 2:
                                    v.wait_ge(P.sem("EXP"), b_exp + ki - 1)
                                off = 128 * (kb - 4 * qt + 31) + 127
                                if rbmode >= 2:
                                    bap = bass.AP(Bh.tensor if hasattr(Bh, "tensor") else Bh,
                                                  off, [[8064, 128], [-1, 512]])
                                else:
                                    bap = Bh[:, 0:512]
                                ins = v.tensor_tensor(out=X2[ki % 2][:],
                                                      in0=Gh[:, qt * 512:(qt + 1) * 512],
                                                      in1=bap, op=ALU.mult)
                                P.inc("vector", ins, "VX2", 1)
                                v.wait_ge(P.sem("VX2"), P.val("VX2"))
                                ins = v.tensor_tensor(out=X2[ki % 2][:], in0=psS[ki % 2][:],
                                                      in1=X2[ki % 2][:], op=ALU.add)
                                P.inc("vector", ins, "VX2", 1)
                        for qb in range(4):
                            tj = gi * 4 + qb
                            tb = qt * 4 + qb
                            v.wait_ge(P.sem("TRD"), b_tr + tj + 1)
                            ins = v.reciprocal(out=lrec[:, 0:1], in_=psT[tj % 2][:, 64:65])
                            P.inc("vector", ins, "RCP", 1)
                            v.wait_ge(P.sem("RCP"), b_rcp + tj + 1)
                            ins = v.tensor_tensor(
                                out=CTX[:, tb * 1024 + h * 64: tb * 1024 + h * 64 + 64],
                                in0=psT[tj % 2][:, 0:64],
                                in1=lrec[:, 0:1].broadcast_to([128, 64]), op=ALU.mult)
                            P.inc("vector", ins, "CTXW", 1)

        P.end_phase()
        st.close()
    return ph


def mk_proj_residual(wname, bname, gname=None):
    def ph(P, G):
        block = G["block"]
        CTX = G["CTX"]
        par = G["par"]
        xsrc = G["x_state"][par]
        xdst = G["x_state"][1 - par]
        G["par"] = 1 - par

        st = ExitStack()
        wos = P.sb("pr_wos", [128, 8 * 1024], BF16, st)
        bor = P.sb("pr_bor", [1, D], BF16, st)
        cfm = P.sb("pr_cfm", [128, 8 * 128], BF16, st)
        xold = [P.sb(f"pr_xold{i}", [128, D], F32, st) for i in range(2)]
        xnew = [P.sb(f"pr_xnew{i}", [128, D], F32, st) for i in range(2)]
        if gname is not None:
            gtile = [P.sb(f"pr_gt{i}", [128, D], F32, st) for i in range(2)]
            gstage = P.sb("pr_gstage", [128, 512], F32, st)
            gsrc = P.scratch(gname, [S_CORE, D])
        psT = [P.ps(f"pr_psT{i}", [128, 128], BF16, st) for i in range(2)]
        psU = [P.ps(f"pr_psU{i}", [128, 512], F32, st) for i in range(4)]

        w_bf = bfview(P, wname, D)

        b_wl = P.val("PWL"); b_la = P.val("PLA"); b_lb = P.val("PLB")
        b_tr = P.val("PTR"); b_te = P.val("PTE")
        b_mm = P.val("PMM"); b_ad = P.val("PAD"); b_sg = P.val("STG"); b_bo = P.val("PBOR")

        npt = 16 if gname is None else 32

        @block.sync
        def _(s):
            P.pw(s)
            s.wait_ge(P.sem("WPREP"), P.val("WPREP"))
            for dc in range(8):
                ins = s.dma_start_transpose(out=wos[:, dc * 1024:(dc + 1) * 1024],
                                            in_=w_bf[0:1024, dc * 128:(dc + 1) * 128])
                P.inc("sync", ins, "PWL", 16)
            for tb in range(NTB):
                if tb >= 2:
                    s.wait_ge(P.sem("PAD"), b_ad + (tb - 1) * 2)
                ins = s.dma_start(out=xold[tb % 2][:], in_=xsrc[tb * 128:(tb + 1) * 128, :])
                P.inc("sync", ins, "PLA" if tb % 2 == 0 else "PLB", 16)
                if gname is not None:
                    ins = s.dma_start(out=gtile[tb % 2][:], in_=gsrc[tb * 128:(tb + 1) * 128, :])
                    P.inc("sync", ins, "PLA" if tb % 2 == 0 else "PLB", 16)

        @block.gpsimd
        def _(g):
            P.pw(g)
            if bname is not None:
                ins = g.dma_start(out=bor[0:1, :], in_=P.inp(bname, [D]).rearrange("(o a) -> o a", o=1))
                P.inc("gpsimd", ins, "PBOR", 16)
            for tb in range(NTB):
                g.wait_ge(P.sem("PAD"), b_ad + tb * 2 + 2)
                ins = g.dma_start(out=xdst[tb * 128:(tb + 1) * 128, :], in_=xnew[tb % 2][:])
                P.inc("gpsimd", ins, "STG", 16)

        @block.tensor
        def _(t):
            P.pw(t)
            t.wait_ge(P.sem("PWL"), b_wl + 8 * 16)
            if bname is not None:
                t.wait_ge(P.sem("PBOR"), b_bo + 16)
            for tb in range(NTB):
                for dc in range(8):
                    j = tb * 8 + dc
                    if j >= 2:
                        j2 = j - 2
                        t.wait_ge(P.sem("PTE"), b_te + (j2 // 8) * 9 + (j2 % 8) + 1)
                    ins = t.transpose(psT[j % 2][:], CTX[:, tb * 1024 + dc * 128: tb * 1024 + (dc + 1) * 128],
                                      G["ident_bf"][:])
                    P.inc("tensor", ins, "PTR", 1)
                for half in range(2):
                    u = tb * 2 + half
                    if u >= 4:
                        t.wait_ge(P.sem("PAD"), b_ad + u - 3)
                    for dc in range(8):
                        t.wait_ge(P.sem("PTE"), b_te + tb * 9 + dc + 1)
                        ins = t.matmul(psU[u % 4][:], cfm[:, dc * 128:(dc + 1) * 128],
                                       wos[:, dc * 1024 + half * 512: dc * 1024 + (half + 1) * 512],
                                       start=(dc == 0), stop=(dc == 7 and bname is None))
                    if bname is not None:
                        ins = t.matmul(psU[u % 4][:], G["ones_bf"][0:1, 0:128],
                                       bor[0:1, half * 512:(half + 1) * 512], start=False, stop=True)
                    P.inc("tensor", ins, "PMM", 1)

        @block.vector
        def _(v):
            P.pw(v)
            for tb in range(NTB):
                for dc in range(8):
                    j = tb * 8 + dc
                    v.wait_ge(P.sem("PTR"), b_tr + j + 1)
                    if tb >= 1:
                        v.wait_ge(P.sem("PMM"), b_mm + tb * 2)
                    ins = v.tensor_copy(out=cfm[:, dc * 128:(dc + 1) * 128], in_=psT[j % 2][:])
                    P.inc("vector", ins, "PTE", 1)
                ins = v.tensor_copy(out=cfm[:, 0:1], in_=cfm[:, 0:1])
                P.inc("vector", ins, "PTE", 1)
                for half in range(2):
                    u = tb * 2 + half
                    v.wait_ge(P.sem("PMM"), b_mm + u + 1)
                    if tb % 2 == 0:
                        v.wait_ge(P.sem("PLA"), b_la + (tb // 2 + 1) * npt)
                    else:
                        v.wait_ge(P.sem("PLB"), b_lb + (tb // 2 + 1) * npt)
                    if tb >= 2:
                        v.wait_ge(P.sem("STG"), b_sg + (tb - 1) * 16)
                    if gname is not None:
                        v.tensor_tensor(out=gstage[:, :], in0=psU[u % 4][:],
                                        in1=gtile[tb % 2][:, half * 512:(half + 1) * 512],
                                        op=ALU.mult)
                        ins = v.tensor_tensor(
                            out=xnew[tb % 2][:, half * 512:(half + 1) * 512],
                            in0=gstage[:, :],
                            in1=xold[tb % 2][:, half * 512:(half + 1) * 512],
                            op=ALU.add)
                    else:
                        ins = v.scalar_tensor_tensor(
                            out=xnew[tb % 2][:, half * 512:(half + 1) * 512],
                            in0=psU[u % 4][:], scalar=1.0,
                            in1=xold[tb % 2][:, half * 512:(half + 1) * 512],
                            op0=ALU.mult, op1=ALU.add)
                    P.inc("vector", ins, "PAD", 1)

        P.end_phase()
        st.close()
    return ph


def mk_dump(which):
    def ph(P, G):
        block = G["block"]
        buf = G[which]
        out = P.outp("dump_" + which, [128, 8 * S_CORE], F32)
        st = ExitStack()
        tmp = [P.sb(f"dp_tmp{i}", [128, 4096], F32, st) for i in range(2)]
        b_dc = P.val("DPC"); b_da = P.val("DPSA"); b_db = P.val("DPSB")

        @block.vector
        def _(v):
            P.pw(v)
            for c in range(8):
                if c >= 2:
                    if c % 2 == 0:
                        v.wait_ge(P.sem("DPSA"), b_da + (c // 2) * 16)
                    else:
                        v.wait_ge(P.sem("DPSB"), b_db + (c // 2) * 16)
                ins = v.tensor_copy(out=tmp[c % 2][:], in_=buf[:, c * 4096:(c + 1) * 4096])
                P.inc("vector", ins, "DPC", 1)

        @block.sync
        def _(s):
            P.pw(s)
            for c in range(8):
                s.wait_ge(P.sem("DPC"), b_dc + c + 1)
                ins = s.dma_start(out=out[:, c * 4096:(c + 1) * 4096], in_=tmp[c % 2][:])
                P.inc("sync", ins, "DPSA" if c % 2 == 0 else "DPSB", 16)

        P.end_phase()
        st.close()
    return ph


def ph_copyout(P, G):
    block = G["block"]

    @block.sync
    def _(s):
        P.pw(s)
        ins = s.dma_start(out=P.outputs["y"][:, :], in_=G["x_state"][G["par"]][:, :])
        P.inc("sync", ins, "YOUT", 16)
        s.wait_ge(P.sem("YOUT"), P.val("YOUT"))

    P.end_phase()




def mk_moe_l0():
    def ph(P, G):
        block = G["block"]
        XHAT = G["XHAT"]; CTX = G["CTX"]
        NE, DFF = 8, 4096
        NQ = 4
        QF = DFF // NQ
        NTT = 8
        NQE = NE * NQ

        w1_in = P.inp("L0_ffn_w1", [NE, D, DFF])
        w2_in = P.inp("L0_ffn_w2", [NE, DFF, D])
        gw_in = P.inp("L0_ffn_gate_w", [D, NE])
        b1_in = P.inp("L0_ffn_b1", [NE, DFF])
        b2_in = P.inp("L0_ffn_b2", [NE, D])

        st = ExitStack()
        gw = P.sb("mo_gw", [128, 8 * 8], BF16, st)
        CB = P.sb("mo_CB", [128, NTB * 8], F32, st)
        rcols = P.sb("mo_rcols", [128, 8], F32, st)
        retile = P.sb("mo_retile", [128, 8], F32, st)
        rmask = P.sb("mo_rmask", [128, 8], F32, st)
        wb1 = P.sb("mo_wb1", [128, 8 * QF], BF16, st)
        wb2 = P.sb("mo_wb2", [128, 8 * D], BF16, st)
        b1c = P.sb("mo_b1c", [128, 8], F32, st)
        b2r = P.sb("mo_b2r", [1, D], BF16, st)
        hb = P.sb("mo_hb", [128, 8 * 512], BF16, st)
        psR = P.ps("mo_psR", [128, 8], F32, st)
        psH = [P.ps(f"mo_psH{i}", [128, 512], F32, st) for i in range(2)]
        psU = [P.ps(f"mo_psU{i}", [128, 512], F32, st) for i in range(4)]

        b_gl = P.val("MGL"); b_rm = P.val("MRM"); b_rv = P.val("MRV")
        b_re = P.val("MRE"); b_rr = P.val("MRR"); b_rc = P.val("MRC")
        b_w1 = P.val("MW1"); b_w2 = P.val("MW2")
        b_hm = P.val("MHM"); b_he = P.val("MHE"); b_ym = P.val("MYM"); b_ya = P.val("MYA")
        b_cz = P.val("MCZ")

        mw2_after = []
        tot = 0
        for qe in range(NQE):
            tot += 8 * 16 + (16 if qe % NQ == 0 else 0)
            mw2_after.append(tot)

        @block.gpsimd
        def _(g):
            P.pw(g)
            for dc in range(8):
                ins = g.dma_start(out=gw[:, dc * 8:(dc + 1) * 8], in_=gw_in[dc * 128:(dc + 1) * 128, :])
                P.inc("gpsimd", ins, "MGL", 16)
            for qe in range(NQE):
                e, q = qe // NQ, qe % NQ
                if qe >= 1:
                    g.wait_ge(P.sem("MHM"), b_hm + qe * NTT * 8)
                for dc in range(8):
                    ins = g.dma_start(out=wb1[:, dc * QF:(dc + 1) * QF],
                                      in_=w1_in[e, dc * 128:(dc + 1) * 128, q * QF:(q + 1) * QF])
                    P.inc("gpsimd", ins, "MW1", 16)
                ins = g.dma_start(out=b1c[:, 0:8],
                                  in_=b1_in[e, q * QF:(q + 1) * QF].rearrange("(c p) -> p c", p=128),
                                  allow_slow_non_contiguous=True)
                P.inc("gpsimd", ins, "MW1", 16)
                if qe >= 1:
                    g.wait_ge(P.sem("MYM"), b_ym + qe * NTT * 8)
                for fc in range(8):
                    ins = g.dma_start(out=wb2[:, fc * D:(fc + 1) * D],
                                      in_=w2_in[e, q * QF + fc * 128: q * QF + (fc + 1) * 128, :])
                    P.inc("gpsimd", ins, "MW2", 16)
                if q == 0:
                    ins = g.dma_start(out=b2r[0:1, :], in_=b2_in[e, :].rearrange("(o a) -> o a", o=1))
                    P.inc("gpsimd", ins, "MW2", 16)

        @block.vector
        def _(v):
            P.pw(v)
            ins = v.memset(CTX[:], 0.0)
            P.inc("vector", ins, "MCZ", 1)
            for tb in range(NTB):
                v.wait_ge(P.sem("MRM"), b_rm + tb + 1)
                ins = v.reduce_max(out=rcols[:, 0:1], in_=psR[:], axis=AX.X)
                P.inc("vector", ins, "MRV", 1)
                v.wait_ge(P.sem("MRV"), b_rv + tb * 2 + 1)
                ins = v.tensor_scalar(out=rcols[:, 1:2], in0=rcols[:, 0:1], scalar1=-1.0,
                                      scalar2=None, op0=ALU.mult)
                P.inc("vector", ins, "MRV", 1)
                v.wait_ge(P.sem("MRE"), b_re + tb + 1)
                ins = v.reciprocal(out=rcols[:, 3:4], in_=rcols[:, 2:3])
                P.inc("vector", ins, "MRR", 1)
                v.wait_ge(P.sem("MRR"), b_rr + tb * 3 + 1)
                ins = v.tensor_single_scalar(out=rmask[:, :], in_=retile[:, :], scalar=1.0, op=ALU.is_equal)
                P.inc("vector", ins, "MRR", 1)
                v.wait_ge(P.sem("MRR"), b_rr + tb * 3 + 2)
                ins = v.tensor_tensor(out=CB[:, tb * 8:(tb + 1) * 8], in0=rmask[:, :],
                                      in1=rcols[:, 3:4].broadcast_to([128, 8]), op=ALU.mult)
                P.inc("vector", ins, "MRR", 1)
                ins = v.tensor_copy(out=rcols[:, 4:5], in_=rcols[:, 3:4])
                P.inc("vector", ins, "MRC", 1)

        @block.scalar
        def _(sc):
            P.pw(sc)
            for tb in range(NTB):
                sc.wait_ge(P.sem("MRV"), b_rv + tb * 2 + 2)
                if tb >= 1:
                    sc.wait_ge(P.sem("MRC"), b_rc + tb)
                ins = sc.activation(out=retile[:, :], in_=psR[:], func=AF.Exp,
                                    bias=rcols[:, 1:2], accum_out=rcols[:, 2:3])
                P.inc("scalar", ins, "MRE", 1)
            for qe in range(NQE):
                for tt in range(NTT):
                    sc.wait_ge(P.sem("MYM"), b_ym + (qe * NTT + tt) * 8)
                    for fc in range(8):
                        gi = (qe * NTT + tt) * 8 + fc
                        sc.wait_ge(P.sem("MHM"), b_hm + gi + 1)
                        ins = sc.activation(out=hb[:, fc * 512:(fc + 1) * 512],
                                            in_=psH[gi % 2][:], func=AF.Gelu,
                                            bias=b1c[:, fc:fc + 1])
                        P.inc("scalar", ins, "MHE", 1)

        @block.tensor
        def _(t):
            P.pw(t)
            t.wait_ge(P.sem("MGL"), b_gl + 128)
            for tb in range(NTB):
                if tb >= 1:
                    t.wait_ge(P.sem("MRE"), b_re + tb)
                for dc in range(8):
                    ins = t.matmul(psR[:], XHAT[:, dc * S_CORE + tb * 128: dc * S_CORE + (tb + 1) * 128],
                                   gw[:, dc * 8:(dc + 1) * 8], start=(dc == 0), stop=(dc == 7))
                P.inc("tensor", ins, "MRM", 1)
            for qe in range(NQE):
                e, q = qe // NQ, qe % NQ
                t.wait_ge(P.sem("MW1"), b_w1 + (qe + 1) * 9 * 16)
                t.wait_ge(P.sem("MW2"), b_w2 + mw2_after[qe])
                for tt in range(NTT):
                    for fc in range(8):
                        gi = (qe * NTT + tt) * 8 + fc
                        if gi >= 2:
                            t.wait_ge(P.sem("MHE"), b_he + gi - 1)
                        for dc in range(8):
                            ins = t.matmul(psH[gi % 2][:],
                                           wb1[:, dc * QF + fc * 128: dc * QF + (fc + 1) * 128],
                                           XHAT[:, dc * S_CORE + tt * 512: dc * S_CORE + (tt + 1) * 512],
                                           start=(dc == 0), stop=(dc == 7))
                        P.inc("tensor", ins, "MHM", 1)
                    t.wait_ge(P.sem("MHE"), b_he + (qe * NTT + tt + 1) * 8)
                    for tk in range(4):
                        for half in range(2):
                            u = ((qe * NTT + tt) * 4 + tk) * 2 + half
                            if u >= 4:
                                t.wait_ge(P.sem("MYA"), b_ya + u - 3)
                            for fc in range(8):
                                ins = t.matmul(psU[u % 4][:],
                                               hb[:, fc * 512 + tk * 128: fc * 512 + (tk + 1) * 128],
                                               wb2[:, fc * D + half * 512: fc * D + (half + 1) * 512],
                                               start=(fc == 0), stop=(fc == 7 and q != 0))
                            if q == 0:
                                ins = t.matmul(psU[u % 4][:], G["ones_bf"][0:1, 0:128],
                                               b2r[0:1, half * 512:(half + 1) * 512],
                                               start=False, stop=True)
                            P.inc("tensor", ins, "MYM", 1)

        @block.vector
        def _(v):
            for qe in range(NQE):
                e = qe // NQ
                for tt in range(NTT):
                    for tk in range(4):
                        tb = tt * 4 + tk
                        for half in range(2):
                            u = ((qe * NTT + tt) * 4 + tk) * 2 + half
                            v.wait_ge(P.sem("MYM"), b_ym + u + 1)
                            ins = v.scalar_tensor_tensor(
                                out=CTX[:, tb * 1024 + half * 512: tb * 1024 + (half + 1) * 512],
                                in0=psU[u % 4][:], scalar=CB[:, tb * 8 + e: tb * 8 + e + 1],
                                in1=CTX[:, tb * 1024 + half * 512: tb * 1024 + (half + 1) * 512],
                                op0=ALU.mult, op1=ALU.add)
                            P.inc("vector", ins, "MYA", 1)

        P.end_phase()
        st.close()
    return ph


def mk_residual_add():
    def ph(P, G):
        block = G["block"]
        CTX = G["CTX"]
        par = G["par"]
        xsrc = G["x_state"][par]
        xdst = G["x_state"][1 - par]
        G["par"] = 1 - par

        st = ExitStack()
        xold = [P.sb(f"ra_xold{i}", [128, D], F32, st) for i in range(2)]
        xnew = [P.sb(f"ra_xnew{i}", [128, D], F32, st) for i in range(2)]
        b_la = P.val("RLA"); b_lb = P.val("RLB"); b_ad = P.val("RAD"); b_sg = P.val("STG")

        @block.sync
        def _(s):
            P.pw(s)
            for tb in range(NTB):
                if tb >= 2:
                    s.wait_ge(P.sem("RAD"), b_ad + tb - 1)
                ins = s.dma_start(out=xold[tb % 2][:], in_=xsrc[tb * 128:(tb + 1) * 128, :])
                P.inc("sync", ins, "RLA" if tb % 2 == 0 else "RLB", 16)

        @block.vector
        def _(v):
            P.pw(v)
            for tb in range(NTB):
                if tb % 2 == 0:
                    v.wait_ge(P.sem("RLA"), b_la + (tb // 2 + 1) * 16)
                else:
                    v.wait_ge(P.sem("RLB"), b_lb + (tb // 2 + 1) * 16)
                if tb >= 2:
                    v.wait_ge(P.sem("STG"), b_sg + (tb - 1) * 16)
                ins = v.tensor_tensor(out=xnew[tb % 2][:], in0=xold[tb % 2][:],
                                      in1=CTX[:, tb * 1024:(tb + 1) * 1024], op=ALU.add)
                P.inc("vector", ins, "RAD", 1)

        @block.gpsimd
        def _(g):
            P.pw(g)
            for tb in range(NTB):
                g.wait_ge(P.sem("RAD"), b_ad + tb + 1)
                ins = g.dma_start(out=xdst[tb * 128:(tb + 1) * 128, :], in_=xnew[tb % 2][:])
                P.inc("gpsimd", ins, "STG", 16)

        P.end_phase()
        st.close()
    return ph


def l0_ffn_phases():
    return [ph_ln, mk_moe_l0(), mk_residual_add()]


def mk_moe_l1():
    def ph(P, G):
        import os
        ROUTER_ONLY = bool(os.environ.get("L1F_ROUTER_ONLY"))
        block = G["block"]
        XHAT = G["XHAT"]; CTX = G["CTX"]
        NE, FI = 16, 512
        NTT = 8

        rw_bf = bfview(P, "L1_ffn_router_weight", D)
        wg_bf = bfview(P, "L1_ffn_w_gate", D)
        wu_bf = bfview(P, "L1_ffn_w_up", D)
        wd_bf = bfview(P, "L1_ffn_w_down", FI)
        rb_in = P.inp("L1_ffn_router_bias", [NE])

        st = ExitStack()
        rw = P.sb("m1_rw", [128, 8 * NE], BF16, st)
        rbr = P.sb("m1_rbr", [1, NE], F32, st)
        rbb = P.sb("m1_rbb", [128, NE], F32, st)
        CB = P.sb("m1_CB", [128, NTB * NE], F32, st)
        sco = P.sb("m1_sco", [128, NE], F32, st)
        sch = P.sb("m1_sch", [128, NE], F32, st)
        cur = P.sb("m1_cur", [128, NE], F32, st)
        curB = P.sb("m1_curB", [128, NE], F32, st)
        msk = P.sb("m1_msk", [128, NE], F32, st)
        wv = P.sb("m1_wv", [128, NE], F32, st)
        neg2 = P.sb("m1_neg2", [128, NE], F32, st)
        msk8 = P.sb("m1_msk8", [128, NE], mybir.dt.uint8, st)
        gm8 = P.sb("m1_gm8", [128, 4], mybir.dt.uint8, st)
        gsc = P.sb("m1_gsc", [128, 40], F32, st)
        rc = P.sb("m1_rc", [128, 8], F32, st)
        wgT = P.sb("m1_wgT", [128, 8 * FI], BF16, st)
        wuT = P.sb("m1_wuT", [128, 8 * FI], BF16, st)
        wdT = P.sb("m1_wdT", [128, 4 * D], BF16, st)
        gb = P.sb("m1_gb", [128, 4 * 512], BF16, st)
        gu = P.sb("m1_gu", [128, 4 * 512], BF16, st)
        psR = P.ps("m1_psR", [128, 16], F32, st)
        psB = P.ps("m1_psB", [128, 16], F32, st)
        psG = [P.ps(f"m1_psG{i}", [128, 512], F32, st) for i in range(2)]
        psU = [P.ps(f"m1_psU{i}", [128, 512], F32, st) for i in range(4)]

        b_rl = P.val("NRL"); b_rm = P.val("NRM"); b_rsg = P.val("NSG")
        b_rv = P.val("NRV"); b_rc = P.val("NRC")
        b_wl = P.val("NWL"); b_gm = P.val("NGM"); b_ge = P.val("NGE")
        b_um = P.val("NUM"); b_gu = P.val("NGU"); b_ym = P.val("NYM"); b_ya = P.val("NYA")
        b_cz = P.val("NCZ"); b_bb = P.val("NBB")

        @block.sync
        def _(s):
            P.pw(s)
            s.wait_ge(P.sem("WPREP"), P.val("WPREP"))
            for dc in range(8):
                ins = s.dma_start_transpose(out=rw[:, dc * NE:(dc + 1) * NE],
                                            in_=rw_bf[0:16, dc * 128:(dc + 1) * 128])
                P.inc("sync", ins, "NRL", 16)
            for e in range(NE if not ROUTER_ONLY else 0):
                if e >= 1:
                    s.wait_ge(P.sem("NYM"), b_ym + e * NTT * 8)
                for dc in range(8):
                    i1 = s.dma_start_transpose(out=wgT[:, dc * FI:(dc + 1) * FI],
                                               in_=wg_bf[e * FI:(e + 1) * FI, dc * 128:(dc + 1) * 128])
                    P.inc("sync", i1, "NWL", 16)
                    i2 = s.dma_start_transpose(out=wuT[:, dc * FI:(dc + 1) * FI],
                                               in_=wu_bf[e * FI:(e + 1) * FI, dc * 128:(dc + 1) * 128])
                    P.inc("sync", i2, "NWL", 16)
                for ic in range(4):
                    i3 = s.dma_start_transpose(out=wdT[:, ic * D:(ic + 1) * D],
                                               in_=wd_bf[e * D:(e + 1) * D, ic * 128:(ic + 1) * 128])
                    P.inc("sync", i3, "NWL", 16)

        @block.gpsimd
        def _(g):
            P.pw(g)
            ins = g.dma_start(out=rbr[0:1, :], in_=rb_in.rearrange("(o a) -> o a", o=1))
            P.inc("gpsimd", ins, "NBB", 16)

        @block.tensor
        def _(t):
            P.pw(t)
            t.wait_ge(P.sem("NRL"), b_rl + 128)
            t.wait_ge(P.sem("NBB"), b_bb + 16)
            ins = t.matmul(psB[:], G["ones_f32"][0:1, 0:128], rbr[0:1, :], start=True, stop=True)
            P.inc("tensor", ins, "NRM", 1)
            for tb in range(NTB):
                if tb >= 1:
                    t.wait_ge(P.sem("NSG"), b_rsg + tb)
                for dc in range(8):
                    ins = t.matmul(psR[:], XHAT[:, dc * S_CORE + tb * 128: dc * S_CORE + (tb + 1) * 128],
                                   rw[:, dc * NE:(dc + 1) * NE], start=(dc == 0), stop=(dc == 7))
                P.inc("tensor", ins, "NRM", 1)
            for e in range(NE if not ROUTER_ONLY else 0):
                t.wait_ge(P.sem("NWL"), b_wl + (e + 1) * 20 * 16)
                for tt in range(NTT):
                    t.wait_ge(P.sem("NYA"), b_ya + (e * NTT + tt) * 8)
                    for fc in range(4):
                        gi = (e * NTT + tt) * 4 + fc
                        if gi >= 2:
                            t.wait_ge(P.sem("NGE"), b_ge + gi - 1)
                        for dc in range(8):
                            ins = t.matmul(psG[gi % 2][:],
                                           wgT[:, dc * FI + fc * 128: dc * FI + (fc + 1) * 128],
                                           XHAT[:, dc * S_CORE + tt * 512: dc * S_CORE + (tt + 1) * 512],
                                           start=(dc == 0), stop=(dc == 7))
                        P.inc("tensor", ins, "NGM", 1)
                        uu = gi % 4
                        t.wait_ge(P.sem("NGU"), b_gu + max(0, gi - 3))
                        for dc in range(8):
                            ins = t.matmul(psU[uu][:],
                                           wuT[:, dc * FI + fc * 128: dc * FI + (fc + 1) * 128],
                                           XHAT[:, dc * S_CORE + tt * 512: dc * S_CORE + (tt + 1) * 512],
                                           start=(dc == 0), stop=(dc == 7))
                        P.inc("tensor", ins, "NUM", 1)
                    t.wait_ge(P.sem("NGU"), b_gu + (e * NTT + tt + 1) * 4)
                    for tk in range(4):
                        for half in range(2):
                            u = ((e * NTT + tt) * 4 + tk) * 2 + half
                            uu = u % 4
                            t.wait_ge(P.sem("NYA"), b_ya + max(0, u - 3))
                            for ic in range(4):
                                ins = t.matmul(psU[uu][:],
                                               gu[:, ic * 512 + tk * 128: ic * 512 + (tk + 1) * 128],
                                               wdT[:, ic * D + half * 512: ic * D + (half + 1) * 512],
                                               start=(ic == 0), stop=(ic == 3))
                            P.inc("tensor", ins, "NYM", 1)

        @block.scalar
        def _(sc):
            P.pw(sc)
            for tb in range(NTB):
                sc.wait_ge(P.sem("NRM"), b_rm + tb + 2)
                if tb >= 1:
                    sc.wait_ge(P.sem("NRC"), b_rc + tb)
                ins = sc.activation(out=sco[:, :], in_=psR[:], func=AF.Sigmoid)
                P.inc("scalar", ins, "NSG", 1)
            for e in range(NE if not ROUTER_ONLY else 0):
                for tt in range(NTT):
                    sc.wait_ge(P.sem("NGU"), b_gu + (e * NTT + tt) * 4)
                    for fc in range(4):
                        gi = (e * NTT + tt) * 4 + fc
                        sc.wait_ge(P.sem("NGM"), b_gm + gi + 1)
                        ins = sc.activation(out=gb[:, fc * 512:(fc + 1) * 512],
                                            in_=psG[gi % 2][:], func=AF.Silu)
                        P.inc("scalar", ins, "NGE", 1)

        @block.vector
        def _(v):
            P.pw(v)
            ins = v.memset(CTX[:], 0.0)
            P.inc("vector", ins, "NCZ", 1)
            ins = v.memset(neg2[:, :], -2.0e9)
            P.inc("vector", ins, "NCZ", 1)
            v.wait_ge(P.sem("NRM"), b_rm + 1)
            ins = v.tensor_copy(out=rbb[:, :], in_=psB[:])
            P.inc("vector", ins, "NCZ", 1)
            RT = "NRT"
            rt = P.val(RT)
            def step(instr):
                nonlocal rt
                rt = P.inc("vector", instr, RT, 1)
                v.wait_ge(P.sem(RT), rt)
            sgT = P.sb("m1_sgT", [128, NE], F32, st)
            m16 = P.sb("m1_m16", [128, NE], mybir.dt.uint8, st)
            zt = P.sb("m1_zt", [128, NE], F32, st)
            step0 = v.memset(zt[:, :], 0.0)
            P.inc("vector", step0, "NRT", 1)
            pairs = [(0, 1), (0, 2), (0, 3), (1, 2), (1, 3), (2, 3)]
            for tb in range(NTB):
                v.wait_ge(P.sem("NSG"), b_rsg + tb + 1)
                step(v.tensor_tensor(out=sch[:, :], in0=sco[:, :], in1=rbb[:, :], op=ALU.add))
                step(v.tensor_copy(out=sgT[:, :].rearrange("p (k g) -> p k g", g=4),
                                  in_=sch[:, :].rearrange("p (g k) -> p k g", k=4)))
                for pi, (a, b) in enumerate(pairs):
                    step(v.tensor_tensor(out=gsc[:, pi * 4:(pi + 1) * 4],
                                         in0=sgT[:, a * 4:(a + 1) * 4], in1=sgT[:, b * 4:(b + 1) * 4],
                                         op=ALU.add))
                step(v.tensor_tensor(out=gsc[:, 24:28], in0=gsc[:, 0:4], in1=gsc[:, 4:8], op=ALU.max))
                step(v.tensor_tensor(out=gsc[:, 24:28], in0=gsc[:, 24:28], in1=gsc[:, 8:12], op=ALU.max))
                step(v.tensor_tensor(out=gsc[:, 24:28], in0=gsc[:, 24:28], in1=gsc[:, 12:16], op=ALU.max))
                step(v.tensor_tensor(out=gsc[:, 24:28], in0=gsc[:, 24:28], in1=gsc[:, 16:20], op=ALU.max))
                step(v.tensor_tensor(out=gsc[:, 24:28], in0=gsc[:, 24:28], in1=gsc[:, 20:24], op=ALU.max))
                gg = gsc[:, 24:28]
                for pi, (a, b) in enumerate(pairs):
                    step(v.tensor_tensor(out=gsc[:, 28 + pi:29 + pi],
                                         in0=gg[:, a:a + 1], in1=gg[:, b:b + 1], op=ALU.min))
                step(v.tensor_tensor(out=rc[:, 0:1], in0=gsc[:, 28:29], in1=gsc[:, 29:30], op=ALU.max))
                step(v.tensor_tensor(out=rc[:, 0:1], in0=rc[:, 0:1], in1=gsc[:, 30:31], op=ALU.max))
                step(v.tensor_tensor(out=rc[:, 0:1], in0=rc[:, 0:1], in1=gsc[:, 31:32], op=ALU.max))
                step(v.tensor_tensor(out=rc[:, 0:1], in0=rc[:, 0:1], in1=gsc[:, 32:33], op=ALU.max))
                step(v.tensor_tensor(out=rc[:, 0:1], in0=rc[:, 0:1], in1=gsc[:, 33:34], op=ALU.max))
                step(v.tensor_tensor(out=gm8[:, :], in0=gsc[:, 24:28],
                                     in1=rc[:, 0:1].broadcast_to([128, 4]), op=ALU.is_ge))
                for g8 in range(4):
                    step(v.tensor_copy(out=m16[:, g8 * 4:(g8 + 1) * 4],
                                       in_=gm8[:, g8:g8 + 1].broadcast_to([128, 4])))
                step(v.select(out=cur[:, :], mask=m16[:, :], on_true=sch[:, :], on_false=neg2[:, :]))
                bufs = [cur, curB]
                for r in range(4):
                    src_t = bufs[r % 2]
                    step(v.reduce_max(out=rc[:, 1:2], in_=src_t[:, :], axis=AX.X))
                    if r < 3:
                        step(v.tensor_tensor(out=msk8[:, :], in0=src_t[:, :],
                                             in1=rc[:, 1:2].broadcast_to([128, NE]), op=ALU.is_ge))
                        step(v.select(out=bufs[(r + 1) % 2][:, :], mask=msk8[:, :],
                                      on_true=neg2[:, :], on_false=src_t[:, :]))
                step(v.select(out=msk[:, :], mask=m16[:, :], on_true=sch[:, :], on_false=neg2[:, :]))
                step(v.tensor_tensor(out=msk8[:, :], in0=msk[:, :],
                                     in1=rc[:, 1:2].broadcast_to([128, NE]), op=ALU.is_ge))
                step(v.select(out=wv[:, :], mask=msk8[:, :], on_true=sco[:, :], on_false=zt[:, :]))
                step(v.reduce_sum(out=rc[:, 2:3], in_=wv[:, :], axis=AX.X))
                step(v.tensor_scalar(out=rc[:, 3:4], in0=rc[:, 2:3], scalar1=1e-20,
                                     scalar2=None, op0=ALU.add))
                step(v.reciprocal(out=rc[:, 4:5], in_=rc[:, 3:4]))
                step(v.tensor_scalar(out=rc[:, 5:6], in0=rc[:, 4:5], scalar1=2.5,
                                     scalar2=None, op0=ALU.mult))
                ins = v.tensor_tensor(out=CB[:, tb * NE:(tb + 1) * NE], in0=wv[:, :],
                                      in1=rc[:, 5:6].broadcast_to([128, NE]), op=ALU.mult)
                P.inc("vector", ins, "NRC", 1)
            if ROUTER_ONLY:
                v.wait_ge(P.sem("NRC"), b_rc + NTB)
                ins = v.tensor_copy(out=CTX[:, 0:NTB * NE], in_=CB[:, :])
                P.inc("vector", ins, "NYA", 1)
                P.end_phase()
                st.close()
                return
            for e in range(NE):
                for tt in range(NTT):
                    v.wait_ge(P.sem("NYM"), b_ym + (e * NTT + tt) * 8)
                    for fc in range(4):
                        gi = (e * NTT + tt) * 4 + fc
                        uu = gi % 4
                        v.wait_ge(P.sem("NGE"), b_ge + gi + 1)
                        v.wait_ge(P.sem("NUM"), b_um + gi + 1)
                        ins = v.tensor_tensor(out=gu[:, fc * 512:(fc + 1) * 512],
                                              in0=gb[:, fc * 512:(fc + 1) * 512],
                                              in1=psU[uu][:], op=ALU.mult)
                        P.inc("vector", ins, "NGU", 1)
                    for tk in range(4):
                        tb = tt * 4 + tk
                        for half in range(2):
                            u = ((e * NTT + tt) * 4 + tk) * 2 + half
                            v.wait_ge(P.sem("NYM"), b_ym + u + 1)
                            if e == 0 and tt == 0:
                                v.wait_ge(P.sem("NCZ"), b_cz + 3)
                                v.wait_ge(P.sem("NRC"), b_rc + NTB)
                            ins = v.scalar_tensor_tensor(
                                out=CTX[:, tb * 1024 + half * 512: tb * 1024 + (half + 1) * 512],
                                in0=psU[u % 4][:], scalar=CB[:, tb * NE + e: tb * NE + e + 1],
                                in1=CTX[:, tb * 1024 + half * 512: tb * 1024 + (half + 1) * 512],
                                op0=ALU.mult, op1=ALU.add)
                            P.inc("vector", ins, "NYA", 1)

        P.end_phase()
        st.close()
    return ph


def l1_ffn_phases():
    return [
        mk_prep([("L1_ffn_router_weight", [16, D]), ("L1_ffn_w_gate", [16, 512, D]),
                 ("L1_ffn_w_up", [16, 512, D]), ("L1_ffn_w_down", [16, D, 512])]),
        ph_ln, mk_moe_l1(), mk_residual_add(),
    ]


def mk_relbias_prep():
    def ph(P, G):
        block = G["block"]
        XHAT = G["XHAT"]
        oh = P.inp("onehot_norm", [320, 8192])
        re_in = P.inp("L1_attn_b_rel_embed", [320, 16])
        gw_in = P.inp("L1_attn_b_gru_W", [8, 64])
        gb_in = P.inp("L1_attn_b_gru_b", [8])
        gc_in = P.inp("L1_attn_b_gru_const", [1, 16, 1, 1])
        biasnorm = P.scratch("biasnorm", [16, 8192])

        st = ExitStack()
        ret = P.sb("rb_ret", [128, 3 * 16], F32, st)
        oht = [P.sb(f"rb_oht{i}", [128, 3 * 512], F32, st) for i in range(2)]
        bout = [P.sb(f"rb_bout{i}", [16, 512], F32, st) for i in range(2)]
        gwL = P.sb("rb_gwL", [8, 64], F32, st)
        gwTT = P.sb("rb_gwTT", [128, 8], BF16, st)
        gbr = P.sb("rb_gbr", [1, 8], BF16, st)
        gcr = P.sb("rb_gcr", [1, 16], F32, st)
        gcc = P.sb("rb_gcc", [128, 16], F32, st)
        gcols = P.sb("rb_gcols", [128, 8], F32, st)
        gsig = P.sb("rb_gsig", [128, 2], F32, st)
        gout = P.sb("rb_gout", [128, 1], F32, st)
        growrow = P.sb("rb_growrow", [1, S_CORE], F32, st)
        grows_d = P.scratch("grows", [16, S_CORE])
        psB = [P.ps(f"rb_psB{i}", [16, 512], F32, st) for i in range(2)]
        psG = P.ps("rb_psG", [128, 8], F32, st)
        psC = P.ps("rb_psC", [128, 16], F32, st)
        psW = P.ps("rb_psW", [128, 8], F32, st)
        psR = P.ps("rb_psR", [1, 128], F32, st)

        b_ol = P.val("RBOA"); b_ob = P.val("RBOB"); b_rl = P.val("RBRL")
        b_mm = P.val("RBMM"); b_bv = P.val("RBEV"); b_bs = P.val("RBST")
        b_gl = P.val("RBGL"); b_gm = P.val("RBGM"); b_gs = P.val("RBGS")
        b_sg = P.val("RBSG"); b_gv = P.val("RBGV"); b_gt = P.val("RBGT"); b_gr = P.val("RBGR")
        b_ge = P.val("RBGE")
        b_gw2 = P.val("RBGW2")
        b_gwst = P.val("RBGW")

        NSL = 16
        NG = 16 * NTB

        @block.sync
        def _(s):
            P.pw(s)
            i = s.dma_start(out=ret[:, 0:16], in_=re_in[0:128, :])
            P.inc("sync", i, "RBRL", 16)
            i = s.dma_start(out=ret[:, 16:32], in_=re_in[128:256, :])
            P.inc("sync", i, "RBRL", 16)
            i = s.dma_start(out=ret[0:64, 32:48], in_=re_in[256:320, :])
            P.inc("sync", i, "RBRL", 16)
            i = s.dma_start(out=gwL[:, :], in_=gw_in[:, :])
            P.inc("sync", i, "RBRL", 16)
            i = s.dma_start(out=gcr[0:1, :], in_=gc_in.rearrange("o h a b -> (o a b) h"))
            P.inc("sync", i, "RBRL", 16)
            for sl in range(NSL):
                par = "RBOA" if sl % 2 == 0 else "RBOB"
                if sl >= 2:
                    s.wait_ge(P.sem("RBMM"), b_mm + sl - 1)
                i1 = s.dma_start(out=oht[sl % 2][:, 0:512], in_=oh[0:128, sl * 512:(sl + 1) * 512])
                P.inc("sync", i1, par, 16)
                i2 = s.dma_start(out=oht[sl % 2][:, 512:1024], in_=oh[128:256, sl * 512:(sl + 1) * 512])
                P.inc("sync", i2, par, 16)
                i3 = s.dma_start(out=oht[sl % 2][0:64, 1024:1536], in_=oh[256:320, sl * 512:(sl + 1) * 512])
                P.inc("sync", i3, par, 16)

        @block.gpsimd
        def _(g):
            P.pw(g)
            i = g.dma_start(out=gbr[0:1, :], in_=gb_in.rearrange("(o a) -> o a", o=1))
            P.inc("gpsimd", i, "RBGL", 16)

            for sl in range(NSL):
                g.wait_ge(P.sem("RBEV"), b_bv + sl + 1)
                ins = g.dma_start(out=biasnorm[:, sl * 512:(sl + 1) * 512], in_=bout[sl % 2][:, :])
                P.inc("gpsimd", ins, "RBST", 16)
            g.wait_ge(P.sem("RBGE"), b_ge + 1)
            ins = g.dma_start(out=gwTT[64:128, :], in_=gwTT[0:64, :])
            P.inc("gpsimd", ins, "RBGW2", 16)
            for h in range(16):
                g.wait_ge(P.sem("RBGR"), b_gr + (h + 1) * NTB)
                ins = g.dma_start(out=grows_d[h:h + 1, :], in_=growrow[0:1, :])
                P.inc("gpsimd", ins, "RBGW", 16)

        @block.tensor
        def _(t):
            P.pw(t)
            t.wait_ge(P.sem("RBRL"), b_rl + 80)
            ins = t.transpose(psW[0:64, :], gwL[0:8, 0:64], G["ident_f32"][0:8, 0:8])
            P.inc("tensor", ins, "RBGM", 1)
            ins = t.matmul(psC[:], G["ones_f32"][0:1, 0:128], gcr[0:1, :], start=True, stop=True)
            P.inc("tensor", ins, "RBGM", 1)
            for sl in range(NSL):
                if sl % 2 == 0:
                    t.wait_ge(P.sem("RBOA"), b_ol + (sl // 2 + 1) * 48)
                else:
                    t.wait_ge(P.sem("RBOB"), b_ob + (sl // 2 + 1) * 48)
                if sl >= 2:
                    t.wait_ge(P.sem("RBEV"), b_bv + sl - 1)
                t.matmul(psB[sl % 2][:], ret[:, 0:16], oht[sl % 2][:, 0:512], start=True, stop=False)
                t.matmul(psB[sl % 2][:], ret[:, 16:32], oht[sl % 2][:, 512:1024], start=False, stop=False)
                ins = t.matmul(psB[sl % 2][:], ret[0:64, 32:48], oht[sl % 2][0:64, 1024:1536],
                               start=False, stop=True)
                P.inc("tensor", ins, "RBMM", 1)
            t.wait_ge(P.sem("RBGE"), b_ge + 2)
            t.wait_ge(P.sem("RBGW2"), b_gw2 + 16)
            t.wait_ge(P.sem("RBGL"), b_gl + 16)
            for h in range(16):
                for tb in range(NTB):
                    gi = h * NTB + tb
                    if gi >= 1:
                        t.wait_ge(P.sem("RBGS"), b_gs + gi)
                    r0g = 64 * (h % 2)
                    t.matmul(psG[:], XHAT[r0g:r0g + 64,
                                          (h // 2) * S_CORE + tb * 128:(h // 2) * S_CORE + (tb + 1) * 128],
                             gwTT[r0g:r0g + 64, :], start=True, stop=False)
                    ins = t.matmul(psG[:], G["ones_bf"][0:1, 0:128], gbr[0:1, :], start=False, stop=True)
                    P.inc("tensor", ins, "RBGM", 1)
                    t.wait_ge(P.sem("RBGV"), b_gv + gi + 1)
                    if gi >= 1:
                        t.wait_ge(P.sem("RBGR"), b_gr + gi)
                    ins = t.transpose(psR[:, :], gout[:, 0:1], G["ident_f32"][:, :])
                    P.inc("tensor", ins, "RBGT", 1)

        @block.vector
        def _(v):
            P.pw(v)
            for sl in range(NSL):
                v.wait_ge(P.sem("RBMM"), b_mm + sl + 1)
                if sl >= 2:
                    v.wait_ge(P.sem("RBST"), b_bs + (sl - 1) * 16)
                ins = v.tensor_copy(out=bout[sl % 2][:, :], in_=psB[sl % 2][:, :])
                P.inc("vector", ins, "RBEV", 1)
            v.wait_ge(P.sem("RBGM"), b_gm + 2)
            ins = v.tensor_copy(out=gwTT[0:64, :], in_=psW[0:64, :])
            P.inc("vector", ins, "RBGE", 1)
            ins = v.tensor_copy(out=gcc[:, :], in_=psC[:, :])
            P.inc("vector", ins, "RBGE", 1)
            RT = "RBRT"
            rt = P.val(RT)
            def step(instr):
                nonlocal rt
                rt = P.inc("vector", instr, RT, 1)
                v.wait_ge(P.sem(RT), rt)
            for h in range(16):
                for tb in range(NTB):
                    gi = h * NTB + tb
                    v.wait_ge(P.sem("RBGM"), b_gm + 2 + gi + 1)
                    if gi >= 1:
                        v.wait_ge(P.sem("RBSG"), b_sg + gi)
                    step(v.reduce_sum(out=gcols[:, 0:1], in_=psG[:, 0:4], axis=AX.X))
                    ins = v.reduce_sum(out=gcols[:, 1:2], in_=psG[:, 4:8], axis=AX.X)
                    P.inc("vector", ins, "RBGS", 1)
                    v.wait_ge(P.sem("RBSG"), b_sg + gi + 1)
                    step(v.tensor_tensor(out=gcols[:, 3:4], in0=gsig[:, 1:2], in1=gcc[:, h:h + 1], op=ALU.mult))
                    step(v.tensor_scalar(out=gcols[:, 4:5], in0=gcols[:, 3:4], scalar1=-1.0,
                                         scalar2=None, op0=ALU.add))
                    step(v.tensor_tensor(out=gcols[:, 5:6], in0=gsig[:, 0:1], in1=gcols[:, 4:5], op=ALU.mult))
                    ins = v.tensor_scalar(out=gout[:, 0:1], in0=gcols[:, 5:6], scalar1=2.0,
                                          scalar2=None, op0=ALU.add)
                    P.inc("vector", ins, "RBGV", 1)
                    v.wait_ge(P.sem("RBGT"), b_gt + gi + 1)
                    if tb == 0 and h >= 1:
                        v.wait_ge(P.sem("RBGW"), b_gwst + h * 16)
                    ins = v.tensor_copy(out=growrow[0:1, tb * 128:(tb + 1) * 128],
                                        in_=psR[0:1, :])
                    P.inc("vector", ins, "RBGR", 1)

        @block.scalar
        def _(sc):
            P.pw(sc)
            for h in range(16):
                for tb in range(NTB):
                    gi = h * NTB + tb
                    sc.wait_ge(P.sem("RBGS"), b_gs + gi + 1)
                    if gi >= 1:
                        sc.wait_ge(P.sem("RBGV"), b_gv + gi)
                    ins = sc.activation(out=gsig[:, 0:2], in_=gcols[:, 0:2], func=AF.Sigmoid)
                    P.inc("scalar", ins, "RBSG", 1)

        P.end_phase()
        st.close()
    return ph


class Chain:

    def __init__(self, P, name):
        self.P = P
        self.name = name
        self.n = P.val(name)

    def do(self, engname, eng, instr):
        self.n = self.P.inc(engname, instr, self.name, 1)
        return instr

    def wait(self, eng):
        eng.wait_ge(self.P.sem(self.name), self.n)


def mk_mamba():
    def ph(P, G):
        block = G["block"]
        XHAT = G["XHAT"]; CTX = G["CTX"]
        H, NS, CH, PD, PROJ = 16, 16, 64, 64, 2576
        NCH = S_CORE // CH
        yscr = P.scratch("mamba_y", [S_CORE, D])

        win_bf = bfview(P, "L0_attn_b_W_in", D)
        dtw_in = P.inp("L0_attn_b_dt_W", [16, 16])
        dtb_in = P.inp("L0_attn_b_dt_b", [16])
        al_in = P.inp("L0_attn_b_A_log", [16])
        dp_in = P.inp("L0_attn_b_D_param", [16])
        nw_in = P.inp("L0_attn_b_norm_w", [D])
        rep16 = P.inp("rep16", [16, 1024])
        maskrep = P.inp("maskrep", [64, 1024])

        st = ExitStack()
        wTf = P.sb("mb_wTf", [128, 8 * 448], BF16, st)
        repf = P.sb("mb_repf", [16, 1024], F32, st)
        mrep = P.sb("mb_mrep", [64, 1024], F32, st)
        dtwT = P.sb("mb_dtwT", [16, 16], BF16, st)
        prow = P.sb("mb_prow", [1, 64], F32, st)
        arow = P.sb("mb_arow", [1, 16], F32, st)
        acol = P.sb("mb_acol", [16, 1], F32, st)
        dcol = P.sb("mb_dcol", [16, 1], F32, st)
        dtbrow = P.sb("mb_dtbrow", [1, 16], BF16, st)
        nwrow = P.sb("mb_nwrow", [1, D], F32, st)
        nwb = P.sb("mb_nwb", [64, D], F32, st)
        dbc = P.sb("mb_dbc", [64, 1024], F32, st)
        drow = P.sb("mb_drow", [1, 1024], F32, st)
        pjz = P.sb("mb_pjz", [64, 1040], F32, st)
        pjb = P.sb("mb_pjb", [64, PROJ], BF16, st)
        dtlT = P.sb("mb_dtlT", [16, 64], BF16, st)
        dtt = P.sb("mb_dtt", [64, 16], F32, st)
        sp1 = P.sb("mb_sp1", [64, 16], F32, st)
        sp2 = P.sb("mb_sp2", [64, 16], F32, st)
        dtf = P.sb("mb_dtf", [16, 64], F32, st)
        dafm = P.sb("mb_dafm", [16, 64], F32, st)
        lcs = P.sb("mb_lcs", [16, 64], F32, st)
        zz16 = P.sb("mb_zz16", [16, 64], F32, st)
        t2m = P.sb("mb_t2m", [64, 1024], F32, st)
        tdt = P.sb("mb_tdt", [64, 1024], F32, st)
        e1 = P.sb("mb_e1", [64, 1024], F32, st)
        sS = P.sb("mb_sS", [64, 1024], BF16, st)
        bcfm = P.sb("mb_bcfm", [16, 2048], BF16, st)
        explf = P.sb("mb_explf", [16, 64], F32, st)
        expltm = P.sb("mb_expltm", [64, 16], F32, st)
        d2e = P.sb("mb_d2e", [16, 64], F32, st)
        ecol = P.sb("mb_ecol", [16, 1], F32, st)
        wx = P.sb("mb_wx", [64, 1024], BF16, st)
        stt_ = P.sb("mb_stt", [16, 1024], F32, st)
        stb = P.sb("mb_stb", [16, 1024], BF16, st)
        ytm = P.sb("mb_ytm", [64, 1024], F32, st)
        rcols = P.sb("mb_rcols", [64, 8], F32, st)
        psP = P.ps("mb_psP", [64, 512], F32, st)
        psA = P.ps("mb_psA", [64, 1024], F32, st)
        psB2 = P.ps("mb_psB2", [64, 1024], F32, st)
        psD2 = P.ps("mb_psD2", [16, 1024], F32, st)
        psBC = P.ps("mb_psBC", [16, 1024], BF16, st)

        b_wl = P.val("ZWL"); b_cl = P.val("ZCL")
        C = Chain(P, "ZCH")
        dmn = [P.val("ZDM")]
        dtn = [P.val("ZDT2")]


        @block.gpsimd
        def _(g):
            P.pw(g)
            for dst, srcap in [
                (repf[:, :], rep16[:, :]),
                (mrep[:, :], maskrep[:, :]),
                (prow[0:1, 0:16], al_in.rearrange("(o a) -> o a", o=1)),
                (prow[0:1, 32:48], dp_in.rearrange("(o a) -> o a", o=1)),
                (nwrow[0:1, :], nw_in.rearrange("(o a) -> o a", o=1)),
                (dtbrow[0:1, :], dtb_in.rearrange("(o a) -> o a", o=1)),
            ]:
                ins = g.dma_start(out=dst, in_=srcap)
                P.inc("gpsimd", ins, "ZCL", 16)
            ins = g.dma_start(out=dtwT[:, :], in_=dtw_in.rearrange("a b -> b a"),
                              allow_slow_non_contiguous=True)
            P.inc("gpsimd", ins, "ZCL", 16)

        script = []

        def op(engname, fn):
            script.append((engname, fn))

        def chain_op(engname, emit):
            def fn(h, emit=emit, engname=engname):
                C.wait(h)
                h.wait_ge(P.sem("ZDM"), dmn[0])
                h.wait_ge(P.sem("ZDT2"), dtn[0])
                ins = emit(h)
                C.do(engname, h, ins)
            op(engname, fn)

        def chain_dma_sync(emit):
            def fn(h, emit=emit):
                C.wait(h)
                h.wait_ge(P.sem("ZDM"), dmn[0])
                h.wait_ge(P.sem("ZDT2"), dtn[0])
                ins = emit(h)
                if ins is not None:
                    dtn[0] = P.inc("sync", ins, "ZDT2", 16)
            op("sync", fn)

        def chain_dma(emit):
            def fn(h, emit=emit):
                C.wait(h)
                h.wait_ge(P.sem("ZDM"), dmn[0])
                ins = emit(h)
                dmn[0] = P.inc("gpsimd", ins, "ZDM", 16)
            op("gpsimd", fn)

        chain_op("scalar", lambda sc: sc.activation(out=arow[0:1, :], in_=prow[0:1, 0:16], func=AF.Exp))
        chain_op("vector", lambda v: v.tensor_scalar(out=arow[0:1, :], in0=arow[0:1, :], scalar1=-1.0, scalar2=None, op0=ALU.mult))
        chain_op("tensor", lambda t: t.transpose(psA[0:16, 0:1], arow[0:1, 0:16], G["ident_f32"][0:1, 0:1]))
        chain_op("vector", lambda v: v.tensor_copy(out=acol[:, :], in_=psA[0:16, 0:1]))
        chain_op("tensor", lambda t: t.transpose(psA[0:16, 0:1], prow[0:1, 32:48], G["ident_f32"][0:1, 0:1]))
        chain_op("vector", lambda v: v.tensor_copy(out=dcol[:, :], in_=psA[0:16, 0:1]))
        def mm_drow(t):
            t.matmul(psA[0:1, 0:512], dcol[:, 0:1], repf[:, 0:512], start=True, stop=True)
            return t.matmul(psA[0:1, 512:1024], dcol[:, 0:1], repf[:, 512:1024], start=True, stop=True)
        chain_op("tensor", mm_drow)
        chain_op("vector", lambda v: v.tensor_copy(out=drow[0:1, :], in_=psA[0:1, 0:1024]))
        def mm_dbc(t):
            t.matmul(psA[:, 0:512], G["ones_f32"][0:1, 0:64], drow[0:1, 0:512], start=True, stop=True)
            return t.matmul(psA[:, 512:1024], G["ones_f32"][0:1, 0:64], drow[0:1, 512:1024], start=True, stop=True)
        chain_op("tensor", mm_dbc)
        chain_op("vector", lambda v: v.tensor_copy(out=dbc[:, :], in_=psA[0:64, :]))
        chain_op("tensor", lambda t: t.matmul(psP[:], G["ones_f32"][0:1, 0:64], nwrow[0:1, 0:512], start=True, stop=True))
        chain_op("vector", lambda v: v.tensor_copy(out=nwb[:, 0:512], in_=psP[:]))
        chain_op("tensor", lambda t: t.matmul(psP[:], G["ones_f32"][0:1, 0:64], nwrow[0:1, 512:1024], start=True, stop=True))
        chain_op("vector", lambda v: v.tensor_copy(out=nwb[:, 512:1024], in_=psP[:]))
        chain_op("vector", lambda v: v.memset(zz16[:, :], 0.0))
        chain_op("vector", lambda v: v.memset(stt_[:, :], 0.0))
        chain_op("vector", lambda v: v.memset(stb[:, :], 0.0))

        for c in range(NCH):
            tok0 = c * CH
            for sl in range(6):
                w0 = sl * 448
                wid = 448 if sl < 5 else PROJ - 5 * 448
                def ld_w(s, w0=w0, wid=wid):
                    for dc in range(8):
                        ins = s.dma_start_transpose(
                            out=wTf[:, dc * 448: dc * 448 + wid],
                            in_=win_bf[w0:w0 + wid, dc * 128:(dc + 1) * 128])
                        dtn[0] = P.inc("sync", ins, "ZDT2", 16)
                    return None
                chain_dma_sync(ld_w)
                def mm_sl(t, w0=w0, wid=wid, sl=sl, tok0=tok0):
                    last = None
                    for dc in range(8):
                        last = t.matmul(psP[:, 0:wid],
                                        XHAT[:, dc * S_CORE + tok0: dc * S_CORE + tok0 + CH],
                                        wTf[:, dc * 448: dc * 448 + wid],
                                        start=(dc == 0), stop=(dc == 7))
                    return last
                chain_op("tensor", mm_sl)
                def ev_sl(v, w0=w0, wid=wid, sl=sl):
                    last = v.tensor_copy(out=pjb[:, w0:w0 + wid], in_=psP[:, 0:wid])
                    lo = max(w0, 1024); hi = min(w0 + wid, 2048)
                    if lo < hi:
                        last = v.tensor_copy(out=pjz[:, lo - 1024:hi - 1024],
                                             in_=psP[:, lo - w0:hi - w0])
                    lo = max(w0, 2560); hi = min(w0 + wid, 2576)
                    if lo < hi:
                        last = v.tensor_copy(out=pjz[:, 1024 + lo - 2560:1024 + hi - 2560],
                                             in_=psP[:, lo - w0:hi - w0])
                    return last
                chain_op("vector", ev_sl)
            chain_op("tensor", lambda t: t.transpose(psA[0:16, 0:64], pjz[:, 1024:1040], G["ident_f32"][0:64, 0:64]))
            chain_op("vector", lambda v: v.tensor_copy(out=dtlT[:, :], in_=psA[0:16, 0:64]))
            def mm_dt(t):
                t.matmul(psP[:, 0:16], dtlT[:, :], dtwT[:, :], start=True, stop=False)
                return t.matmul(psP[:, 0:16], G["ones_bf"][0:1, 0:64], dtbrow[0:1, :], start=False, stop=True)
            chain_op("tensor", mm_dt)
            chain_op("vector", lambda v: v.tensor_copy(out=sp2[:, :], in_=psP[:, 0:16]))
            chain_op("vector", lambda v: v.tensor_scalar(out=sp1[:, :], in0=sp2[:, :], scalar1=-1.0, scalar2=None, op0=ALU.mult))
            chain_op("vector", lambda v: v.tensor_tensor(out=sp1[:, :], in0=sp1[:, :], in1=sp2[:, :], op=ALU.max))
            chain_op("scalar", lambda sc: sc.activation(out=dtt[:, :], in_=sp1[:, :], func=AF.Exp, scale=-1.0))
            chain_op("scalar", lambda sc: sc.activation(out=sp1[:, :], in_=dtt[:, :], func=AF.Ln, bias=1.0))
            chain_op("vector", lambda v: v.tensor_relu(out=sp2[:, :], in_=sp2[:, :]))
            chain_op("vector", lambda v: v.tensor_tensor(out=dtt[:, :], in0=sp2[:, :], in1=sp1[:, :], op=ALU.add))
            chain_op("tensor", lambda t: t.transpose(psA[0:16, 0:64], dtt[:, :], G["ident_f32"][0:64, 0:64]))
            chain_op("vector", lambda v: v.tensor_copy(out=dtf[:, :], in_=psA[0:16, 0:64]))
            chain_op("vector", lambda v: v.tensor_tensor(out=dafm[:, :], in0=dtf[:, :], in1=acol[:, 0:1].broadcast_to([16, 64]), op=ALU.mult))
            chain_op("vector", lambda v: v.tensor_tensor_scan(out=lcs[:, :], data0=dafm[:, :], data1=zz16[:, :], initial=0.0, op0=ALU.add, op1=ALU.add))
            def mm_t2(t):
                t.matmul(psA[:, 0:512], lcs[:, :], repf[:, 0:512], start=True, stop=True)
                return t.matmul(psA[:, 512:1024], lcs[:, :], repf[:, 512:1024], start=True, stop=True)
            chain_op("tensor", mm_t2)
            chain_op("vector", lambda v: v.tensor_copy(out=t2m[:, :], in_=psA[0:64, :]))
            def mm_tdt(t):
                t.matmul(psA[:, 0:512], dtf[:, :], repf[:, 0:512], start=True, stop=True)
                return t.matmul(psA[:, 512:1024], dtf[:, :], repf[:, 512:1024], start=True, stop=True)
            chain_op("tensor", mm_tdt)
            chain_op("vector", lambda v: v.tensor_copy(out=tdt[:, :], in_=psA[0:64, :]))
            def mm_t1(t):
                last = None
                for h in range(16):
                    last = t.transpose(psB2[:, h * 64:(h + 1) * 64], t2m[:, h * 64:(h + 1) * 64],
                                       G["ident_f32"][0:64, 0:64])
                return last
            chain_op("tensor", mm_t1)
            chain_op("vector", lambda v: v.tensor_tensor(out=e1[:, :], in0=psB2[0:64, :], in1=t2m[:, :], op=ALU.subtract))
            chain_op("vector", lambda v: v.tensor_tensor(out=e1[:, :], in0=e1[:, :], in1=mrep[:, :], op=ALU.add))
            chain_op("scalar", lambda sc: sc.activation(out=e1[:, :], in_=e1[:, :], func=AF.Exp))
            def mm_bc(t):
                last = None
                for h in range(16):
                    last = t.transpose(psBC[:, h * 64:(h + 1) * 64], pjb[:, 2048 + h * 16:2048 + (h + 1) * 16],
                                       G["ident_bf"][0:64, 0:64])
                return last
            chain_op("tensor", mm_bc)
            chain_op("vector", lambda v: v.tensor_copy(out=bcfm[:, 0:1024], in_=psBC[0:16, :]))
            def mm_cc(t):
                last = None
                for h in range(16):
                    last = t.transpose(psBC[:, h * 64:(h + 1) * 64], pjb[:, 2304 + h * 16:2304 + (h + 1) * 16],
                                       G["ident_bf"][0:64, 0:64])
                return last
            chain_op("tensor", mm_cc)
            chain_op("vector", lambda v: v.tensor_copy(out=bcfm[:, 1024:2048], in_=psBC[0:16, :]))
            def mm_sc(t):
                last = None
                for h in range(16):
                    last = t.matmul(psA[:, h * 64:(h + 1) * 64], bcfm[0:16, h * 64:(h + 1) * 64],
                                    bcfm[0:16, 1024 + h * 64:1024 + (h + 1) * 64], start=True, stop=True)
                return last
            chain_op("tensor", mm_sc)
            chain_op("vector", lambda v: v.tensor_tensor(out=sS[:, :], in0=psA[0:64, :], in1=e1[:, :], op=ALU.mult))
            chain_op("vector", lambda v: v.tensor_tensor(out=sS[:, :], in0=sS[:, :], in1=tdt[:, :], op=ALU.mult))
            def mm_yi(t):
                last = None
                for h in range(16):
                    last = t.matmul(psB2[:, h * 64:(h + 1) * 64], sS[:, h * 64:(h + 1) * 64],
                                    pjb[:, h * 64:(h + 1) * 64], start=True, stop=True)
                return last
            chain_op("tensor", mm_yi)
            def mm_yc(t):
                last = None
                for h in range(16):
                    last = t.matmul(psA[:, h * 64:(h + 1) * 64], bcfm[0:16, 1024 + h * 64:1024 + (h + 1) * 64],
                                    stb[:, h * 64:(h + 1) * 64], start=True, stop=True)
                return last
            chain_op("tensor", mm_yc)
            chain_op("scalar", lambda sc: sc.activation(out=explf[:, :], in_=lcs[:, :], func=AF.Exp))
            chain_op("tensor", lambda t: t.transpose(psP[:, 0:16], explf[:, :], G["ident_f32"][0:16, 0:16]))
            chain_op("vector", lambda v: v.tensor_copy(out=expltm[:, :], in_=psP[0:64, 0:16]))
            chain_op("vector", lambda v: v.tensor_copy(out=ytm[:, :], in_=psB2[0:64, :]))
            def sc_yc(v):
                last = None
                for h in range(16):
                    last = v.scalar_tensor_tensor(out=ytm[:, h * 64:(h + 1) * 64], in0=psA[0:64, h * 64:(h + 1) * 64],
                                                  scalar=expltm[:, h:h + 1], in1=ytm[:, h * 64:(h + 1) * 64],
                                                  op0=ALU.mult, op1=ALU.add)
                return last
            chain_op("vector", sc_yc)
            def d_term(v):
                v.tensor_tensor(out=e1[:, :], in0=pjb[:, 0:1024], in1=dbc[:, :], op=ALU.mult)
                return v.tensor_tensor(out=ytm[:, :], in0=ytm[:, :], in1=e1[:, :], op=ALU.add)
            chain_op("vector", d_term)
            chain_op("vector", lambda v: v.tensor_tensor(out=d2e[:, :], in0=lcs[:, 63:64].broadcast_to([16, 64]), in1=lcs[:, :], op=ALU.subtract))
            chain_op("scalar", lambda sc: sc.activation(out=d2e[:, :], in_=d2e[:, :], func=AF.Exp))
            def mm_d2e(t):
                t.matmul(psA[:, 0:512], d2e[:, :], repf[:, 0:512], start=True, stop=True)
                return t.matmul(psA[:, 512:1024], d2e[:, :], repf[:, 512:1024], start=True, stop=True)
            chain_op("tensor", mm_d2e)
            chain_op("vector", lambda v: v.tensor_tensor(out=e1[:, :], in0=psA[0:64, :], in1=tdt[:, :], op=ALU.mult))
            chain_op("vector", lambda v: v.tensor_tensor(out=wx[:, :], in0=e1[:, :], in1=pjb[:, 0:1024], op=ALU.mult))
            def mm_cs(t):
                last = None
                for h in range(16):
                    last = t.matmul(psD2[:, h * 64:(h + 1) * 64], pjb[:, 2048 + h * 16:2048 + (h + 1) * 16],
                                    wx[:, h * 64:(h + 1) * 64], start=True, stop=True)
                return last
            chain_op("tensor", mm_cs)
            chain_op("vector", lambda v: v.tensor_copy(out=ecol[:, :], in_=explf[:, 63:64]))
            def mm_cd(t):
                t.matmul(psA[0:1, 0:512], ecol[:, 0:1], repf[:, 0:512], start=True, stop=True)
                return t.matmul(psA[0:1, 512:1024], ecol[:, 0:1], repf[:, 512:1024], start=True, stop=True)
            chain_op("tensor", mm_cd)
            chain_op("vector", lambda v: v.tensor_copy(out=drow[0:1, :], in_=psA[0:1, 0:1024]))
            def mm_cdb2(t):
                t.matmul(psB2[0:16, 0:512], G["ones_f32"][0:1, 0:16], drow[0:1, 0:512], start=True, stop=True)
                return t.matmul(psB2[0:16, 512:1024], G["ones_f32"][0:1, 0:16], drow[0:1, 512:1024], start=True, stop=True)
            chain_op("tensor", mm_cdb2)
            chain_op("vector", lambda v: v.tensor_tensor(out=stt_[:, :], in0=stt_[:, :], in1=psB2[0:16, :], op=ALU.mult))
            chain_op("vector", lambda v: v.tensor_tensor(out=stt_[:, :], in0=stt_[:, :], in1=psD2[0:16, :], op=ALU.add))
            chain_op("vector", lambda v: v.tensor_copy(out=stb[:, :], in_=stt_[:, :]))
            chain_op("scalar", lambda sc: sc.activation(out=t2m[:, :], in_=pjz[:, 0:1024], func=AF.Silu))
            chain_op("scalar", lambda sc: sc.activation(out=e1[:, :], in_=ytm[:, :], func=AF.Square, accum_out=rcols[:, 0:1]))
            chain_op("vector", lambda v: v.tensor_scalar(out=rcols[:, 1:2], in0=rcols[:, 0:1], scalar1=1.0 / D, scalar2=1e-6, op0=ALU.mult, op1=ALU.add))
            chain_op("scalar", lambda sc: sc.activation(out=rcols[:, 2:3], in_=rcols[:, 1:2], func=AF.Ln))
            chain_op("scalar", lambda sc: sc.activation(out=rcols[:, 3:4], in_=rcols[:, 2:3], func=AF.Exp, scale=-0.5))
            chain_op("vector", lambda v: v.tensor_tensor(out=ytm[:, :], in0=ytm[:, :], in1=rcols[:, 3:4].broadcast_to([64, D]), op=ALU.mult))
            chain_op("vector", lambda v: v.tensor_tensor(out=ytm[:, :], in0=ytm[:, :], in1=nwb[:, :], op=ALU.mult))
            chain_op("vector", lambda v: v.tensor_tensor(out=ytm[:, :], in0=ytm[:, :], in1=t2m[:, :], op=ALU.mult))
            def st_y(g, tok0=tok0):
                return g.dma_start(out=yscr[tok0:tok0 + CH, :], in_=ytm[:, :])
            chain_dma(st_y)

        for tb in range(NTB):
            def ld2(g, tb=tb):
                return g.dma_start(out=CTX[:, tb * 1024:(tb + 1) * 1024],
                                   in_=yscr[tb * 128:(tb + 1) * 128, :])
            chain_dma(ld2)

        def emit():
            i = 0
            n = len(script)
            first = {e: True for e in ("tensor", "vector", "scalar", "gpsimd", "sync")}
            while i < n:
                eng = script[i][0]
                j = i
                fns = []
                while j < n and script[j][0] == eng:
                    fns.append(script[j][1])
                    j += 1
                sec = getattr(block, eng)

                def body(h, fns=fns, eng=eng):
                    if first[eng]:
                        P.pw(h)
                        h.wait_ge(P.sem("ZCL"), P.val("ZCL"))
                        if eng == "sync":
                            h.wait_ge(P.sem("WPREP"), P.val("WPREP"))
                        first[eng] = False
                    for f in fns:
                        f(h)
                sec(body)
                i = j

        emit()
        P.end_phase()
        st.close()
    return ph


def l0_attn_b_phases():
    return [
        mk_prep([("L0_attn_b_W_in", [2576, D]), ("L0_attn_b_W_out", [D, D])]),
        ph_ln,
        mk_mamba(),
        mk_proj_residual("L0_attn_b_W_out", None),
    ]


def _serial_chain(P, G, prefix):
    C = Chain(P, prefix + "CH")
    dmn = [P.val(prefix + "DM")]
    dtn = [P.val(prefix + "DT")]
    script = []

    def chain_op(engname, emit_fn):
        def fn(h, emit_fn=emit_fn, engname=engname):
            C.wait(h)
            h.wait_ge(P.sem(prefix + "DM"), dmn[0])
            h.wait_ge(P.sem(prefix + "DT"), dtn[0])
            ins = emit_fn(h)
            C.do(engname, h, ins)
        script.append((engname, fn))

    def chain_dma_syncN(emit_fn):
        def fn(h, emit_fn=emit_fn):
            C.wait(h)
            h.wait_ge(P.sem(prefix + "DM"), dmn[0])
            h.wait_ge(P.sem(prefix + "DT"), dtn[0])

            def inc16(ins):
                dtn[0] = P.inc("sync", ins, prefix + "DT", 16)
            emit_fn(h, inc16)
        script.append(("sync", fn))

    def chain_dma(emit_fn):
        def fn(h, emit_fn=emit_fn):
            C.wait(h)
            h.wait_ge(P.sem(prefix + "DM"), dmn[0])
            ins = emit_fn(h)
            dmn[0] = P.inc("gpsimd", ins, prefix + "DM", 16)
        script.append(("gpsimd", fn))

    def emit():
        block = G["block"]
        i = 0
        n = len(script)
        first = {e: True for e in ("tensor", "vector", "scalar", "gpsimd", "sync")}
        while i < n:
            eng = script[i][0]
            j = i
            fns = []
            while j < n and script[j][0] == eng:
                fns.append(script[j][1])
                j += 1
            sec = getattr(block, eng)

            def body(h, fns=fns, eng=eng):
                if first[eng]:
                    P.pw(h)
                    if eng == "sync":
                        h.wait_ge(P.sem("WPREP"), P.val("WPREP"))
                    first[eng] = False
                for f in fns:
                    f(h)
            sec(body)
            i = j
    return chain_op, chain_dma_syncN, chain_dma, emit


def mk_delta_pre():
    def ph(P, G):
        XHAT = G["XHAT"]
        qkv_bf = bfview(P, "L1_attn_a_qkv_w", D)
        ctrl_bf = bfview(P, "L1_attn_a_ctrl_w", D)
        ctrl_w_in = P.inp("L1_attn_a_ctrl_w", [1056, D])
        cb_in = P.inp("L1_attn_a_ctrl_b", [1056])
        convs = [P.inp("L1_attn_a_convq_w", [D, 4]), P.inp("L1_attn_a_convk_w", [D, 4]),
                 P.inp("L1_attn_a_convv_w", [D, 4])]
        tscr = [P.scratch("dn_q", [S_CORE, D]), P.scratch("dn_k", [S_CORE, D]),
                P.scratch("dn_v", [S_CORE, D])]
        labscr = P.scratch("dn_lab", [S_CORE, 32])
        gscr = P.scratch("dn_gate", [S_CORE, D])

        st = ExitStack()
        wT = P.sb("dp_wT", [128, 8 * 128], BF16, st)
        wTab = P.sb("dp_wTab", [128, 8 * 32], BF16, st)
        wTg = P.sb("dp_wTg", [128, 8 * 1024], BF16, st)
        cw = P.sb("dp_cw", [128, 96], F32, st)
        cbr = P.sb("dp_cbr", [1, 1056], BF16, st)
        fmb = P.sb("dp_fmb", [128, 4100], F32, st)
        cst = P.sb("dp_cst", [128, 4096], F32, st)
        tmS = P.sb("dp_tmS", [128, 128], F32, st)
        sgt = P.sb("dp_sgt", [128, 1024], F32, st)
        abt = P.sb("dp_abt", [128, 32], F32, st)
        lbt = P.sb("dp_lbt", [128, 32], F32, st)
        lb2 = P.sb("dp_lb2", [128, 32], F32, st)
        psP = P.ps("dp_psP", [128, 512], F32, st)
        psT = P.ps("dp_psT", [128, 512], F32, st)

        chain_op, chain_dma_syncN, chain_dma, emit = _serial_chain(P, G, "DP")

        for wi in range(3):
            for cc in range(8):
                chain_dma(lambda g, wi=wi, cc=cc: g.dma_start(
                    out=cw[:, wi * 32 + cc * 4: wi * 32 + (cc + 1) * 4],
                    in_=convs[wi][cc * 128:(cc + 1) * 128, :]))
        chain_dma(lambda g: g.dma_start(out=cbr[0:1, :], in_=cb_in.rearrange("(o a) -> o a", o=1)))
        for dc in range(8):
            chain_dma(lambda g, dc=dc: g.dma_start(
                out=wTab[:, dc * 32:(dc + 1) * 32],
                in_=ctrl_w_in.rearrange("a b -> b a")[dc * 128:(dc + 1) * 128, 0:32],
                allow_slow_non_contiguous=True))
        chain_op("vector", lambda v: v.memset(fmb[:, 0:4], 0.0))

        for wi in range(3):
            for oc in range(8):
                def ld_w(s, inc16, wi=wi, oc=oc):
                    for dc in range(8):
                        inc16(s.dma_start_transpose(
                            out=wT[:, dc * 128:(dc + 1) * 128],
                            in_=qkv_bf[wi * 1024 + oc * 128: wi * 1024 + (oc + 1) * 128,
                                       dc * 128:(dc + 1) * 128]))
                chain_dma_syncN(ld_w)
                for t8 in range(8):
                    def mm8(t, t8=t8):
                        last = None
                        for dc in range(8):
                            last = t.matmul(psP[:, 0:512], wT[:, dc * 128:(dc + 1) * 128],
                                            XHAT[:, dc * S_CORE + t8 * 512: dc * S_CORE + (t8 + 1) * 512],
                                            start=(dc == 0), stop=(dc == 7))
                        return last
                    chain_op("tensor", mm8)
                    chain_op("vector", lambda v, t8=t8: v.tensor_copy(
                        out=fmb[:, 4 + t8 * 512: 4 + (t8 + 1) * 512], in_=psP[0:128, 0:512]))
                cwo = wi * 32 + oc * 4
                chain_op("vector", lambda v, cwo=cwo: v.tensor_tensor(
                    out=cst[:, :], in0=fmb[:, 1:4097],
                    in1=cw[:, cwo:cwo + 1].broadcast_to([128, 4096]), op=ALU.mult))
                for j in range(1, 4):
                    chain_op("vector", lambda v, cwo=cwo, j=j: v.scalar_tensor_tensor(
                        out=cst[:, :], in0=fmb[:, 1 + j:1 + j + 4096],
                        scalar=cw[:, cwo + j:cwo + j + 1],
                        in1=cst[:, :], op0=ALU.mult, op1=ALU.add))
                chain_op("scalar", lambda sc: sc.activation(out=fmb[:, 4:4100], in_=cst[:, :], func=AF.Silu))
                for tb in range(NTB):
                    chain_op("tensor", lambda t, tb=tb: t.transpose(
                        psT[:, 0:128], fmb[:, 4 + tb * 128: 4 + (tb + 1) * 128], G["ident_f32"][:, :]))
                    chain_op("vector", lambda v: v.tensor_copy(out=tmS[:, :], in_=psT[0:128, 0:128]))
                    chain_dma(lambda g, wi=wi, oc=oc, tb=tb: g.dma_start(
                        out=tscr[wi][tb * 128:(tb + 1) * 128, oc * 128:(oc + 1) * 128], in_=tmS[:, :]))

        def ld_wg(s, inc16):
            for dc in range(8):
                for oc in range(8):
                    inc16(s.dma_start_transpose(
                        out=wTg[:, dc * 1024 + oc * 128: dc * 1024 + (oc + 1) * 128],
                        in_=ctrl_bf[32 + oc * 128: 32 + (oc + 1) * 128, dc * 128:(dc + 1) * 128]))
        chain_dma_syncN(ld_wg)
        for t3 in range(NTB):
            def mmab(t, t3=t3):
                for dc in range(8):
                    t.matmul(psP[:, 0:32], XHAT[:, dc * S_CORE + t3 * 128: dc * S_CORE + (t3 + 1) * 128],
                             wTab[:, dc * 32:(dc + 1) * 32], start=(dc == 0), stop=False)
                return t.matmul(psP[:, 0:32], G["ones_bf"][0:1, 0:128], cbr[0:1, 0:32],
                                start=False, stop=True)
            chain_op("tensor", mmab)
            chain_op("scalar", lambda sc: sc.activation(out=abt[:, :], in_=psP[0:128, 0:32], func=AF.Sigmoid))
            chain_op("vector", lambda v: v.tensor_scalar(out=lbt[:, 0:16], in0=abt[:, 0:16],
                                                         scalar1=0.98, scalar2=0.01, op0=ALU.mult, op1=ALU.add))
            chain_op("scalar", lambda sc: sc.activation(out=lb2[:, 0:16], in_=lbt[:, 0:16], func=AF.Ln))
            chain_op("vector", lambda v: v.tensor_copy(out=lb2[:, 16:32], in_=abt[:, 16:32]))
            chain_dma(lambda g, t3=t3: g.dma_start(out=labscr[t3 * 128:(t3 + 1) * 128, :], in_=lb2[:, :]))
            for half in range(2):
                def mmg(t, t3=t3, half=half):
                    for dc in range(8):
                        t.matmul(psP[:, 0:512], XHAT[:, dc * S_CORE + t3 * 128: dc * S_CORE + (t3 + 1) * 128],
                                 wTg[:, dc * 1024 + half * 512: dc * 1024 + (half + 1) * 512],
                                 start=(dc == 0), stop=False)
                    return t.matmul(psP[:, 0:512], G["ones_bf"][0:1, 0:128],
                                    cbr[0:1, 32 + half * 512: 32 + (half + 1) * 512], start=False, stop=True)
                chain_op("tensor", mmg)
                chain_op("scalar", lambda sc, half=half: sc.activation(
                    out=sgt[:, half * 512:(half + 1) * 512], in_=psP[0:128, 0:512], func=AF.Sigmoid))
            chain_dma(lambda g, t3=t3: g.dma_start(out=gscr[t3 * 128:(t3 + 1) * 128, :], in_=sgt[:, :]))

        emit()
        P.end_phase()
        st.close()
    return ph


def mk_delta_main():
    def ph(P, G):
        CTX = G["CTX"]
        dq_scr = P.scratch("dn_q", [S_CORE, D])
        dk_scr = P.scratch("dn_k", [S_CORE, D])
        dv_scr = P.scratch("dn_v", [S_CORE, D])
        lab_scr = P.scratch("dn_lab", [S_CORE, 32])
        yscr = P.scratch("mamba_y", [S_CORE, D])
        msl_in = P.inp("masksl", [128, 128])
        mge_in = P.inp("maskge", [128, 128])
        rpf_in = P.inp("repf128", [16, 2048])
        nw_in = P.inp("L1_attn_a_norm_w", [64])

        st = ExitStack()
        msl = P.sb("dm_msl", [128, 128], F32, st)
        mge = P.sb("dm_mge", [128, 128], F32, st)
        rpf = P.sb("dm_rpf", [16, 2048], F32, st)
        nwrow = P.sb("dm_nwrow", [1, 64], F32, st)
        nwb = P.sb("dm_nwb", [128, 64], F32, st)
        qtm = P.sb("dm_qtm", [128, 1024], F32, st)
        ktm = P.sb("dm_ktm", [128, 1024], F32, st)
        vtm = P.sb("dm_vtm", [128, 1024], F32, st)
        labtm = P.sb("dm_labtm", [128, 32], F32, st)
        laf = P.sb("dm_laf", [16, 128], F32, st)
        lcs = P.sb("dm_lcs", [16, 128], F32, st)
        zz = P.sb("dm_zz", [16, 128], F32, st)
        explf = P.sb("dm_explf", [16, 128], F32, st)
        dtmp16 = P.sb("dm_dtmp16", [16, 128], F32, st)
        gtailf = P.sb("dm_gtailf", [16, 128], F32, st)
        gcr = P.sb("dm_gcr", [1, 16], F32, st)
        cg = P.sb("dm_cg", [128, 96], F32, st)
        sscols = P.sb("dm_ss", [128, 32], F32, st)
        rcA = P.sb("dm_rcA", [128, 32], F32, st)
        rcB = P.sb("dm_rcB", [128, 32], F32, st)
        kqfm = P.sb("dm_kqfm", [64, 4096], F32, st)
        t2m = P.sb("dm_t2m", [128, 2048], F32, st)
        e1sh = P.sb("dm_e1sh", [128, 128], F32, st)
        e2sh = P.sb("dm_e2sh", [128, 128], F32, st)
        e1sx = P.sb("dm_e1sx", [128, 128], F32, st)
        e2tx = P.sb("dm_e2tx", [128, 128], F32, st)
        nA = P.sb("dm_nA", [128, 128], F32, st)
        B0 = P.sb("dm_B0", [128, 128], F32, st)
        MbT = P.sb("dm_MbT", [128, 128], F32, st)
        BD = [P.sb(f"dm_BD{i}", [128, 256], F32, st) for i in range(2)]
        Wc = [P.sb(f"dm_W{i}", [128, 64], F32, st) for i in range(2)]
        tmpW = P.sb("dm_tmpW", [128, 64], F32, st)
        q0s = P.sb("dm_q0s", [128, 64], F32, st)
        Wg = P.sb("dm_Wg", [128, 64], F32, st)
        Sst = [P.sb(f"dm_S{i}", [64, 1024], F32, st) for i in range(2)]
        ytm = P.sb("dm_ytm", [128, 1024], F32, st)
        rms = P.sb("dm_rms", [128, 64], F32, st)
        psBig = P.ps("dm_psBig", [128, 2048], F32, st)
        psKKQ = P.ps("dm_psKKQ", [128, 512], F32, st)
        psM = P.ps("dm_psM", [128, 512], F32, st)
        psBD = P.ps("dm_psBD", [128, 512], F32, st)
        psW = P.ps("dm_psW", [128, 512], F32, st)

        chain_op, chain_dma_syncN, chain_dma, emit = _serial_chain(P, G, "DQ")

        chain_dma(lambda g: g.dma_start(out=msl[:, :], in_=msl_in[:, :]))
        chain_dma(lambda g: g.dma_start(out=mge[:, :], in_=mge_in[:, :]))
        chain_dma(lambda g: g.dma_start(out=rpf[:, :], in_=rpf_in[:, :]))
        chain_dma(lambda g: g.dma_start(out=nwrow[0:1, :], in_=nw_in.rearrange("(o a) -> o a", o=1)))
        chain_op("tensor", lambda t: t.matmul(psM[:, 0:64], G["ones_f32"][0:1, 0:128],
                                              nwrow[0:1, 0:64], start=True, stop=True))
        chain_op("vector", lambda v: v.tensor_copy(out=nwb[:, :], in_=psM[0:128, 0:64]))
        chain_op("vector", lambda v: v.memset(zz[:, :], 0.0))
        chain_op("vector", lambda v: v.memset(Sst[0][:, :], 0.0))

        for c in range(32):
            tok0 = c * 128
            sp, sq = c % 2, 1 - c % 2
            chain_dma(lambda g, tok0=tok0: g.dma_start(out=qtm[:, :], in_=dq_scr[tok0:tok0 + 128, :]))
            chain_dma(lambda g, tok0=tok0: g.dma_start(out=ktm[:, :], in_=dk_scr[tok0:tok0 + 128, :]))
            chain_dma(lambda g, tok0=tok0: g.dma_start(out=vtm[:, :], in_=dv_scr[tok0:tok0 + 128, :]))
            chain_dma(lambda g, tok0=tok0: g.dma_start(out=labtm[:, :], in_=lab_scr[tok0:tok0 + 128, :]))
            chain_op("tensor", lambda t: t.transpose(psM[0:16, 0:128], labtm[:, 0:16], G["ident_f32"][:, :]))
            chain_op("vector", lambda v: v.tensor_copy(out=laf[:, :], in_=psM[0:16, 0:128]))
            chain_op("vector", lambda v: v.tensor_tensor_scan(out=lcs[:, :], data0=laf[:, :], data1=zz[:, :],
                                                              initial=0.0, op0=ALU.add, op1=ALU.add))
            chain_op("scalar", lambda sc: sc.activation(out=explf[:, :], in_=lcs[:, :], func=AF.Exp))
            chain_op("vector", lambda v: v.tensor_tensor(out=dtmp16[:, :],
                                                         in0=lcs[:, 127:128].broadcast_to([16, 128]),
                                                         in1=lcs[:, :], op=ALU.subtract))
            chain_op("scalar", lambda sc: sc.activation(out=gtailf[:, :], in_=dtmp16[:, :], func=AF.Exp))
            chain_op("tensor", lambda t: t.transpose(psM[0:1, 128:144], explf[:, 127:128],
                                                     G["ident_f32"][0:16, 0:16]))
            chain_op("vector", lambda v: v.tensor_copy(out=gcr[0:1, :], in_=psM[0:1, 128:144]))
            chain_op("tensor", lambda t: t.transpose(psM[0:128, 128:144], explf[:, :],
                                                     G["ident_f32"][0:16, 0:16]))
            chain_op("vector", lambda v: v.tensor_copy(out=cg[:, 0:16], in_=psM[0:128, 128:144]))
            chain_op("tensor", lambda t: t.transpose(psM[0:128, 144:160], gtailf[:, :],
                                                     G["ident_f32"][0:16, 0:16]))
            chain_op("vector", lambda v: v.tensor_copy(out=cg[:, 16:32], in_=psM[0:128, 144:160]))
            chain_op("vector", lambda v: v.tensor_scalar(out=cg[:, 32:48], in0=labtm[:, 16:32],
                                                         scalar1=-1.0, scalar2=None, op0=ALU.mult))
            chain_op("vector", lambda v: v.tensor_scalar(out=cg[:, 48:64], in0=cg[:, 0:16],
                                                         scalar1=-1.0, scalar2=None, op0=ALU.mult))
            chain_op("tensor", lambda t: t.matmul(psM[:, 160:176], G["ones_f32"][0:1, 0:128],
                                                  gcr[0:1, 0:16], start=True, stop=True))
            chain_op("vector", lambda v: v.tensor_copy(out=cg[:, 64:80], in_=psM[0:128, 160:176]))
            def mmt2(t):
                last = None
                for j in range(4):
                    last = t.matmul(psBig[:, j * 512:(j + 1) * 512], lcs[:, :],
                                    rpf[:, j * 512:(j + 1) * 512], start=True, stop=True)
                return last
            chain_op("tensor", mmt2)
            chain_op("vector", lambda v: v.tensor_copy(out=t2m[:, :], in_=psBig[0:128, :]))
            def mmt1(t):
                last = None
                for h in range(16):
                    last = t.transpose(psBig[:, h * 128:(h + 1) * 128], t2m[:, h * 128:(h + 1) * 128],
                                       G["ident_f32"][:, :])
                return last
            chain_op("tensor", mmt1)
            chain_op("vector", lambda v: v.tensor_tensor(out=t2m[:, :], in0=t2m[:, :],
                                                         in1=psBig[0:128, :], op=ALU.subtract))
            def sqqk(sc):
                last = None
                for h in range(16):
                    sc.activation(out=rms[:, 0:64], in_=qtm[:, h * 64:(h + 1) * 64],
                                  func=AF.Square, accum_out=sscols[:, h:h + 1])
                    last = sc.activation(out=rms[:, 0:64], in_=ktm[:, h * 64:(h + 1) * 64],
                                         func=AF.Square, accum_out=sscols[:, 16 + h:17 + h])
                return last
            chain_op("scalar", sqqk)
            chain_op("scalar", lambda sc: sc.activation(out=rcA[:, :], in_=sscols[:, :], func=AF.Ln))
            chain_op("scalar", lambda sc: sc.activation(out=rcB[:, :], in_=rcA[:, :], func=AF.Exp, scale=0.5))
            chain_op("vector", lambda v: v.tensor_scalar(out=rcA[:, :], in0=rcB[:, :],
                                                         scalar1=1e-6, scalar2=None, op0=ALU.max))
            chain_op("vector", lambda v: v.reciprocal(out=rcB[:, :], in_=rcA[:, :]))
            def nrm(v):
                last = None
                for h in range(16):
                    v.tensor_tensor(out=qtm[:, h * 64:(h + 1) * 64], in0=qtm[:, h * 64:(h + 1) * 64],
                                    in1=rcB[:, h:h + 1].broadcast_to([128, 64]), op=ALU.mult)
                    last = v.tensor_tensor(out=ktm[:, h * 64:(h + 1) * 64], in0=ktm[:, h * 64:(h + 1) * 64],
                                           in1=rcB[:, 16 + h:17 + h].broadcast_to([128, 64]), op=ALU.mult)
                return last
            chain_op("vector", nrm)
            for h in range(16):
                chain_op("tensor", lambda t, h=h: (
                    t.transpose(psW[0:64, 128:256], ktm[:, h * 64:(h + 1) * 64], G["ident_f32"][:, :]),
                    t.transpose(psW[0:64, 256:384], qtm[:, h * 64:(h + 1) * 64], G["ident_f32"][:, :]))[-1])
                chain_op("vector", lambda v, h=h: v.tensor_copy(
                    out=kqfm[:, h * 256:(h + 1) * 256], in_=psW[0:64, 128:384]))
            for h in range(16):
                def masks_v(v, h=h):
                    v.tensor_tensor(out=e1sh[:, :], in0=t2m[:, h * 128:(h + 1) * 128],
                                    in1=msl[:, :], op=ALU.add)
                    return v.tensor_tensor(out=e2sh[:, :], in0=mge[:, :],
                                           in1=t2m[:, h * 128:(h + 1) * 128], op=ALU.subtract)
                chain_op("vector", masks_v)

                def masks_s(sc):
                    sc.activation(out=e1sx[:, :], in_=e1sh[:, :], func=AF.Exp)
                    return sc.activation(out=e2tx[:, :], in_=e2sh[:, :], func=AF.Exp)
                chain_op("scalar", masks_s)
                chain_op("tensor", lambda t, h=h: t.matmul(
                    psKKQ[:, 0:256], kqfm[:, h * 256: h * 256 + 128],
                    kqfm[:, h * 256: h * 256 + 256], start=True, stop=True))
                chain_op("vector", lambda v, h=h: v.scalar_tensor_tensor(
                    out=nA[:, :], in0=psKKQ[0:128, 0:128], scalar=cg[:, 32 + h:33 + h],
                    in1=e1sx[:, :], op0=ALU.mult, op1=ALU.mult))
                chain_op("vector", lambda v: v.tensor_tensor(
                    out=MbT[:, :], in0=psKKQ[0:128, 128:256], in1=e2tx[:, :], op=ALU.mult))
                chain_op("tensor", lambda t: t.transpose(psW[:, 0:128], nA[:, :], G["ident_f32"][:, :]))
                chain_op("vector", lambda v: v.tensor_copy(out=B0[:, :], in_=psW[0:128, 0:128]))
                def mmk0(t, h=h, sp=sp):
                    t.matmul(psM[:, 0:64], kqfm[:, h * 256: h * 256 + 128],
                             Sst[sp][:, h * 64:(h + 1) * 64], start=True, stop=True)
                    return t.matmul(psM[:, 64:128], kqfm[:, h * 256 + 128: h * 256 + 256],
                                    Sst[sp][:, h * 64:(h + 1) * 64], start=True, stop=True)
                chain_op("tensor", mmk0)
                chain_op("vector", lambda v: v.tensor_copy(out=q0s[:, :], in_=psM[0:128, 64:128]))
                chain_op("vector", lambda v, h=h: v.scalar_tensor_tensor(
                    out=tmpW[:, :], in0=psM[0:128, 0:64], scalar=cg[:, 48 + h:49 + h],
                    in1=vtm[:, h * 64:(h + 1) * 64], op0=ALU.mult, op1=ALU.add))
                chain_op("vector", lambda v, h=h: v.tensor_tensor(
                    out=Wc[0][:, :], in0=tmpW[:, :],
                    in1=labtm[:, 16 + h:17 + h].broadcast_to([128, 64]), op=ALU.mult))
                for kk in range(7):
                    if kk == 0:
                        Bap, Dap = B0[:, 0:128], nA[:, 0:128]
                    else:
                        Bap, Dap = BD[(kk - 1) % 2][:, 0:128], BD[(kk - 1) % 2][:, 128:256]

                    def neu_t(t, Bap=Bap, Dap=Dap, kk=kk):
                        last = t.matmul(psW[:, 0:64], Bap, Wc[kk % 2][:, :], start=True, stop=True)
                        if kk < 6:
                            t.matmul(psBD[:, 0:128], Dap, Bap, start=True, stop=True)
                            last = t.matmul(psBD[:, 128:256], Bap, Dap, start=True, stop=True)
                        return last
                    chain_op("tensor", neu_t)

                    def neu_v(v, kk=kk):
                        last = v.tensor_tensor(out=Wc[(kk + 1) % 2][:, :], in0=Wc[kk % 2][:, :],
                                               in1=psW[0:128, 0:64], op=ALU.add)
                        if kk < 6:
                            last = v.tensor_copy(out=BD[kk % 2][:, :], in_=psBD[0:128, 0:256])
                        return last
                    chain_op("vector", neu_v)
                chain_op("vector", lambda v, h=h: v.tensor_tensor(
                    out=Wg[:, :], in0=Wc[1][:, :],
                    in1=cg[:, 16 + h:17 + h].broadcast_to([128, 64]), op=ALU.mult))
                chain_op("tensor", lambda t, h=h: t.matmul(
                    psM[0:64, 192:256], ktm[:, h * 64:(h + 1) * 64], Wg[:, :], start=True, stop=True))
                chain_op("vector", lambda v, h=h, sp=sp, sq=sq: v.scalar_tensor_tensor(
                    out=Sst[sq][:, h * 64:(h + 1) * 64], in0=Sst[sp][:, h * 64:(h + 1) * 64],
                    scalar=cg[0:64, 64 + h:65 + h], in1=psM[0:64, 192:256],
                    op0=ALU.mult, op1=ALU.add))
                chain_op("tensor", lambda t: t.matmul(
                    psM[:, 128:192], MbT[:, :], Wc[1][:, :], start=True, stop=True))
                chain_op("vector", lambda v, h=h: v.scalar_tensor_tensor(
                    out=ytm[:, h * 64:(h + 1) * 64], in0=q0s[:, :],
                    scalar=cg[:, h:h + 1], in1=psM[0:128, 128:192], op0=ALU.mult, op1=ALU.add))
                chain_op("scalar", lambda sc, h=h: sc.activation(
                    out=rms[:, 0:64], in_=ytm[:, h * 64:(h + 1) * 64],
                    func=AF.Square, accum_out=sscols[:, h:h + 1]))
            chain_op("vector", lambda v: v.tensor_scalar(out=rcA[:, 0:16], in0=sscols[:, 0:16],
                                                         scalar1=1.0 / 64, scalar2=1e-6,
                                                         op0=ALU.mult, op1=ALU.add))
            chain_op("scalar", lambda sc: sc.activation(out=rcB[:, 0:16], in_=rcA[:, 0:16], func=AF.Ln))
            chain_op("scalar", lambda sc: sc.activation(out=rcA[:, 16:32], in_=rcB[:, 0:16],
                                                        func=AF.Exp, scale=-0.5))
            def yfin(v):
                last = None
                for h in range(16):
                    last = v.scalar_tensor_tensor(out=ytm[:, h * 64:(h + 1) * 64],
                                                  in0=ytm[:, h * 64:(h + 1) * 64],
                                                  scalar=rcA[:, 16 + h:17 + h], in1=nwb[:, :],
                                                  op0=ALU.mult, op1=ALU.mult)
                return last
            chain_op("vector", yfin)
            chain_dma(lambda g, tok0=tok0: g.dma_start(out=yscr[tok0:tok0 + 128, :], in_=ytm[:, :]))

        for tb in range(NTB):
            chain_dma(lambda g, tb=tb: g.dma_start(out=CTX[:, tb * 1024:(tb + 1) * 1024],
                                                   in_=yscr[tb * 128:(tb + 1) * 128, :]))
        emit()
        P.end_phase()
        st.close()
    return ph


def l1_attn_a_phases():
    return [
        mk_prep([("L1_attn_a_qkv_w", [3 * D, D]), ("L1_attn_a_ctrl_w", [1056, D]),
                 ("L1_attn_a_out_w", [D, D])]),
        ph_ln,
        mk_delta_pre(),
        mk_delta_main(),
        mk_proj_residual("L1_attn_a_out_w", "L1_attn_a_out_b", gname="dn_gate"),
    ]



def l0_attn_a_phases():
    return [
        mk_prep([("L0_attn_a_Wq", [D, D]), ("L0_attn_a_Wk", [D, D]),
                 ("L0_attn_a_Wv", [D, D]), ("L0_attn_a_Wo", [D, D])]),
        ph_ln,
        mk_attn("L0_attn_a_Wq", "L0_attn_a_Wk", "L0_attn_a_Wv",
                "L0_attn_a_bq", "L0_attn_a_bk", "L0_attn_a_bv", qscale=0.125),
        mk_proj_residual("L0_attn_a_Wo", "L0_attn_a_bo"),
    ]


def l1_attn_b_phases():
    return [
        mk_prep([("L1_attn_b_Wq", [D, D]), ("L1_attn_b_Wk", [D, D]),
                 ("L1_attn_b_Wv", [D, D]), ("L1_attn_b_Wo", [D, D])]),
        ph_ln,
        mk_relbias_prep(),
        mk_attn("L1_attn_b_Wq", "L1_attn_b_Wk", "L1_attn_b_Wv",
                "L1_attn_b_bq", "L1_attn_b_bk", "L1_attn_b_bv", qscale=0.125, relbias=True),
        mk_proj_residual("L1_attn_b_Wo", "L1_attn_b_bo"),
    ]


def _make_onehot_rev():
    import math
    dd = np.arange(8192)
    rel = dd - 4095
    nb = 160
    buckets = (rel > 0).astype(np.int64) * nb
    rel_abs = np.abs(rel)
    max_exact = 80
    is_small = rel_abs < max_exact
    large = np.log(np.maximum(rel_abs, 1) / max_exact) / math.log(800 / 80) * (nb - max_exact)
    large = (max_exact + large).astype(np.int64)
    large = np.minimum(large, nb - 1)
    b = buckets + np.where(is_small, rel_abs, large)
    oh = np.zeros((320, 8192), dtype=np.float32)
    oh[b, np.arange(8192)] = 1.0
    return oh


def _run(phase_lists, inputs, n_cores=N_CORES, outs=None):
    phases = [ph_init]
    for pl in phase_lists:
        phases.extend(pl)
    phases.append(ph_copyout)
    P = build_program(phases)
    x = np.ascontiguousarray(inputs["L0_attn_a_inputs"], dtype=np.float32)
    ident = np.eye(128, dtype=np.float32)
    consts = {"ident": ident}
    if "onehot_norm" in P.inputs:
        consts["onehot_norm"] = _make_onehot_rev()
    if "rep16" in P.inputs:
        r = np.zeros((16, 1024), dtype=np.float32)
        for h in range(16):
            r[h, h * 64:(h + 1) * 64] = 1.0
        consts["rep16"] = r
        m = np.zeros((64, 1024), dtype=np.float32)
        jj = np.arange(64)[:, None]
        ii = np.arange(64)[None, :]
        blkmask = np.where(ii >= jj, 0.0, -1e30).astype(np.float32)
        for h in range(16):
            m[:, h * 64:(h + 1) * 64] = blkmask
        consts["maskrep"] = m
    if "repf128" in P.inputs:
        r = np.zeros((16, 2048), dtype=np.float32)
        for h in range(16):
            r[h, h * 128:(h + 1) * 128] = 1.0
        consts["repf128"] = r
        jj = np.arange(128)[:, None]
        ii = np.arange(128)[None, :]
        consts["masksl"] = np.where(ii < jj, 0.0, -1e30).astype(np.float32)
        consts["maskge"] = np.where(ii >= jj, 0.0, -1e30).astype(np.float32)
    in_maps = []
    for b in range(n_cores):
        m = {"x_in": x[b]}
        m.update(consts)
        for name in P.inputs:
            if name in m or name == "x_in":
                continue
            m[name] = np.ascontiguousarray(inputs[name], dtype=np.float32)
        in_maps.append(m)
    res = run_bass_kernel_spmd(P.nc, in_maps, core_ids=list(range(n_cores)))
    keys = outs or ["y"]
    return np.stack([r[keys[0]] for r in res.results], axis=0)


def kernel(**inputs):
    return _run([
        l0_attn_a_phases(),
        l0_ffn_phases(),
        l0_attn_b_phases(),
        l0_ffn_phases(),
        l1_attn_a_phases(),
        l1_ffn_phases(),
        l1_attn_b_phases(),
        l1_ffn_phases(),
    ], inputs)
